# version 18
# baseline (speedup 1.0000x reference)
import math
from contextlib import ExitStack

import numpy as np
import concourse.bass as bass
import concourse.mybir as mybir
from concourse.bass_utils import run_bass_kernel_spmd

F32 = mybir.dt.float32
F16 = mybir.dt.float16
BF16 = mybir.dt.bfloat16
I32 = mybir.dt.int32
AF = mybir.ActivationFunctionType
OP = mybir.AluOpType
AX = mybir.AxisListType

ALIBI_SLOPES = [0.7071067811865476 ** (i + 1) for i in range(16)]
C_STAB = 16.0
NEGBIG = -32768.0


class Dims:
    def __init__(self, T=4096, D=1024, H=16, NE=8, CAP=2048, DH=2048, R=4, W=128):
        self.T, self.D, self.H, self.NE, self.CAP, self.DH, self.R, self.W = \
            T, D, H, NE, CAP, DH, R, W
        self.HD = 64
        self.NC = 8
        self.NT = T // 128
        self.ND = D // 128
        self.TG = T * self.NC
        self.Q = self.TG // 128
        self.NW = T // W
        self.slopes = [ALIBI_SLOPES[h * (16 // H)] for h in range(H)]


ENGS = ("te", "ve", "act", "pool", "sp")
QOF = {"qsp": "sp", "qact": "act", "qpool": "pool"}


class Prog:

    NLANES = 6
    QENG = {"qsp": "sp", "qact": "act", "qpool": "pool", "cc": "pool"}

    def __init__(self, nc):
        self.nc = nc
        self.ops = {e: [] for e in ENGS}
        self.count = {s: 0 for s in ("te", "ve", "act", "pool")}
        self.sems = {}
        self.lane = {}
        self.fcount = {}
        self.fcover = {}
        self.rr = {}
        for q in ("qsp", "qact", "qpool", "cc"):
            n = 1 if q == "cc" else self.NLANES
            for l in range(n):
                self.lane[(q, l)] = 0
                self.fcover[(q, l)] = 0
            self.fcount[q] = 0
            self.rr[q] = 0
        self.waited = {e: {} for e in ENGS}

    def make_sems(self, ctx):
        for s in ("te", "ve", "act", "pool"):
            self.sems[s] = ctx.enter_context(self.nc.semaphore("s_" + s))
        for (q, l) in self.lane:
            self.sems[(q, l)] = ctx.enter_context(self.nc.semaphore(f"s_{q}{l}"))
        for q in self.fcount:
            self.sems["f" + q] = ctx.enter_context(self.nc.semaphore("sf_" + q))

    def _wait(self, eng, tok):
        if tok is None:
            return
        sem, val = tok
        if isinstance(sem, tuple):
            q, l = sem
            if self.fcover[(q, l)] < val:
                qeng = self.QENG[q]
                total = self.lane[(q, l)]
                self.ops[qeng].append(("wait", self.sems[(q, l)], total))
                self.waited[qeng][(q, l)] = max(
                    self.waited[qeng].get((q, l), 0), total)
                fs = "f" + q
                self.fcount[fs[1:]] += 1
                self.ops[qeng].append(("inc", self.sems[fs], 1))
                self.fcover[(q, l)] = total
            sem2, val2 = "f" + q, self.fcount[q]
            if self.waited[eng].get(sem2, 0) >= val2:
                return
            self.waited[eng][sem2] = val2
            self.ops[eng].append(("wait", self.sems[sem2], val2))
            return
        if val <= 0 or self.waited[eng].get(sem, 0) >= val:
            return
        self.waited[eng][sem] = val
        self.ops[eng].append(("wait", self.sems[sem], val))

    def fence_queue(self, q):
        qeng = self.QENG[q]
        made = False
        for l in range(1 if q == "cc" else self.NLANES):
            total = self.lane[(q, l)]
            if self.fcover[(q, l)] < total:
                self.ops[qeng].append(("wait", self.sems[(q, l)], total))
                self.waited[qeng][(q, l)] = max(
                    self.waited[qeng].get((q, l), 0), total)
                self.fcover[(q, l)] = total
                made = True
        if made:
            self.fcount[q] += 1
            self.ops[qeng].append(("inc", self.sems["f" + q], 1))

    def op(self, eng, fn, deps=()):
        for d in deps:
            self._wait(eng, d)
        self.count[eng] += 1
        self.ops[eng].append(("op", fn, self.sems[eng]))
        return (eng, self.count[eng])

    def _lane_pick(self, q):
        l = self.rr[q] % (1 if q == "cc" else self.NLANES)
        self.rr[q] += 1
        return l

    def dma(self, queue, out, in_, deps=(), **kw):
        eng = self.QENG[queue]
        for d in deps:
            self._wait(eng, d)
        l = self._lane_pick(queue)
        self.lane[(queue, l)] += 16
        self.ops[eng].append(("dma", (out, in_, kw), self.sems[(queue, l)]))
        return ((queue, l), self.lane[(queue, l)])

    def indirect(self, out, out_off, in_, in_off, bounds, deps=(), compute_op=None):
        for d in deps:
            self._wait("pool", d)
        l = self._lane_pick("qpool")
        self.lane[("qpool", l)] += 16
        kw = {"compute_op": compute_op} if compute_op is not None else {}
        self.ops["pool"].append(
            ("ind", (out, out_off, in_, in_off, bounds, kw),
             self.sems[("qpool", l)]))
        return (("qpool", l), self.lane[("qpool", l)])

    def collective(self, kind, op, ins, outs, deps=()):
        for d in deps:
            self._wait("pool", d)
        self.lane[("cc", 0)] += 1
        self.ops["pool"].append(("cc", (kind, op, ins, outs),
                                 self.sems[("cc", 0)]))
        return (("cc", 0), self.lane[("cc", 0)])

    def finish(self, block):
        def ap(x):
            return x.ap() if not isinstance(x, bass.AP) and hasattr(x, "ap") else x

        def run(engine, stream):
            for n, item in enumerate(stream):
                kind = item[0]
                self._replay_pos = (engine, n, kind)
                if kind == "wait":
                    engine.wait_ge(item[1], item[2])
                elif kind == "inc":
                    engine.sem_inc(item[1], item[2])
                elif kind == "op":
                    item[1](engine).then_inc(item[2], 1)
                elif kind == "dma":
                    (out, in_, kw), sem = item[1], item[2]
                    engine.dma_start(out=ap(out), in_=ap(in_), **kw).then_inc(sem, 16)
                elif kind == "ind":
                    (out, oo, in_, io_, b, kw), sem = item[1], item[2]
                    if b not in self._breg:
                        r = engine.alloc_register(f"bnd{len(self._breg)}")
                        engine.reg_mov(r, b)
                        self._breg[b] = r
                    engine.indirect_dma_start(
                        out=ap(out), out_offset=oo, in_=ap(in_), in_offset=io_,
                        bounds_check=self._breg[b],
                        oob_is_err=False, **kw).then_inc(sem, 16)
                elif kind == "cc":
                    (ck, cop, ins, outs), sem = item[1], item[2]
                    engine.collective_compute(
                        ck, cop, replica_groups=[list(range(8))],
                        ins=ins, outs=outs).then_inc(sem, 1)

        @block.tensor
        def _(tensor):
            run(tensor, self.ops["te"])

        @block.vector
        def _(vector):
            run(vector, self.ops["ve"])

        @block.scalar
        def _(scalar):
            run(scalar, self.ops["act"])
            for l in range(self.NLANES):
                if self.lane[("qact", l)]:
                    scalar.wait_ge(self.sems[("qact", l)], self.lane[("qact", l)])

        @block.gpsimd
        def _(gpsimd):
            self._breg = {}
            run(gpsimd, self.ops["pool"])
            for l in range(self.NLANES):
                if self.lane[("qpool", l)]:
                    gpsimd.wait_ge(self.sems[("qpool", l)], self.lane[("qpool", l)])
            if self.lane[("cc", 0)]:
                gpsimd.wait_ge(self.sems[("cc", 0)], self.lane[("cc", 0)])

        @block.sync
        def _(sync):
            run(sync, self.ops["sp"])
            for l in range(self.NLANES):
                if self.lane[("qsp", l)]:
                    sync.wait_ge(self.sems[("qsp", l)], self.lane[("qsp", l)])


class Builder:
    def __init__(self, dm: Dims, debug=()):
        self.dm = dm
        self.debug = set(debug)
        self.nc = bass.Bass("TRN2", target_bir_lowering=False)
        self.ctx = ExitStack()
        self.p = Prog(self.nc)
        self.io = {}
        self.dbg_names = []

    def ein(self, name, shape, dt=F32):
        t = self.nc.dram_tensor(name, shape, dt, kind="ExternalInput")
        self.io[name] = t
        return t

    def eout(self, name, shape, dt=F32):
        t = self.nc.dram_tensor(name, shape, dt, kind="ExternalOutput")
        self.io[name] = t
        return t

    def idram(self, name, shape, dt):
        t = self.nc.dram_tensor(name, shape, dt)
        self.io[name] = t
        return t

    def sb(self, name, shape, dt):
        return self.ctx.enter_context(self.nc.sbuf_tensor(name, shape, dt))

    def ps(self, name, shape, dt=F32):
        return self.ctx.enter_context(self.nc.psum_tensor(name, shape, dt))

    def want(self, name):
        return name in self.debug

    def dump(self, name, shape, dt, src_fn, deps):
        if not self.want(name):
            return None
        ap = self.eout("dbg_" + name, shape, dt)
        self.dbg_names.append("dbg_" + name)
        return self.p.dma("qact", ap, src_fn(), deps=deps)

    def build(self, upto="all"):
        dm = self.dm
        p = self.p
        with self.ctx:
            p.make_sems(self.ctx)
            self.declare_io()
            self.alloc_consts()
            NSQ = dm.CAP // min(512, dm.CAP)
            H = dm.H
            self.PARTB_AT = min(3, H - 2)
            self.PARTC_AT = min(5, H - 1)
            self.SCAT_START = self.PARTB_AT + 1
            self.MOE_AT = {min(7, H - 1): list(range(NSQ))}
            self.AG_AT = min(9, H - 1)
            with self.nc.Block() as block:
                self.emit_consts()
                self.phase_ln_router()
                self.phase_routing()
                if upto in ("p1", "p2"):
                    self.emit_routing_b()
                    self.emit_routing_c()
                if upto != "p1":
                    self.phase_moe_scatter()
                if upto == "p2":
                    self.emit_scatter_groups(10 ** 6)
                    ep = {}
                    pv = None
                    for k in range(NSQ):
                        ht, yt = self.emit_moe_chunk(k, ep, pv)
                        ep = {0: ht, 1: ht}
                        pv = yt
                    self.emit_moe_ag()
                if upto not in ("p1", "p2"):
                    self.phase_attn_l0()
                    self.phase_wout_l0()
                    self.phase_combine_residual()
                    self.phase_attn_l1()
                    self.phase_wout_l1()
                    self.phase_final()
                self.p.finish(block)
        return self.nc

    def declare_io(self):
        dm = self.dm
        T, D, NE, TG = dm.T, dm.D, dm.NE, dm.TG
        self.x_in = self.ein("x", [T, D])
        self.rw_in = self.ein("rw", [NE, D])
        self.rb_in = self.ein("rb", [1, NE])
        self.rmask_in = self.ein("rank_mask", [8, 1])
        self.esel_in = self.ein("esel", [128, 8])
        self.slmidx_in = self.ein("slmidx", [128, 1])
        self.mw_in = self.ein("mw", [D, dm.DH])
        self.mv_in = self.ein("mv", [D, dm.DH])
        self.mwo_in = self.ein("mwo", [dm.DH, D])
        self.wqkv_in = self.ein("wqkv", [D, 3 * D])
        self.wout_in = self.ein("wout", [D, D])
        self.l1g_in = self.ein("l1g", [1, D])
        self.l1b_in = self.ein("l1b", [1, D])
        self.wqkv1_in = self.ein("wqkv1", [D, 3 * D])
        self.wout1_in = self.ein("wout1", [D, D])
        self.gw_in = self.ein("gw", [NE, D])
        self.wlin_in = self.ein("wlin", [D, D])
        self.blin_in = self.ein("blin", [1, D])
        self.left_in = self.ein("left_all", [D, NE * dm.R])
        self.right_in = self.ein("right_all", [NE * dm.R, D])
        self.out = self.eout("out", [T, D])
        self.cc_xn_in = self.idram("cc_xn_in", [T, D], BF16)
        self.cc_xn_out = self.idram("cc_xn_out", [TG, D], BF16)
        self.cc_tot_in = self.idram("cc_tot_in", [1, 16], F32)
        self.cc_tot_out = self.idram("cc_tot_out", [8, 16], F32)
        self.cc_sl_in = self.idram("cc_sl_in", [16, T], F32)
        self.cc_sl_out = self.idram("cc_sl_out", [128, T], F32)
        self.ind_dram = self.idram("ind_dram", [T, 128], BF16)
        self.slm_dram = self.idram("slm_dram", [1, TG], F32)
        self.offs_dram = self.idram("offs_dram", [1, 16], F32)
        self.woutbf = self.idram("woutbf", [D, D], BF16)
        self.wout1bf = self.idram("wout1bf", [D, D], BF16)
        self.wqkv1bf = self.idram("wqkv1bf", [D, 3 * D], BF16)
        self.wlinbf = self.idram("wlinbf", [D, D], BF16)
        self.brow_dram = self.idram("brow_dram", [1, 3 * D], BF16)
        self.x1_dram = self.idram("x1_dram", [T, D], F32)
        self.xn1_dram = self.idram("xn1_dram", [T, D], BF16)
        self.wbf = self.idram("wbf", [dm.DH, D], BF16)
        self.wqkvbf = self.idram("wqkvbf", [D, 3 * D], BF16)
        self.vbf = self.idram("vbf", [dm.DH, D], BF16)
        self.wobf = self.idram("wobf", [dm.DH, D], BF16)
        self.disp = self.idram("disp", [dm.CAP, D], BF16)
        self.cc_y_in = self.idram("cc_y_in", [dm.CAP, D], BF16)
        self.cc_y_out = self.idram("cc_y_out", [8 * dm.CAP, D], BF16)

    def alloc_consts(self):
        dm = self.dm
        T, D, NE, NT, ND = dm.T, dm.D, dm.NE, dm.NT, dm.ND
        sb = self.sb
        self.identF = sb("identF", [128, 128], F32)
        self.identB = sb("identB", [128, 128], BF16)
        self.iotaE = sb("iotaE", [128, NE], F32)
        self.e2048 = sb("e2048", [128, 16], F32)
        self.onesK1_f = sb("onesK1_f", [1, 128], F32)
        self.onesK1_b = sb("onesK1_b", [1, 128], BF16)
        self.ones_col_b = sb("ones_col_b", [128, 1], BF16)
        self.xnT_bf = sb("xnT_bf", [128, ND * T], BF16)
        self.logits = sb("logits", [128, NT * NE], F32)
        self.p1c = sb("p1c", [128, NT], F32)
        self.p2c = sb("p2c", [128, NT], F32)
        self.oh1 = sb("oh1", [128, NT * NE], BF16)
        self.oh2 = sb("oh2", [128, NT * NE], BF16)
        self.flat1 = sb("flat1", [128, NT], I32)
        self.flat2 = sb("flat2", [128, NT], I32)
        self.xs = [sb("xs0", [128, D], F32)]
        self.xs.append(self.xs[0])
        self.stat = sb("stat", [128, 16], F32)
        self.xn_f = [sb("xn_f0", [128, D], F32)]
        self.xn_f.append(self.xn_f[0])
        self.xnT_st = sb("xnT_st", [128, D], F32)
        self.xn_bf_t = sb("xn_bf_t", [128, D], BF16)
        self.rwT = sb("rwT", [128, ND * NE], F32)
        self.rb_sb = sb("rb_sb", [1, NE], F32)
        self.rmask_sb = sb("rmask_sb", [8, 1], F32)
        self.esel_sb = sb("esel_sb", [128, 8], F32)
        self.tsc = [sb(f"tsc{i}", [128, NE], F32) for i in range(4)]
        self.vcol = [sb(f"vcol{i}", [128, 1], F32) for i in range(6)]
        self.totAG = sb("totAG", [8, 16], F32)
        self.offs_c = sb("offs_c", [16, 1], F32)
        self.slots_tok = sb("slots_tok", [128, NT * NE], F32)
        self.myslot_i = sb("myslot_i", [128, dm.Q], I32)
        self.slmidx_sb = sb("slmidx_sb", [128, 1], F32)
        self.slmidx_i = sb("slmidx_i", [128, 1], I32)
        rt = ExitStack()
        self.seqF = rt.enter_context(self.nc.sbuf_tensor("seqF", [16, T], F32))
        self.slchunk = rt.enter_context(
            self.nc.sbuf_tensor("slchunk", [128, max(512, dm.Q)], F32))
        self.myslot_f = self.slchunk
        self.rw_sb = rt.enter_context(
            self.nc.sbuf_tensor("rw_sb", [NE, 128], F32))
        rt.close()
        SQW = min(512, dm.CAP)
        NHT = dm.DH // 128
        DCW = min(512, D)
        carve = {}
        off = 0
        for nm, sz in (("xinT", ND * SQW), ("h_buf", NHT * SQW),
                       ("w0", ND * 128), ("w1", ND * 128),
                       ("v0", ND * 128), ("v1", ND * 128)):
            carve[nm] = off
            off += sz
        self.carve = carve
        NS = dm.H * 64 // 128
        if off <= (NS // 2) * T:
            self.OT_BASE = 0
            self.moe_perm = lambda a, NS=NS: (a + NS // 2) % NS
            bufB_sz = max(ND * T, 4 * T)
        else:
            self.OT_BASE = off
            self.moe_perm = lambda a: a
            bufB_sz = max(off + NS * T, ND * T, 4 * T)
        self.bufB = sb("bufB", [128, bufB_sz], BF16)
        self.qT = sb("qT", [64, T], BF16)
        self.kT = sb("kT", [64, T], BF16)
        self.v_h = sb("v_h", [128, NT * 65], BF16)
        self.l65 = sb("l65", [65, 512], F32)
        self.ones65 = sb("ones65", [65, 128], F32)
        self.srow = self.l65
        self.phat = [sb(f"phat{i}", [128, 512], BF16) for i in range(2)]
        self.hsc = sb("hsc", [128, 512], BF16)
        self.ysb = sb("ysb", [128, 512], BF16)
        MM = T // 128 - 1
        self.MM = MM
        self.Tbase = sb("Tbase", [128, min(512, T)], F16)
        self.biascols = sb("biascols", [128, dm.H * (MM + 1)], F32)
        self.mpcols = sb("mpcols", [128, MM + 1], F32)
        self.m16 = sb("m16", [128, 1], F32)
        QW_ = min(512, T)
        BASEW = 128 * (QW_ // 128 - 1)
        self.dmaskG = sb("dmaskG", [128, QW_ + BASEW], BF16)
        self.dmask_off = [BASEW - 128 * m for m in range(QW_ // 128)]
        self.BASEW = BASEW
        self.bcst = sb("bcst", [128, 512], F32)
        self.wq_sb = [sb("wq_sb0", [128, ND * 192], BF16)]
        self.wq_sb.append(self.wq_sb[0])
        self.wrhs = [sb("wrhs0", [128, (D // 128) * min(512, D)], BF16)]
        self.wrhs.append(self.wrhs[0])
        self.gA = sb("gA", [128, D], BF16)
        self.xt1 = self.gA
        self.gB = sb("gB", [128, D], BF16)
        self.zbf = sb("zbf", [128, D], BF16)
        self.gwT = sb("gwT", [128, ND * NE], F32)
        self.wsel1 = sb("wsel1", [128, NT * NE], F32)
        self.gcols = sb("gcols", [128, ND], F32)
        self.bcols = sb("bcols", [128, ND], F32)
        self.bcols_b = sb("bcols_b", [128, ND], BF16)
        self.brow_h = sb("brow_h", [1, 192], BF16)
        self.ones_row_b = sb("ones_row_b", [1, 512], BF16)
        self.lmask = sb("lmask", [128, 256], BF16)
        self.t2_sb = sb("t2_sb", [32, 128], BF16)
        self.wselT = sb("wselT", [8, 128], BF16)
        self.wexpT = sb("wexpT", [32, 128], BF16)
        self.E832 = sb("E832", [8, 32], BF16)
        self.blin_sb = sb("blin_sb", [1, min(512, D)], BF16)
        self.right_sb = sb("right_sb", [32, min(512, D)], BF16)
        self.left_sb = sb("left_sb", [128, ND * 32], BF16)
        self.psb = [self.ps(f"psb{i}", [128, 512], F32) for i in range(6)]
        self.psT = self.ps("psT", [128, 256], BF16)
        self.psS = self.ps("psS", [128, 256], F32)

    def emit_consts(self):
        p, dm = self.p, self.dm
        NE = dm.NE

        def aff_ident(tile):
            t = p.op("pool", lambda e, _t=tile: e.memset(_t[:], 1.0))
            return p.op("pool", lambda e, _t=tile: e.affine_select(
                out=_t[:], in_=_t[:], pattern=[[-1, 128]],
                compare_op=OP.is_equal, fill=0.0, base=0, channel_multiplier=1),
                deps=[t])

        self.t_identB = aff_ident(self.identB)
        self.t_identF = aff_ident(self.identF)
        self.t_iotaE = p.op("pool", lambda e: e.iota(
            self.iotaE[:], [[1, NE]], base=0, channel_multiplier=0,
            allow_small_or_imprecise_dtypes=True))
        self.t_e2048 = p.op("pool", lambda e: e.iota(
            self.e2048[:], [[dm.CAP, 16]], base=0, channel_multiplier=0,
            allow_small_or_imprecise_dtypes=True))
        self.t_onesK1f = p.op("pool", lambda e: e.memset(self.onesK1_f[:], 1.0))
        self.t_onesK1b = p.op("pool", lambda e: e.memset(self.onesK1_b[:], 1.0))
        self.t_onescb = p.op("pool", lambda e: e.memset(self.ones_col_b[:], 1.0))
        self.t_vones = p.op("pool", lambda e: e.memset(self.v_h[:], 1.0))
        self.t_o65 = p.op("pool", lambda e: e.memset(self.ones65[:], 1.0))
        self.t_z16 = None
        QW = min(512, dm.T)
        self.QW = QW
        t_tb = p.op("pool", lambda e, QW=QW: e.iota(
            self.Tbase[:, 0:QW], [[-1, QW]], base=0,
            channel_multiplier=1, allow_small_or_imprecise_dtypes=True))
        self.t_Tbase = t_tb
        self.t_m16 = p.op("pool", lambda e: e.memset(self.m16[:], -C_STAB))
        MM1 = self.MM + 1
        t_mp = p.op("pool", lambda e, MM1=MM1: e.iota(
            self.mpcols[:, 0:MM1], [[128, MM1]], base=0,
            channel_multiplier=0, allow_small_or_imprecise_dtypes=True))
        tb = t_mp
        for h in range(dm.H):
            sl = dm.slopes[h]
            tb = p.op("act", lambda e, h=h, sl=sl, MM1=MM1: e.activation(
                out=self.biascols[:, h * MM1:(h + 1) * MM1],
                in_=self.mpcols[:, 0:MM1], func=AF.Copy, scale=-sl),
                deps=[t_mp])
        t_bc2 = p.op("ve", lambda e: e.tensor_scalar(
            self.biascols[:], self.biascols[:], -C_STAB, None, OP.add),
            deps=[tb])
        self.t_biascols = t_bc2
        WG = QW + self.BASEW
        tz = p.op("pool", lambda e, WG=WG: e.memset(self.dmaskG[:, 0:WG], 0.0))
        t_dg = p.op("pool", lambda e, WG=WG: e.affine_select(
            out=self.dmaskG[:, 0:WG], in_=self.dmaskG[:, 0:WG],
            pattern=[[-1, WG]], compare_op=OP.is_ge, fill=NEGBIG,
            base=self.BASEW, channel_multiplier=1), deps=[tz])
        self.t_dmask = [t_dg] * (QW // 128)
        self.t_rb = p.dma("qsp", self.rb_sb[:], self.rb_in[:])
        self.t_rmask = p.dma("qsp", self.rmask_sb[:], self.rmask_in[:])
        self.t_esel = p.dma("qsp", self.esel_sb[:], self.esel_in[:])
        self.t_slmidx = p.dma("qsp", self.slmidx_sb[:], self.slmidx_in[:])
        tprev = None
        for src_t, dst in ((self.rw_in, self.rwT), (self.gw_in, self.gwT)):
            for c in range(dm.ND):
                t_rw = p.dma("qsp", self.rw_sb[:],
                             src_t[0:NE, c * 128:(c + 1) * 128],
                             deps=([tprev] if tprev else []))
                tt = p.op("te", lambda e: e.transpose(
                    self.psS[0:128, 0:NE], self.rw_sb[0:NE, 0:128],
                    self.identF[0:NE, 0:NE]),
                    deps=[t_rw, self.t_identF] + ([tprev] if tprev else []))
                tprev = p.op("ve", lambda e, c=c, dst=dst: e.tensor_copy(
                    dst[:, c * NE:(c + 1) * NE], self.psS[0:128, 0:NE]),
                    deps=[tt])
        self.t_rwT = tprev
        self.t_gwT = tprev
        self.t_orow = p.op("pool", lambda e: e.memset(self.ones_row_b[:], 1.0))
        tE = p.op("pool", lambda e: e.memset(self.E832[:], 1.0))
        self.t_E832 = p.op("pool", lambda e: e.affine_select(
            out=self.E832[:], in_=self.E832[:], pattern=[[-1, 8], [0, dm.R]],
            compare_op=OP.is_equal, fill=0.0, base=0, channel_multiplier=1),
            deps=[tE])
        tm = p.op("pool", lambda e: e.memset(self.lmask[:], 0.0))
        tm = p.op("pool", lambda e: e.affine_select(
            out=self.lmask[:, 0:128], in_=self.lmask[:, 0:128],
            pattern=[[-1, 128]], compare_op=OP.is_ge, fill=8.0 * NEGBIG,
            base=0, channel_multiplier=1), deps=[tm])
        self.t_lmask = p.op("pool", lambda e: e.affine_select(
            out=self.lmask[:, 128:256], in_=self.lmask[:, 128:256],
            pattern=[[1, 128]], compare_op=OP.is_ge, fill=8.0 * NEGBIG,
            base=0, channel_multiplier=-1), deps=[tm])
        leftr = self.left_in.ap().rearrange("(c p) r -> p c r", p=128)
        self.t_left = p.dma("qpool", self.left_sb.ap().rearrange(
            "p (c r) -> p c r", r=32), leftr)
        self.t_gcols = []
        self.t_bcols = []
        for c in range(dm.ND):
            gc = self.l1g_in.ap().rearrange("a (c p) -> (a c) p", p=128)
            self.t_gcols.append(p.dma("qsp", self.gcols[:, c:c + 1],
                                gc[c:c + 1, :].rearrange("a p -> p a")))
            bc = self.l1b_in.ap().rearrange("a (c p) -> (a c) p", p=128)
            self.t_bcols.append(p.dma("qsp", self.bcols[:, c:c + 1],
                                bc[c:c + 1, :].rearrange("a p -> p a")))
        self.t_bcolsb = p.op("ve", lambda e: e.tensor_copy(
            self.bcols_b[:], self.bcols[:]), deps=self.t_bcols)
        self.t_wqkvc = p.dma("qpool", self.wqkvbf[:], self.wqkv_in[:])
        mwr = self.mw_in.ap().rearrange("(c p) h -> p c h", p=128)
        mvr = self.mv_in.ap().rearrange("(c p) h -> p c h", p=128)
        self.t_wc = []
        self.t_vc = []
        NHT_ = dm.DH // 128
        for t_ in range(NHT_):
            self.t_wc.append(p.dma(
                "qpool", self.wbf[t_ * 128:(t_ + 1) * 128, :],
                mwr[:, :, t_ * 128:(t_ + 1) * 128]))
            self.t_vc.append(p.dma(
                "qpool", self.vbf[t_ * 128:(t_ + 1) * 128, :],
                mvr[:, :, t_ * 128:(t_ + 1) * 128]))
        self.t_oc = p.dma("qpool", self.wobf[:], self.mwo_in[:])
        self.t_wq1c = p.dma("qpool", self.wqkv1bf[:], self.wqkv1_in[:])
        self.t_wo1c = p.dma("qpool", self.wout1bf[:], self.wout1_in[:])
        self.t_wlc = p.dma("qpool", self.wlinbf[:], self.wlin_in[:])
        p.fence_queue("qpool")

    def phase_ln_router(self):
        p, dm = self.p, self.dm
        T, D, NE, NT, ND = dm.T, dm.D, dm.NE, dm.NT, dm.ND
        eps = 1e-5
        xn_done = []
        logit_done = []
        self.t_xnbf = []
        xs, xn_f = self.xs[0], self.xn_f[0]
        st = lambda b, k: self.stat[:, b * 8 + k:b * 8 + k + 1]
        stA = {}
        stB = {}

        def stage_a(i):
            b = i % 2
            t_x = p.dma("qsp", xs[:], self.x_in[i * 128:(i + 1) * 128, :],
                        deps=stA.get("xs_war", []))
            t_s1 = p.op("ve", lambda e, b=b: e.reduce_sum(
                st(b, 0), xs[:], axis=AX.X), deps=[t_x])
            t_sq = p.op("act", lambda e, b=b: e.activation(
                out=self.xn_bf_t[:], in_=xs[:], func=AF.Square,
                accum_out=st(b, 1)), deps=[t_x] + stA.get("sq_war", []))
            t_m = p.op("ve", lambda e, b=b: e.tensor_scalar(
                st(b, 2), st(b, 0), 1.0 / D, None, OP.mult), deps=[t_s1])
            t_ex2 = p.op("ve", lambda e, b=b: e.tensor_scalar(
                st(b, 3), st(b, 1), 1.0 / D, eps, OP.mult, OP.add),
                deps=[t_sq])
            t_m2 = p.op("ve", lambda e, b=b: e.tensor_tensor(
                st(b, 4), st(b, 2), st(b, 2), OP.mult), deps=[t_m])
            t_var = p.op("ve", lambda e, b=b: e.tensor_tensor(
                st(b, 5), st(b, 3), st(b, 4), OP.subtract),
                deps=[t_ex2, t_m2])
            t_sd = p.op("act", lambda e, b=b: e.activation(
                out=st(b, 6), in_=st(b, 5), func=AF.Ln), deps=[t_var])
            t_rstd = p.op("act", lambda e, b=b: e.activation(
                out=st(b, 5), in_=st(b, 6), func=AF.Exp, scale=-0.5),
                deps=[t_sd])
            t_nmr = p.op("ve", lambda e, b=b: e.scalar_tensor_tensor(
                st(b, 4), st(b, 2), -1.0, st(b, 5), OP.mult, OP.mult),
                deps=[t_rstd, t_m])
            stA[("done", i)] = (t_x, t_nmr, t_sq)

        def stage_b(i):
            b = i % 2
            t_x, t_nmr, t_sq = stA.pop(("done", i))
            t_xn = p.op("ve", lambda e, b=b: e.tensor_scalar(
                xn_f[:], xs[:], st(b, 5), st(b, 4), OP.mult, OP.add),
                deps=[t_nmr, t_x] + stB.get("xnf_war", []))
            t_cb = p.op("ve", lambda e: e.tensor_copy(
                self.xn_bf_t[:], xn_f[:]),
                deps=[t_xn, t_sq] + (xn_done[-1:] if xn_done else []))
            t_st = p.dma("qact", self.cc_xn_in[i * 128:(i + 1) * 128, :],
                         self.xn_bf_t[:], deps=[t_cb])
            xn_done.append(t_st)
            stB[i] = t_xn
            stA["xs_war"] = [t_xn, t_sq]
            stA["sq_war"] = [t_st]

        def stage_c(i):
            t_xn = stB.pop(i)
            t_rmm = None
            psev = getattr(self, "_psS_ev", {})
            t_bf = None
            last_tt = None
            for c in range(ND):
                sl_ = c % 2
                tt = p.op("te", lambda e, c=c, sl_=sl_: e.transpose(
                    self.psS[:, sl_ * 128:(sl_ + 1) * 128],
                    xn_f[:, c * 128:(c + 1) * 128],
                    self.identF[:]),
                    deps=[t_xn, self.t_identF] +
                         ([psev[sl_]] if sl_ in psev else []))
                last_tt = tt
                tev = p.op("ve", lambda e, c=c, sl_=sl_: e.tensor_copy(
                    self.xnT_st[:, c * 128:(c + 1) * 128],
                    self.psS[:, sl_ * 128:(sl_ + 1) * 128]),
                    deps=[tt])
                psev[sl_] = tev
                self._psS_ev = psev
                self._t_psS_last = tev
                t_bf = p.op("ve", lambda e, c=c, i=i: e.tensor_copy(
                    self.xnT_bf[:, c * T + i * 128:c * T + (i + 1) * 128],
                    self.xnT_st[:, c * 128:(c + 1) * 128]), deps=[tev])
                rdeps = [tev, self.t_rwT]
                if c == 0 and logit_done:
                    rdeps.append(logit_done[-1])
                t_rmm = p.op("te", lambda e, c=c, i=i: e.matmul(
                    self.psb[3][0:128, 0:NE],
                    self.xnT_st[:, c * 128:(c + 1) * 128],
                    self.rwT[:, c * NE:(c + 1) * NE],
                    start=(c == 0), stop=False), deps=rdeps)
            t_rmm = p.op("te", lambda e: e.matmul(
                self.psb[3][0:128, 0:NE], self.onesK1_f[:], self.rb_sb[:],
                start=False, stop=True),
                deps=[t_rmm, self.t_onesK1f, self.t_rb])
            t_lg = p.op("ve", lambda e, i=i: e.tensor_copy(
                self.logits[:, i * NE:(i + 1) * NE], self.psb[3][0:128, 0:NE]),
                deps=[t_rmm])
            logit_done.append(t_lg)
            self.t_xnbf.append(t_bf)
            stB["xnf_war"] = [last_tt, t_bf]

        stage_a(0)
        stage_b(0)
        for i in range(NT):
            stage_c(i)
            if i + 1 < NT:
                stage_a(i + 1)
                stage_b(i + 1)
        self.t_xn_all = xn_done
        self.t_logits = logit_done
        self.t_ag_xn = p.collective("AllGather", OP.bypass,
                                    [self.cc_xn_in.ap().opt()],
                                    [self.cc_xn_out.ap().opt()],
                                    deps=xn_done)
        self.dump("logits", [128, NT * NE], F32, lambda: self.logits[:],
                  logit_done)
        self.dump("xnT_bf", [128, ND * T], BF16, lambda: self.xnT_bf[:],
                  logit_done)

    def phase_routing(self):
        p, dm = self.p, self.dm
        T, D, NE, NT = dm.T, dm.D, dm.NE, dm.NT
        CAP, TG, Q = dm.CAP, dm.TG, dm.Q
        a, b2, c3, d4 = self.tsc
        mx, mx2, icol, dcol, pcol, qcol = self.vcol
        ind_st = []
        for i in range(NT):
            lg = self.logits[:, i * NE:(i + 1) * NE]
            dep0 = [self.t_logits[i]]
            t1 = p.op("ve", lambda e, lg=lg: e.reduce_max(mx[:], lg, axis=AX.X),
                      deps=dep0)
            teq = p.op("ve", lambda e, lg=lg: e.tensor_scalar(
                a[:], lg, mx[:], None, OP.is_equal), deps=[t1])
            trv = p.op("ve", lambda e: e.scalar_tensor_tensor(
                b2[:], self.iotaE[:], -1.0, a[:], OP.mult, OP.mult),
                deps=[teq, self.t_iotaE])
            trv2 = p.op("ve", lambda e: e.scalar_tensor_tensor(
                b2[:], a[:], float(NE), b2[:], OP.mult, OP.add), deps=[trv])
            tmx = p.op("ve", lambda e: e.reduce_max(icol[:], b2[:], axis=AX.X),
                       deps=[trv2])
            ti1 = p.op("ve", lambda e: e.tensor_scalar(
                icol[:], icol[:], -1.0, float(NE), OP.mult, OP.add), deps=[tmx])
            to1 = p.op("ve", lambda e, i=i: e.tensor_scalar(
                self.oh1[:, i * NE:(i + 1) * NE], self.iotaE[:], icol[:], None,
                OP.is_equal), deps=[ti1])
            tml = p.op("ve", lambda e, lg=lg, i=i: e.scalar_tensor_tensor(
                c3[:], self.oh1[:, i * NE:(i + 1) * NE], float(NEGBIG), lg,
                OP.mult, OP.add), deps=[to1])
            t2 = p.op("ve", lambda e: e.reduce_max(mx2[:], c3[:], axis=AX.X),
                      deps=[tml])
            teq2 = p.op("ve", lambda e: e.tensor_scalar(
                a[:], c3[:], mx2[:], None, OP.is_equal), deps=[t2])
            trv3 = p.op("ve", lambda e: e.scalar_tensor_tensor(
                b2[:], self.iotaE[:], -1.0, a[:], OP.mult, OP.mult), deps=[teq2])
            trv4 = p.op("ve", lambda e: e.scalar_tensor_tensor(
                b2[:], a[:], float(NE), b2[:], OP.mult, OP.add), deps=[trv3])
            tmx2 = p.op("ve", lambda e: e.reduce_max(dcol[:], b2[:], axis=AX.X),
                        deps=[trv4])
            ti2 = p.op("ve", lambda e: e.tensor_scalar(
                dcol[:], dcol[:], -1.0, float(NE), OP.mult, OP.add), deps=[tmx2])
            to2 = p.op("ve", lambda e, i=i: e.tensor_scalar(
                self.oh2[:, i * NE:(i + 1) * NE], self.iotaE[:], dcol[:], None,
                OP.is_equal), deps=[ti2])
            tdv = p.op("ve", lambda e: e.tensor_tensor(
                pcol[:], mx[:], mx2[:], OP.subtract), deps=[t1, t2])
            tp1 = p.op("act", lambda e, i=i: e.activation(
                out=self.p1c[:, i:i + 1], in_=pcol[:], func=AF.Exp,
                scale=-1.0), deps=[tdv])
            tp1 = p.op("ve", lambda e, i=i: e.tensor_scalar(
                self.p1c[:, i:i + 1], self.p1c[:, i:i + 1], 1.0, None, OP.add),
                deps=[tp1])
            tp1 = p.op("ve", lambda e, i=i: e.reciprocal(
                self.p1c[:, i:i + 1], self.p1c[:, i:i + 1]), deps=[tp1])
            tp2 = p.op("ve", lambda e, i=i: e.tensor_scalar(
                self.p2c[:, i:i + 1], self.p1c[:, i:i + 1], -1.0, 1.0,
                OP.mult, OP.add), deps=[tp1])
            tind = p.op("ve", lambda e, i=i: e.tensor_tensor(
                d4[:], self.oh1[:, i * NE:(i + 1) * NE],
                self.oh2[:, i * NE:(i + 1) * NE], OP.add), deps=[to1, to2])
            if not ind_st:
                tz = p.op("ve", lambda e: e.memset(self.xn_bf_t[0:128, 0:128], 0.0),
                          deps=list(self.t_xn_all[-1:]))
            else:
                tz = ind_st[-1][0]
            tcv = p.op("ve", lambda e: e.tensor_copy(
                self.xn_bf_t[0:128, 0:8], d4[:]), deps=[tind, tz])
            tst = p.dma("qact", self.ind_dram[i * 128:(i + 1) * 128, :],
                        self.xn_bf_t[0:128, 0:128], deps=[tcv])
            ind_st.append((tst, tp2))
        self.t_z16 = p.op("ve", lambda e: e.memset(
            self.bufB[0:16, 3 * T:4 * T], 0.0))
        cum = self.seqF[0:16, 0:T]
        t_ld = p.dma("qact", self.bufB[0:128, 2 * T:3 * T], self.ind_dram[:],
                     deps=[t for t, _ in ind_st], transpose=True)
        t_scan = p.op("ve", lambda e, cum=cum: e.tensor_tensor_scan(
            cum, self.bufB[0:16, 2 * T:3 * T], self.bufB[0:16, 3 * T:4 * T],
            0.0, OP.add, OP.add), deps=[t_ld, self.t_z16])
        t_tot = p.dma("qact", self.cc_tot_in[0:1, 0:16],
                      self.seqF[0:16, T - 1:T], deps=[t_scan])
        t_agt = p.collective("AllGather", OP.bypass,
                             [self.cc_tot_in.ap().opt()],
                             [self.cc_tot_out.ap().opt()], deps=[t_tot])
        kmask = self.bufB[0:16, 3 * T:4 * T]
        self._rb = dict(t_agt=t_agt, t_scan=t_scan, t_tot=t_tot, cum=cum,
                        kmask=kmask)
        return

    def emit_routing_b(self):
        p, dm = self.p, self.dm
        T, NE, NT, CAP = dm.T, dm.NE, dm.NT, dm.CAP
        t_agt = self._rb["t_agt"]
        t_scan = self._rb["t_scan"]
        t_tot = self._rb["t_tot"]
        cum = self._rb["cum"]
        kmask = self._rb["kmask"]
        t_ldt = p.dma("qsp", self.totAG[:], self.cc_tot_out[:], deps=[t_agt])
        t_off = p.op("te", lambda e: e.matmul(
            self.psS[0:1, 0:16], self.rmask_sb[:], self.totAG[:],
            start=True, stop=True), deps=[t_ldt, self.t_rmask])
        t_offr = p.op("ve", lambda e: e.tensor_copy(
            self.srow[0:1, 0:16], self.psS[0:1, 0:16]), deps=[t_off])
        t_offd = p.dma("qact", self.offs_dram[:], self.srow[0:1, 0:16],
                       deps=[t_offr])
        t_offc = p.dma("qsp", self.offs_c[:],
                       self.offs_dram.ap().rearrange("a b -> b a"), deps=[t_offd])
        kcol = self.vcol[0][0:16, 0:1]
        ocol = self.vcol[1][0:16, 0:1]
        t_kc = p.op("ve", lambda e, kcol=kcol: e.tensor_scalar(
            kcol, self.offs_c[:], -1.0, float(CAP) + 0.5, OP.mult, OP.add),
            deps=[t_offc])
        t_oc2 = p.op("ve", lambda e, ocol=ocol: e.tensor_scalar(
            ocol, self.offs_c[:], 1.0, -1.0 - 1.0e6, OP.mult, OP.add),
            deps=[t_offc])
        t_k0 = p.op("ve", lambda e, cum=cum, kmask=kmask, kcol=kcol: e.tensor_scalar(
            kmask, cum, kcol, None, OP.is_lt), deps=[t_kc, t_scan, t_tot])
        t_k1 = p.op("ve", lambda e, kmask=kmask: e.tensor_tensor(
            kmask, kmask, self.bufB[0:16, 2 * T:3 * T], OP.mult), deps=[t_k0])
        t_s0 = p.op("ve", lambda e, cum=cum, ocol=ocol: e.tensor_scalar(
            cum, cum, ocol, None, OP.add), deps=[t_oc2, t_k0])
        t_s1 = p.op("ve", lambda e, cum=cum, kmask=kmask: e.tensor_tensor(
            cum, cum, kmask, OP.mult), deps=[t_s0, t_k1])
        t_s2 = p.op("ve", lambda e, cum=cum: e.tensor_scalar(
            cum, cum, 1.0e6, None, OP.add), deps=[t_s1])
        self.t_slots_done = t_s2
        t_sst = p.dma("qact", self.cc_sl_in[:], self.seqF[0:16, 0:T],
                      deps=[t_s2])
        t_ags = p.collective("AllGather", OP.bypass,
                             [self.cc_sl_in.ap().opt()],
                             [self.cc_sl_out.ap().opt()], deps=[t_sst])
        self.t_ags = t_ags
        t_mi = p.op("ve", lambda e: e.tensor_copy(
            self.slmidx_i[:], self.slmidx_sb[:]), deps=[self.t_slmidx])
        slr = self.cc_sl_out.ap().rearrange("p (k q) -> (p k) q", q=dm.Q)
        t_ms = p.indirect(self.myslot_f[:, 0:dm.Q], None, slr,
                          bass.IndirectOffsetOnAxis(self.slmidx_i[:], 0),
                          128 * (T // dm.Q) - 1, deps=[t_ags, t_mi])
        self.t_myslot = p.op("ve", lambda e: e.tensor_copy(
            self.myslot_i[:], self.myslot_f[:, 0:dm.Q]), deps=[t_ms])
        self._rb["t_ms"] = t_ms
        self._rb["t_s2"] = self.t_slots_done

    def emit_routing_c(self):
        p, dm = self.p, self.dm
        T, NE, NT, CAP = dm.T, dm.NE, dm.NT, dm.CAP
        t_s2 = self._rb["t_s2"]
        t_scan = self._rb["t_scan"]
        tok_sl = []
        tprev = None
        for i in range(NT):
            tt = p.op("te", lambda e, i=i: e.transpose(
                self.psS[0:128, 0:16],
                self.seqF[0:16, i * 128:(i + 1) * 128],
                self.identF[0:16, 0:16]),
                deps=[t_s2, self.t_identF] + ([tprev] if tprev else []))
            tprev = p.op("ve", lambda e, i=i: e.tensor_copy(
                self.slots_tok[:, i * NE:(i + 1) * NE], self.psS[0:128, 0:NE]),
                deps=[tt])
            tok_sl.append(tprev)
            self._last_seqF_rd = tt
        a = self.tsc[0]
        fl_done = []
        for i in range(NT):
            tfc = p.op("ve", lambda e, i=i: e.tensor_tensor(
                a[:], self.slots_tok[:, i * NE:(i + 1) * NE], self.e2048[:, 0:NE],
                OP.add), deps=[tok_sl[i], self.t_e2048])
            tm1 = p.op("ve", lambda e, i=i: e.tensor_tensor(
                self.tsc[1][:], a[:], self.oh1[:, i * NE:(i + 1) * NE], OP.mult),
                deps=[tfc])
            tr1 = p.op("ve", lambda e: e.reduce_sum(
                self.vcol[0][:], self.tsc[1][:], axis=AX.X), deps=[tm1])
            tf1 = p.op("ve", lambda e, i=i: e.tensor_copy(
                self.flat1[:, i:i + 1], self.vcol[0][:]), deps=[tr1])
            tm2 = p.op("ve", lambda e, i=i: e.tensor_tensor(
                self.tsc[1][:], a[:], self.oh2[:, i * NE:(i + 1) * NE], OP.mult),
                deps=[tf1])
            tr2 = p.op("ve", lambda e: e.reduce_sum(
                self.vcol[0][:], self.tsc[1][:], axis=AX.X), deps=[tm2])
            tf2 = p.op("ve", lambda e, i=i: e.tensor_copy(
                self.flat2[:, i:i + 1], self.vcol[0][:]), deps=[tr2])
            fl_done.append(tf2)
        self.t_flats = fl_done
        d1 = self.dump("slotsT", [16, T], F32,
                       lambda: self.seqF[0:16, 0:T], [t_s2])
        d2 = self.dump("cum", [16, T], F32, lambda: self.seqF[0:16, 0:T],
                       [t_scan])
        self.t_seqF_done = [self._last_seqF_rd] + [t for t in (d1, d2) if t]
        self.dump("myslot", [128, dm.Q], F32,
                  lambda: self.myslot_f[:, 0:dm.Q], [self._rb["t_ms"]])
        self.dump("slots_tok", [128, NT * NE], F32, lambda: self.slots_tok[:],
                  tok_sl)
        self.dump("oh1", [128, NT * NE], BF16, lambda: self.oh1[:], fl_done)
        self.dump("oh2", [128, NT * NE], BF16, lambda: self.oh2[:], fl_done)
        self.dump("p1c", [128, NT], F32, lambda: self.p1c[:], fl_done)
        self.dump("p2c", [128, NT], F32, lambda: self.p2c[:], fl_done)
        fl32 = self.sb("fl32", [128, 2 * NT], F32)
        if self.want("flats"):
            t1 = p.op("ve", lambda e: e.tensor_copy(fl32[:, 0:NT], self.flat1[:]),
                      deps=fl_done)
            t2 = p.op("ve", lambda e: e.tensor_copy(fl32[:, NT:2 * NT], self.flat2[:]),
                      deps=[t1])
            self.dump("flats", [128, 2 * NT], F32, lambda: fl32[:], [t2])

    def phase_moe_scatter(self):
        p, dm = self.p, self.dm
        D, CAP, Q = dm.D, dm.CAP, dm.Q
        nwr = ((dm.D // 128) * min(512, dm.D)) // D
        xr_v = [lambda k=k: self.wrhs[0][:, k * D:(k + 1) * D]
                for k in range(nwr)]
        xr_v.append(lambda: self.gA[:])
        xr_v.append(lambda: self.gB[:])
        xr_v.append(lambda: self.zbf[:])
        NXR = len(xr_v)
        t_z = p.op("pool", lambda e: e.memset(xr_v[0](), 0.0))
        zd = []
        for rb in range(CAP // 128):
            zd.append(p.dma("qpool", self.disp[rb * 128:(rb + 1) * 128, :],
                            xr_v[0](), deps=[t_z]))
        xn_all_r = self.cc_xn_out.ap().rearrange("(p q) d -> p q d", q=Q)
        self._scat = dict(xr_v=xr_v, NXR=NXR, nwr=nwr, zd=zd,
                          xn_all_r=xn_all_r, prev_grp=[], last_buf_sc={},
                          next_g0=0)
        self.sc_all = []
        self.t_xr_war = {}
        self._moe = dict(prev_ab={}, y_done=[], wl={}, first_wo=True)

    def emit_scatter_groups(self, ngrp):
        p, dm = self.p, self.dm
        D, CAP, Q = dm.D, dm.CAP, dm.Q
        st_ = self._scat
        xr_v, NXR, nwr = st_["xr_v"], st_["NXR"], st_["nwr"]
        for _ in range(ngrp):
            g0 = st_["next_g0"]
            if g0 >= Q:
                break
            st_["next_g0"] = g0 + NXR
            lds = []
            for k in range(min(NXR, Q - g0)):
                j = g0 + k
                ldeps = (([self.t_ag_xn] + st_["zd"] + st_["prev_grp"])
                         if k == 0 else [])
                lds.append(p.dma("qsp", xr_v[k](),
                                 st_["xn_all_r"][:, j, :], deps=ldeps))
            grp = []
            for k in range(min(NXR, Q - g0)):
                j = g0 + k
                t_sc = p.indirect(self.disp[:], bass.IndirectOffsetOnAxis(
                    self.myslot_i[:, j:j + 1], 0), xr_v[k](), None,
                    CAP - 1, deps=[lds[k], self.t_myslot])
                grp.append(t_sc)
                st_["last_buf_sc"][k] = t_sc
            self.sc_all.extend(grp)
            st_["prev_grp"] = grp
        if st_["next_g0"] >= Q and not self.t_xr_war:
            lb = st_["last_buf_sc"]
            self.t_xr_war = {"gA": lb.get(nwr), "gB": lb.get(nwr + 1),
                             "zbf": lb.get(nwr + 2)}
            if self.want("disp"):
                dd = self.eout("dbg_disp", [dm.CAP, dm.D], BF16)
                p.dma("qpool", dd[:], self.disp[:], deps=self.sc_all)

    def emit_moe_chunk(self, sq, exp_prev, pvtok):
        p, dm = self.p, self.dm
        D, DH, CAP, ND = dm.D, dm.DH, dm.CAP, dm.ND
        SQW = min(512, CAP)
        NHT = DH // 128
        DCW = min(256, D)
        NDC = D // DCW
        cv = self.carve
        B = self.bufB
        xinT = lambda sl: B[:, cv["xinT"] + sl[0]:cv["xinT"] + sl[1]]
        h_buf = lambda sl: B[:, cv["h_buf"] + sl[0]:cv["h_buf"] + sl[1]]
        wo_v = lambda sl: self.wrhs[0][:, sl[0]:sl[1]]
        w_v = [lambda sl, o=cv["w0"]: B[:, o + sl[0]:o + sl[1]],
               lambda sl, o=cv["w1"]: B[:, o + sl[0]:o + sl[1]]]
        v_v = [lambda sl, o=cv["v0"]: B[:, o + sl[0]:o + sl[1]],
               lambda sl, o=cv["v1"]: B[:, o + sl[0]:o + sl[1]]]
        hsc = self.hsc
        ysb = self.ysb
        rbar = self.t_slots_done
        t_wcl, t_vcl, t_oc = self.t_wc, self.t_vc, self.t_oc
        st_ = self._moe
        prev_ab, y_done, wl = st_["prev_ab"], st_["y_done"], st_["wl"]
        wobf_r = self.wobf.ap().rearrange("(t p) d -> p t d", p=128)
        xt = []
        for c in range(ND):
            xdep = self.sc_all + [rbar] + getattr(self, "t_seqF_done", [])
            if "xt" in prev_ab:
                xdep.append(prev_ab["xt"])
            xt.append(p.dma("qsp", xinT((c * SQW, (c + 1) * SQW)),
                            self.disp[sq * SQW:(sq + 1) * SQW,
                                      c * 128:(c + 1) * 128],
                            deps=xdep, transpose=True))
        for ht in range(NHT):
            bsel = ht % 2
            wld = wl.get(("w", bsel))
            t_w = p.dma("qsp", w_v[bsel]((0, ND * 128)),
                        self.wbf[ht * 128:(ht + 1) * 128, :],
                        deps=[t_wcl[ht], rbar] + ([wld] if wld else []))
            t_v = p.dma("qsp", v_v[bsel]((0, ND * 128)),
                        self.vbf[ht * 128:(ht + 1) * 128, :],
                        deps=[t_vcl[ht]] + ([wld] if wld else []))
            mma = None
            mmb = None
            adeps = xt + [t_w]
            bdeps = [t_v]
            if "ab" in prev_ab:
                adeps = adeps + [prev_ab["ab"]]
                bdeps = bdeps + [prev_ab["ab"]]
            if exp_prev.get(0) is not None:
                adeps = adeps + [exp_prev[0]]
            if exp_prev.get(1) is not None:
                bdeps = bdeps + [exp_prev[1]]
            for c in range(ND):
                mma = p.op("te", lambda e, c=c, b=bsel, SQW=SQW: e.matmul(
                    self.psb[0][0:128, 0:SQW],
                    w_v[b]((c * 128, (c + 1) * 128)),
                    xinT((c * SQW, (c + 1) * SQW)),
                    start=(c == 0), stop=(c == ND - 1)),
                    deps=(adeps if c == 0 else []))
            for c in range(ND):
                mmb = p.op("te", lambda e, c=c, b=bsel, SQW=SQW: e.matmul(
                    self.psb[1][0:128, 0:SQW],
                    v_v[b]((c * 128, (c + 1) * 128)),
                    xinT((c * SQW, (c + 1) * SQW)),
                    start=(c == 0), stop=(c == ND - 1)),
                    deps=(bdeps if c == 0 else []))
            t_si = p.op("act", lambda e, SQW=SQW: e.activation(
                out=hsc[:, 0:SQW], in_=self.psb[1][0:128, 0:SQW],
                func=AF.Silu),
                deps=[mmb] + ([prev_ab["ab"]] if "ab" in prev_ab else []))
            t_h = p.op("ve", lambda e, ht=ht, SQW=SQW: e.tensor_tensor(
                h_buf((ht * SQW, (ht + 1) * SQW)), self.psb[0][0:128, 0:SQW],
                hsc[:, 0:SQW], OP.mult), deps=[mma, t_si])
            prev_ab["ab"] = t_h
            wl[("w", bsel)] = t_h
        prev_ab["xt"] = prev_ab["ab"]
        t_h_all = prev_ab["ab"]
        t_ye = None
        for dc in range(NDC):
            wod = wl.get("wo")
            wo_deps = [t_oc, rbar] + ([wod] if wod else [])
            if st_["first_wo"]:
                wo_deps += self.sc_all
                st_["first_wo"] = False
            t_wo = p.dma("qsp", wo_v((0, NHT * DCW)),
                         wobf_r[:, :, dc * DCW:(dc + 1) * DCW],
                         deps=wo_deps)
            for st in range(SQW // 128):
                pb = self.psb[2 + (st % 2)]
                mmy = None
                ydeps = [t_h_all, t_wo, self.t_logits[-1]]
                pv = prev_ab.get(("y", st % 2))
                if pv:
                    ydeps.append(pv)
                if pvtok is not None:
                    ydeps.append(pvtok)
                for ht in range(NHT):
                    mmy = p.op("te", lambda e, ht=ht, st=st, pb=pb, SQW=SQW,
                               DCW=DCW: e.matmul(
                        pb[0:128, 0:DCW],
                        h_buf((ht * SQW + st * 128, ht * SQW + (st + 1) * 128)),
                        wo_v((ht * DCW, (ht + 1) * DCW)),
                        start=(ht == 0), stop=(ht == NHT - 1)),
                        deps=(ydeps if ht == 0 else []))
                t_ye = p.op("ve", lambda e, pb=pb, DCW=DCW: e.tensor_copy(
                    ysb[:, 0:DCW], pb[0:128, 0:DCW]), deps=[mmy] +
                    (y_done[-1:] if y_done else []))
                t_yd = p.dma("qact",
                             self.cc_y_in[sq * SQW + st * 128:
                                          sq * SQW + (st + 1) * 128,
                                          dc * DCW:(dc + 1) * DCW],
                             ysb[:, 0:DCW], deps=[t_ye])
                prev_ab[("y", st % 2)] = t_ye
                y_done.append(t_yd)
            wl["wo"] = t_ye
        return prev_ab["ab"], t_ye

    def emit_moe_ag(self):
        p, dm = self.p, self.dm
        self.t_ag_y = p.collective("AllGather", OP.bypass,
                                   [self.cc_y_in.ap().opt()],
                                   [self.cc_y_out.ap().opt()],
                                   deps=self._moe["y_done"])
        if self.want("y_out"):
            yo = self.eout("dbg_y_out", [8 * dm.CAP, dm.D], BF16)
            p.dma("qpool", yo[:], self.cc_y_out[:], deps=[self.t_ag_y])

    def phase_attn_l0(self):
        p, dm = self.p, self.dm
        T, D, H, ND, NT = dm.T, dm.D, dm.H, dm.ND, dm.NT
        QW = self.QW
        NQC = T // QW
        wqkv_r = self.wqkvbf.ap().rearrange("(c p) n -> p c n", p=128)
        self.t_OT = []
        exp_prev = {}
        last_wq = {}
        last_qkT = None
        t_pe = None
        pvtok = None
        import math as _math
        ngrp_tot = (dm.Q + self._scat["NXR"] - 1) // self._scat["NXR"]
        gph = _math.ceil(ngrp_tot / max(1, min(4, H - self.SCAT_START)))
        for h in range(H):
            sl = self.dm.slopes[h]
            bsel = h % 2
            wq = self.wq_sb[bsel]
            wdep = [last_wq['w']] if 'w' in last_wq else [self.t_wqkvc]
            tws = []
            for g, off in ((0, h * 64), (1, D + h * 64), (2, 2 * D + h * 64)):
                tws.append(p.dma("qsp",
                                 wq.ap().rearrange("p (c x) -> p c x", x=192)[:, :, g * 64:(g + 1) * 64],
                                 wqkv_r[:, :, off:off + 64],
                                 deps=wdep if g == 0 else []))
            qk_war = [last_qkT] if last_qkT else []
            TCW = min(512, T)
            qkvev = {0: getattr(self, "_t_bce_prev", None), 1: t_pe}
            gi = 0
            qkev = []
            for tc in range(T // TCW):
                bk = 4 + gi % 2
                mm = None
                wdeps = [tws[0], tws[1]] + qk_war
                wdeps.append(self.t_xnbf[min((tc + 1) * (TCW // 128) - 1,
                                             NT - 1)])
                if qkvev[gi % 2] is not None:
                    wdeps.append(qkvev[gi % 2])
                for c in range(ND):
                    mm = p.op("te", lambda e, tc=tc, c=c, wq=wq,
                              TCW=TCW, bk=bk: e.matmul(
                        self.psb[bk][0:128, 0:TCW],
                        wq[:, c * 192:c * 192 + 128],
                        self.xnT_bf[:, c * T + tc * TCW:c * T + (tc + 1) * TCW],
                        start=(c == 0), stop=(c == ND - 1)),
                        deps=(wdeps if c == 0 else []))
                ev1 = p.op("ve", lambda e, tc=tc, TCW=TCW, bk=bk: e.tensor_copy(
                    self.qT[0:64, tc * TCW:(tc + 1) * TCW],
                    self.psb[bk][0:64, 0:TCW]), deps=[mm])
                qkvev[gi % 2] = p.op("ve", lambda e, tc=tc, TCW=TCW,
                                     bk=bk: e.tensor_copy(
                    self.kT[0:64, tc * TCW:(tc + 1) * TCW],
                    self.psb[bk][64:128, 0:TCW]), deps=[ev1])
                qkev.append(qkvev[gi % 2])
                gi += 1
            vev = []
            for tt in range(NT):
                bk = 4 + gi % 2
                mm = None
                wdeps = [tws[2], self.t_xnbf[tt]]
                if qkvev[gi % 2] is not None:
                    wdeps.append(qkvev[gi % 2])
                for c in range(ND):
                    mm = p.op("te", lambda e, tt=tt, c=c, wq=wq, bk=bk: e.matmul(
                        self.psb[bk][0:128, 0:64],
                        self.xnT_bf[:, c * T + tt * 128:c * T + (tt + 1) * 128],
                        wq[:, c * 192 + 128:c * 192 + 192],
                        start=(c == 0), stop=(c == ND - 1)),
                        deps=(wdeps if c == 0 else []))
                qkvev[gi % 2] = p.op("ve", lambda e, tt=tt, bk=bk: e.tensor_copy(
                    self.v_h[:, tt * 65:tt * 65 + 64], self.psb[bk][0:128, 0:64]),
                    deps=[mm, self.t_vones])
                vev.append(qkvev[gi % 2])
                gi += 1
            t_pe = qkvev[(gi - 1) % 2]
            t_qkv = p.op("ve", lambda e: e.engine_nop(),
                         deps=[qkvev[0], qkvev[1]])
            last_wq['w'] = t_qkv
            DTRUNC = int(16.0 / sl) + 1
            for qc in range(NQC):
                q0 = qc * QW
                j_lo = q0 // 128
                j_hi = min(T // 128 - 1, (q0 + QW - 1 + DTRUNC) // 128)
                first = True
                pv_mm = None
                phat_war = getattr(self, "_phat_war", {})

                def emit_S(j):
                    sb_ = j % 2
                    sdeps = [qkev[j // (TCW // 128)], qkev[qc * QW // TCW]]
                    if sb_ in exp_prev:
                        sdeps.append(exp_prev[sb_])
                    return p.op("te", lambda e, j=j, qc=qc, sb_=sb_: e.matmul(
                        self.psb[sb_][0:128, 0:QW],
                        self.kT[0:64, j * 128:(j + 1) * 128],
                        self.qT[0:64, qc * QW:(qc + 1) * QW],
                        start=True, stop=True), deps=sdeps)

                t_s = emit_S(j_lo)
                for j in range(j_lo, j_hi + 1):
                    sb_ = j % 2
                    t_s_cur = t_s
                    t_b = p.op("ve", lambda e, sb_=sb_, sl=sl: e.scalar_tensor_tensor(
                        self.psb[sb_][0:128, 0:QW], self.Tbase[:, 0:QW],
                        -8.0 * sl, self.psb[sb_][0:128, 0:QW], OP.mult, OP.add),
                        deps=[t_s_cur, self.t_Tbase])
                    m = j - q0 // 128
                    if 0 <= m < QW // 128:
                        mo = self.dmask_off[m]
                        t_b = p.op("ve", lambda e, sb_=sb_, mo=mo: e.tensor_tensor(
                            self.psb[sb_][0:128, 0:QW], self.psb[sb_][0:128, 0:QW],
                            self.dmaskG[:, mo:mo + QW], OP.add),
                            deps=[t_b, self.t_dmask[m]])
                    edeps = [t_b, self.t_biascols]
                    if sb_ in phat_war:
                        edeps.append(phat_war[sb_])
                    if pvtok is not None:
                        edeps.append(pvtok)
                    bcol = h * (self.MM + 1) + (j - q0 // 128)
                    t_e = p.op("act", lambda e, sb_=sb_, bcol=bcol: e.activation(
                        out=self.phat[sb_][:, 0:QW], in_=self.psb[sb_][0:128, 0:QW],
                        func=AF.Exp, bias=self.biascols[:, bcol:bcol + 1],
                        scale=0.125), deps=edeps)
                    exp_prev[sb_] = t_e
                    if j < j_hi:
                        t_s = emit_S(j + 1)
                    pv_mm = p.op("te", lambda e, j=j, sb_=sb_, first=first,
                                 j_hi=j_hi: e.matmul(
                        self.psb[2][0:65, 0:QW], self.v_h[:, j * 65:(j + 1) * 65],
                        self.phat[sb_][:, 0:QW],
                        start=first, stop=(j == j_hi)),
                        deps=[t_e, vev[j]])
                    phat_war[sb_] = pv_mm
                    first = False
                self._phat_war = phat_war
                pvtok = pv_mm
                t_le = p.op("ve", lambda e: e.tensor_copy(
                    self.l65[0:1, 0:QW], self.psb[2][64:65, 0:QW]),
                    deps=[pv_mm] + ([self._t_lbc_prev]
                                    if getattr(self, "_t_lbc_prev", None) else []))
                t_lr = p.op("act", lambda e: e.activation(
                    out=self.l65[0:1, 0:QW], in_=self.l65[0:1, 0:QW],
                    func=AF.Ln), deps=[t_le])
                t_lr = p.op("act", lambda e: e.activation(
                    out=self.l65[0:1, 0:QW], in_=self.l65[0:1, 0:QW],
                    func=AF.Exp, scale=-1.0), deps=[t_lr])
                bc_deps = [t_lr, self.t_o65]
                if getattr(self, "_t_bce_prev", None):
                    bc_deps.append(self._t_bce_prev)
                t_bc = p.op("te", lambda e: e.matmul(
                    self.psb[4][0:128, 0:QW], self.ones65[0:1, :],
                    self.l65[0:1, 0:QW], start=True, stop=True),
                    deps=bc_deps)
                self._t_lbc_prev = t_bc
                t_bce = p.op("ve", lambda e: e.tensor_copy(
                    self.bcst[:, 0:QW], self.psb[4][0:128, 0:QW]), deps=[t_bc])
                self._t_bce_prev = t_bce
                spb = self.OT_BASE + self.moe_perm(h // 2) * T
                t_no = p.op("ve", lambda e, h=h, qc=qc, spb=spb: e.tensor_tensor(
                    self.bufB[(h % 2) * 64:(h % 2) * 64 + 64,
                              spb + qc * QW:spb + (qc + 1) * QW],
                    self.psb[2][0:64, 0:QW], self.bcst[0:64, 0:QW], OP.mult),
                    deps=[pv_mm, t_bce])
                pvtok = t_no
                self.t_OT.append(t_no)
            last_qkT = pvtok
            if h == self.PARTB_AT:
                self.emit_routing_b()
            if h == self.PARTC_AT:
                self.emit_routing_c()
            if h >= self.SCAT_START:
                self.emit_scatter_groups(gph)
            if h in self.MOE_AT:
                for k in self.MOE_AT[h]:
                    htok, ytok = self.emit_moe_chunk(k, exp_prev, pvtok)
                    exp_prev[0] = htok
                    exp_prev[1] = htok
                    pvtok = ytok
                    last_qkT = ytok
            if h == self.AG_AT:
                self.emit_moe_ag()
        if self.want("OT"):
            ot = self.eout("dbg_OT", [128, ND * T], BF16)
            p.dma("qact", ot[:], self.bufB[:, 0:self.dm.ND * self.dm.T], deps=self.t_OT)
        if self.want("l_h"):
            lh = self.eout("dbg_l_h", [1, T], F32)
            p.dma("qact", lh[:], self.l_h[:], deps=self.t_OT)

    def phase_wout_l0(self):
        p, dm = self.p, self.dm
        T, D, H, ND, NT = dm.T, dm.D, dm.H, dm.ND, dm.NT
        NS = H * 64 // 128
        DCW = min(512, D)
        NDC = D // DCW
        t_cast = p.dma("qpool", self.woutbf[:], self.wout_in[:])
        woutbf_r = self.woutbf.ap().rearrange("(s p) n -> p s n", p=128)
        wl = {}
        ev_prev = {}
        self.t_s0 = []
        for dc in range(NDC):
            bsel = 0
            wdep = [t_cast] + ([wl["w"]] if "w" in wl else [])
            if dc == 0 and getattr(self, "_moe", None):
                wdep += [t for t in (self._moe["wl"].get("wo"),) if t]
            t_wo = p.dma("qsp", self.wrhs[bsel][:, 0:NS * DCW],
                         woutbf_r[:, :, dc * DCW:(dc + 1) * DCW], deps=wdep)
            for i in range(NT):
                pb = 2 + (i % 2)
                mm = None
                mdeps = self.t_OT + [t_wo]
                if pb in ev_prev:
                    mdeps.append(ev_prev[pb])
                for s in range(NS):
                    sb_ = self.OT_BASE + self.moe_perm(s) * T
                    mm = p.op("te", lambda e, s=s, i=i, dc=dc, pb=pb, bsel=bsel,
                              sb_=sb_: e.matmul(
                        self.psb[pb][0:128, 0:DCW],
                        self.bufB[:, sb_ + i * 128:sb_ + (i + 1) * 128],
                        self.wrhs[bsel][:, s * DCW:(s + 1) * DCW],
                        start=(s == 0), stop=(s == NS - 1)),
                        deps=(mdeps if s == 0 else []))
                t_ev = p.op("ve", lambda e, i=i, dc=dc, pb=pb: e.tensor_copy(
                    self.xnT_bf[:, i * D + dc * DCW:i * D + (dc + 1) * DCW],
                    self.psb[pb][0:128, 0:DCW]), deps=[mm])
                ev_prev[pb] = t_ev
                self.t_s0.append(t_ev)
            wl["w"] = t_ev
        if self.want("s0"):
            so = self.eout("dbg_s0", [128, NT * D], BF16)
            p.dma("qact", so[:], self.xnT_bf[:], deps=self.t_s0)

    def phase_combine_residual(self):
        p, dm = self.p, self.dm
        T, D, NE, NT, ND, R = dm.T, dm.D, dm.NE, dm.NT, dm.ND, dm.R
        eps = 1e-5
        xs, xn_f = self.xs[0], self.xn_f[0]
        g0f = self.xnT_st
        a4, b4, c4, d4 = self.tsc
        mx, mx2, icol, dcol, pcol, qcol = self.vcol
        war = {}
        for nm_ in ("gA", "gB", "zbf"):
            tkn = getattr(self, "t_xr_war", {}).get(nm_)
            if tkn is not None:
                war[nm_] = tkn
        x1_done = []
        self.t_zT = []
        t_glg = None
        for i in range(NT):
            t_x = p.dma("qsp", xs[:], self.x_in[i * 128:(i + 1) * 128, :],
                        deps=war.get("xs_list", []))
            gz1 = p.op("pool", lambda e: e.memset(self.gA[:], 0.0),
                       deps=[war.get("gA")] if "gA" in war else [])
            gz2 = p.op("pool", lambda e: e.memset(self.gB[:], 0.0),
                       deps=[war.get("gB")] if "gB" in war else [])
            tg1 = p.indirect(self.gA[:], None, self.cc_y_out[:],
                             bass.IndirectOffsetOnAxis(self.flat1[:, i:i + 1], 0),
                             8 * dm.CAP - 1,
                             deps=[self.t_ag_y, self.t_flats[i], gz1])
            tg2 = p.indirect(self.gB[:], None, self.cc_y_out[:],
                             bass.IndirectOffsetOnAxis(self.flat2[:, i:i + 1], 0),
                             8 * dm.CAP - 1, deps=[gz2, tg1])
            t_c1 = p.op("ve", lambda e, i=i: e.tensor_scalar(
                xn_f[:], self.gB[:], self.p2c[:, i:i + 1], None, OP.mult),
                deps=[tg2, war.get("xn_f"), war.get("gB2")])
            t_c2 = p.op("ve", lambda e, i=i: e.tensor_scalar(
                g0f[:], self.gA[:], self.p1c[:, i:i + 1], None, OP.mult),
                deps=[tg1, tg2, war.get("g0f")])
            t_c3 = p.op("ve", lambda e: e.tensor_tensor(
                g0f[:], g0f[:], xn_f[:], OP.add), deps=[t_c1, t_c2])
            t_g0 = p.op("act", lambda e: e.activation(
                out=g0f[:], in_=g0f[:], func=AF.Silu), deps=[t_c3])
            war["gA"] = t_c2
            war["gB"] = t_c1
            war["gB2"] = t_c3
            t_m1 = p.op("ve", lambda e, i=i: e.tensor_tensor(
                g0f[:], self.xnT_bf[:, i * D:(i + 1) * D], g0f[:], OP.mult),
                deps=[t_g0] + self.t_s0[-1:])
            t_x1 = p.op("ve", lambda e: e.tensor_tensor(
                xs[:], xs[:], g0f[:], OP.add), deps=[t_m1, t_x])
            war["g0f"] = t_x1
            t_xd = p.dma("qact", self.x1_dram[i * 128:(i + 1) * 128, :], xs[:],
                         deps=[t_x1])
            x1_done.append(t_xd)
            t_s1 = p.op("ve", lambda e: e.reduce_sum(
                self.stat[:, 0:1], xs[:], axis=AX.X), deps=[t_x1])
            t_sq = p.op("act", lambda e: e.activation(
                out=self.zbf[:], in_=xs[:], func=AF.Square,
                accum_out=self.stat[:, 1:2]),
                deps=[t_x1, war.get("zbf")])
            t_m = p.op("ve", lambda e: e.tensor_scalar(
                self.stat[:, 2:3], self.stat[:, 0:1], 1.0 / D, None, OP.mult),
                deps=[t_s1])
            t_ex2 = p.op("ve", lambda e: e.tensor_scalar(
                self.stat[:, 3:4], self.stat[:, 1:2], 1.0 / D, eps, OP.mult, OP.add),
                deps=[t_sq])
            t_m2 = p.op("ve", lambda e: e.tensor_tensor(
                self.stat[:, 4:5], self.stat[:, 2:3], self.stat[:, 2:3], OP.mult),
                deps=[t_m])
            t_var = p.op("ve", lambda e: e.tensor_tensor(
                self.stat[:, 5:6], self.stat[:, 3:4], self.stat[:, 4:5],
                OP.subtract), deps=[t_ex2, t_m2])
            t_sd = p.op("act", lambda e: e.activation(
                out=self.stat[:, 6:7], in_=self.stat[:, 5:6], func=AF.Ln),
                deps=[t_var])
            t_rstd = p.op("act", lambda e: e.activation(
                out=self.stat[:, 5:6], in_=self.stat[:, 6:7], func=AF.Exp,
                scale=-0.5), deps=[t_sd])
            t_nmr = p.op("ve", lambda e: e.scalar_tensor_tensor(
                self.stat[:, 4:5], self.stat[:, 2:3], -1.0, self.stat[:, 5:6],
                OP.mult, OP.mult), deps=[t_rstd, t_m])
            t_xn1 = p.op("ve", lambda e: e.tensor_scalar(
                xn_f[:], xs[:], self.stat[:, 5:6], self.stat[:, 4:5],
                OP.mult, OP.add), deps=[t_nmr, t_x1])
            war["xn_f"] = t_xn1
            war["xs_list"] = [t_xn1, t_xd]
            t_xb = p.op("ve", lambda e: e.tensor_copy(self.zbf[:], xn_f[:]),
                        deps=[t_xn1, t_sq])
            t_xbd = p.dma("qact", self.xn1_dram[i * 128:(i + 1) * 128, :],
                          self.zbf[:], deps=[t_xb])
            self.t_xn1w = getattr(self, "t_xn1w", [])
            self.t_xn1w.append(t_xbd)
            psev = self._psS_ev
            t_gmm = None
            for c in range(ND):
                sl_ = c % 2
                tt = p.op("te", lambda e, c=c, sl_=sl_: e.transpose(
                    self.psS[:, sl_ * 128:(sl_ + 1) * 128],
                    xn_f[:, c * 128:(c + 1) * 128],
                    self.identF[:]), deps=[t_xn1] +
                    ([psev[sl_]] if sl_ in psev else []))
                tev = p.op("ve", lambda e, c=c, sl_=sl_: e.tensor_copy(
                    g0f[:, c * 128:(c + 1) * 128],
                    self.psS[:, sl_ * 128:(sl_ + 1) * 128]),
                    deps=[tt, t_m1])
                psev[sl_] = tev
                gdeps = [tev, self.t_gwT]
                if c == 0 and t_glg is not None:
                    gdeps.append(t_glg)
                t_gmm = p.op("te", lambda e, c=c: e.matmul(
                    self.psb[4][0:128, 0:NE], g0f[:, c * 128:(c + 1) * 128],
                    self.gwT[:, c * NE:(c + 1) * NE],
                    start=(c == 0), stop=(c == ND - 1)), deps=gdeps)
            self._t_psS_last = tev
            self._psS_ev = psev
            war["g0f"] = tev
            t_lg1 = p.op("ve", lambda e: e.tensor_copy(
                a4[:], self.psb[4][0:128, 0:NE]), deps=[t_gmm, t_glg])
            t_glg = t_lg1
            self.t_gate_last = t_lg1
            t1 = p.op("ve", lambda e: e.reduce_max(mx[:], a4[:], axis=AX.X),
                      deps=[t_lg1])
            tneg = p.op("ve", lambda e: e.tensor_scalar(
                qcol[:], mx[:], -1.0, None, OP.mult), deps=[t1])
            t_ee = p.op("act", lambda e: e.activation(
                out=b4[:], in_=a4[:], func=AF.Exp, bias=qcol[:]), deps=[tneg])
            t_ss = p.op("ve", lambda e: e.reduce_sum(pcol[:], b4[:], axis=AX.X),
                        deps=[t_ee])
            t_rr = p.op("ve", lambda e: e.reciprocal(pcol[:], pcol[:]), deps=[t_ss])
            t_sc = p.op("ve", lambda e: e.tensor_scalar(
                b4[:], b4[:], pcol[:], None, OP.mult), deps=[t_rr])
            tm1_ = p.op("ve", lambda e: e.reduce_max(mx[:], b4[:], axis=AX.X),
                        deps=[t_sc])
            teq = p.op("ve", lambda e: e.tensor_scalar(
                c4[:], b4[:], mx[:], None, OP.is_equal), deps=[tm1_])
            trv = p.op("ve", lambda e: e.scalar_tensor_tensor(
                d4[:], self.iotaE[:], -1.0, c4[:], OP.mult, OP.mult),
                deps=[teq, self.t_iotaE])
            trv2 = p.op("ve", lambda e: e.scalar_tensor_tensor(
                d4[:], c4[:], float(NE), d4[:], OP.mult, OP.add), deps=[trv])
            tmx = p.op("ve", lambda e: e.reduce_max(icol[:], d4[:], axis=AX.X),
                       deps=[trv2])
            ti1 = p.op("ve", lambda e: e.tensor_scalar(
                icol[:], icol[:], -1.0, float(NE), OP.mult, OP.add), deps=[tmx])
            to1 = p.op("ve", lambda e: e.tensor_scalar(
                c4[:], self.iotaE[:], icol[:], None, OP.is_equal), deps=[ti1])
            tml = p.op("ve", lambda e: e.scalar_tensor_tensor(
                d4[:], c4[:], float(NEGBIG), b4[:], OP.mult, OP.add), deps=[to1])
            t2_ = p.op("ve", lambda e: e.reduce_max(mx2[:], d4[:], axis=AX.X),
                       deps=[tml])
            teq2 = p.op("ve", lambda e: e.tensor_scalar(
                d4[:], d4[:], mx2[:], None, OP.is_equal), deps=[t2_])
            trv3 = p.op("ve", lambda e: e.scalar_tensor_tensor(
                a4[:], self.iotaE[:], -1.0, d4[:], OP.mult, OP.mult), deps=[teq2])
            trv4 = p.op("ve", lambda e: e.scalar_tensor_tensor(
                a4[:], d4[:], float(NE), a4[:], OP.mult, OP.add), deps=[trv3])
            tmx2 = p.op("ve", lambda e: e.reduce_max(dcol[:], a4[:], axis=AX.X),
                        deps=[trv4])
            ti2 = p.op("ve", lambda e: e.tensor_scalar(
                dcol[:], dcol[:], -1.0, float(NE), OP.mult, OP.add), deps=[tmx2])
            to2 = p.op("ve", lambda e: e.tensor_scalar(
                d4[:], self.iotaE[:], dcol[:], None, OP.is_equal), deps=[ti2])
            tw1 = p.op("ve", lambda e: e.tensor_scalar(
                c4[:], c4[:], mx[:], None, OP.mult), deps=[to1, tm1_, tml])
            tw2 = p.op("ve", lambda e: e.tensor_scalar(
                d4[:], d4[:], mx2[:], None, OP.mult), deps=[to2])
            t_ws = p.op("ve", lambda e, i=i: e.tensor_tensor(
                self.wsel1[:, i * NE:(i + 1) * NE], c4[:], d4[:], OP.add),
                deps=[tw1, tw2])
            self.t_wsel = t_ws
            t_r1 = p.op("ve", lambda e: e.tensor_tensor(
                self.stat[:, 0:1], self.stat[:, 5:6], self.stat[:, 5:6],
                OP.mult), deps=[t_xn1])
            t_r2 = p.op("ve", lambda e: e.tensor_scalar(
                self.stat[:, 0:1], self.stat[:, 0:1], -eps, 1.0 + eps,
                OP.mult, OP.add), deps=[t_r1])
            t_sdb = p.op("act", lambda e: e.activation(
                out=self.stat[:, 0:1], in_=self.stat[:, 0:1], func=AF.Ln),
                deps=[t_r2])
            t_rstdb = p.op("act", lambda e: e.activation(
                out=self.stat[:, 0:1], in_=self.stat[:, 0:1], func=AF.Exp,
                scale=-0.5), deps=[t_sdb])
            t_z = p.op("ve", lambda e: e.tensor_scalar(
                self.zbf[:], xn_f[:], self.stat[:, 0:1], None,
                OP.mult), deps=[t_rstdb, t_xbd])
            war["zbf"] = t_z
            tpsT = getattr(self, "_t_psT_last", None)
            for c in range(ND):
                half = c % 2
                tt = p.op("te", lambda e, c=c, half=half: e.transpose(
                    self.psT[:, half * 128:(half + 1) * 128],
                    self.zbf[:, c * 128:(c + 1) * 128], self.identB[:]),
                    deps=[t_z, self.t_identB] + ([tpsT] if tpsT else []))
                tpsT = p.op("ve", lambda e, c=c, i=i, half=half: e.tensor_copy(
                    self.bufB[:, c * T + i * 128:c * T + (i + 1) * 128],
                    self.psT[:, half * 128:(half + 1) * 128]), deps=[tt])
                self.t_zT.append(tpsT)
            self._t_psT_last = tpsT
            war["zbf"] = tpsT
        self.t_x1 = x1_done
        if self.want("x1"):
            xo = self.eout("dbg_x1", [T, D], F32)
            p.dma("qpool", xo[:], self.x1_dram[:], deps=x1_done)
        if self.want("zT"):
            zo = self.eout("dbg_zT", [128, ND * T], BF16)
            tzd = p.dma("qact", zo[:], self.bufB[:, 0:self.dm.ND * self.dm.T],
                        deps=self.t_zT)
            self.t_zT.append(tzd)
        if self.want("wsel1"):
            wo = self.eout("dbg_wsel1", [128, NT * NE], F32)
            p.dma("qact", wo[:], self.wsel1[:], deps=[self.t_wsel])

    def phase_attn_l1(self):
        p, dm = self.p, self.dm
        T, D, H, ND, NT, W = dm.T, dm.D, dm.H, dm.ND, dm.NT, dm.W
        NW = dm.NW
        QW = self.QW
        t_bb = p.op("ve", lambda e: e.tensor_copy(self.bcols_b[:], self.bcols[:]),
                    deps=self.t_bcols)
        t_row = None
        brow_all = []
        CW = min(512, D)
        for ch in range(3 * D // CW):
            mm = None
            for c in range(ND):
                t_wl = p.dma("qsp", self.wrhs[0][:, 0:CW],
                             self.wqkv1bf[c * 128:(c + 1) * 128,
                                          ch * CW:(ch + 1) * CW],
                             deps=[self.t_wq1c] + ([mm] if mm else
                                                   ([t_row] if t_row else [])))
                mm = p.op("te", lambda e, c=c, ch=ch, CW=CW: e.matmul(
                    self.psb[4][0:1, 0:CW], self.bcols_b[:, c:c + 1],
                    self.wrhs[0][:, 0:CW], start=(c == 0), stop=(c == ND - 1)),
                    deps=[t_wl, t_bb] + ([t_row] if t_row and c == 0 else []))
            t_rowc = p.op("ve", lambda e, CW=CW: e.tensor_copy(
                self.phat[0][0:1, 0:CW], self.psb[4][0:1, 0:CW]),
                deps=[mm] + ([t_row] if t_row else []))
            t_row = p.dma("qact", self.brow_dram[0:1, ch * CW:(ch + 1) * CW],
                          self.phat[0][0:1, 0:CW], deps=[t_rowc])
            brow_all.append(t_row)
        t_brow = t_row
        self.t_OT1 = []
        self._l_war1 = self.t_s0[-1]
        exp_prev = {}
        last_wq = {}
        last_qkT = None
        t_pe = None
        pvtok = None
        wqkv1_r = self.wqkv1bf.ap().rearrange("(c p) n -> p c n", p=128)
        for h in range(H):
            bsel = h % 2
            wq = self.wq_sb[bsel]
            wdep = [last_wq['w']] if 'w' in last_wq else []
            tws = []
            for g, off in ((0, h * 64), (1, D + h * 64), (2, 2 * D + h * 64)):
                tws.append(p.dma("qsp",
                                 wq.ap().rearrange("p (c x) -> p c x", x=192)[:, :, g * 64:(g + 1) * 64],
                                 wqkv1_r[:, :, off:off + 64],
                                 deps=wdep if g == 0 else []))
            t_bhs = []
            for g, off in ((0, h * 64), (1, D + h * 64), (2, 2 * D + h * 64)):
                t_bhs.append(p.dma("qsp", self.brow_h[0:1, g * 64:(g + 1) * 64],
                             self.brow_dram[0:1, off:off + 64],
                             deps=brow_all + wdep if g == 0 else []))
            t_bh = t_bhs
            t_gs = None
            for c in range(ND):
                t_gs = p.op("ve", lambda e, c=c, wq=wq: e.tensor_scalar(
                    wq[:, c * 192:(c + 1) * 192], wq[:, c * 192:(c + 1) * 192],
                    self.gcols[:, c:c + 1], None, OP.mult),
                    deps=[tws[0], tws[1], tws[2]] + self.t_gcols)
            qk_war = [last_qkT] if last_qkT else []
            TCW = min(512, T)
            qkvev = {0: (self.t_gate_last if h == 0
                         else getattr(self, "_t_bce_prev", None)), 1: t_pe}
            gi = 0
            for tc in range(T // TCW):
                bk = 4 + gi % 2
                mm = None
                wdeps = [t_gs] + t_bh + qk_war + self.t_zT[-1:]
                if qkvev[gi % 2] is not None:
                    wdeps.append(qkvev[gi % 2])
                for c in range(ND):
                    mm = p.op("te", lambda e, tc=tc, c=c, wq=wq,
                              TCW=TCW, bk=bk: e.matmul(
                        self.psb[bk][0:128, 0:TCW],
                        wq[:, c * 192:c * 192 + 128],
                        self.bufB[:, c * T + tc * TCW:c * T + (tc + 1) * TCW],
                        start=(c == 0), stop=False),
                        deps=(wdeps if c == 0 else []))
                mm = p.op("te", lambda e, tc=tc, TCW=TCW, bk=bk: e.matmul(
                    self.psb[bk][0:128, 0:TCW],
                    self.brow_h[0:1, 0:128],
                    self.ones_row_b[0:1, 0:TCW], start=False, stop=True),
                    deps=[self.t_orow] + t_bh)
                ev1 = p.op("ve", lambda e, tc=tc, TCW=TCW, bk=bk: e.tensor_copy(
                    self.qT[0:64, tc * TCW:(tc + 1) * TCW],
                    self.psb[bk][0:64, 0:TCW]), deps=[mm])
                qkvev[gi % 2] = p.op("ve", lambda e, tc=tc, TCW=TCW,
                                     bk=bk: e.tensor_copy(
                    self.kT[0:64, tc * TCW:(tc + 1) * TCW],
                    self.psb[bk][64:128, 0:TCW]), deps=[ev1])
                gi += 1
            for tt in range(NT):
                bk = 4 + gi % 2
                mm = None
                wdeps = [t_gs]
                if qkvev[gi % 2] is not None:
                    wdeps.append(qkvev[gi % 2])
                for c in range(ND):
                    mm = p.op("te", lambda e, tt=tt, c=c, wq=wq, bk=bk: e.matmul(
                        self.psb[bk][0:128, 0:64],
                        self.bufB[:, c * T + tt * 128:c * T + (tt + 1) * 128],
                        wq[:, c * 192 + 128:c * 192 + 192],
                        start=(c == 0), stop=False),
                        deps=(wdeps if c == 0 else []))
                mm = p.op("te", lambda e, tt=tt, bk=bk: e.matmul(
                    self.psb[bk][0:128, 0:64], self.onesK1_b[:],
                    self.brow_h[0:1, 128:192],
                    start=False, stop=True), deps=[self.t_onesK1b] + t_bh)
                qkvev[gi % 2] = p.op("ve", lambda e, tt=tt, bk=bk: e.tensor_copy(
                    self.v_h[:, tt * 65:tt * 65 + 64], self.psb[bk][0:128, 0:64]),
                    deps=[mm, self.t_vones])
                gi += 1
            t_qkv = p.op("ve", lambda e: e.engine_nop(),
                         deps=[qkvev[0], qkvev[1]])
            last_wq['w'] = t_qkv
            if h == 0 and self.want("qkT1"):            t_qkv = t_pe
            last_wq['w'] = t_qkv
            if h == 0 and self.want("qkT1"):
                qo = self.eout("dbg_q1", [64, T], BF16)
                ko = self.eout("dbg_k1", [64, T], BF16)
                vo = self.eout("dbg_v1", [128, NT * 65], BF16)
                p.dma("qact", qo[:], self.qT[0:64, 0:T], deps=[t_qkv])
                p.dma("qact", ko[:], self.kT[0:64, 0:T], deps=[t_qkv])
                t_qkv = p.dma("qact", vo[:], self.v_h[:], deps=[t_qkv])
                last_wq['w'] = t_qkv
            phat_war = getattr(self, "_phat_war1", {})
            WPC = QW // dm.W
            for w in range(NW):
                sb_ = w % 2
                wid = 256 if w > 0 else 128
                k0 = (w - 1) * 128 if w > 0 else 0
                sdeps = [t_qkv]
                if sb_ in exp_prev:
                    sdeps.append(exp_prev[sb_])
                mmS = None
                for kk in range(wid // 128):
                    mmS = p.op("te", lambda e, w=w, k0=k0, kk=kk, sb_=sb_: e.matmul(
                        self.psb[sb_][0:128, kk * 128:(kk + 1) * 128],
                        self.kT[0:64, k0 + kk * 128:k0 + (kk + 1) * 128],
                        self.qT[0:64, w * 128:(w + 1) * 128],
                        start=True, stop=True),
                        deps=(sdeps if kk == 0 else []))
                moff = 0 if w > 0 else 128
                t_b = p.op("ve", lambda e, sb_=sb_, wid=wid, moff=moff: e.tensor_tensor(
                    self.psb[sb_][0:128, 0:wid], self.psb[sb_][0:128, 0:wid],
                    self.lmask[:, moff:moff + wid], OP.add),
                    deps=[mmS, self.t_lmask])
                edeps = [t_b, self.t_m16]
                if sb_ in phat_war:
                    edeps.append(phat_war[sb_])
                if pvtok is not None:
                    edeps.append(pvtok)
                t_e = p.op("act", lambda e, sb_=sb_, wid=wid: e.activation(
                    out=self.phat[sb_][:, 0:wid], in_=self.psb[sb_][0:128, 0:wid],
                    func=AF.Exp, bias=self.m16[:], scale=0.125), deps=edeps)
                exp_prev[sb_] = t_e
                pv_mm = None
                l_war = getattr(self, "_l_war1", None)
                for kk in range(wid // 128):
                    jj = (k0 // 128) + kk
                    pv_mm = p.op("te", lambda e, jj=jj, kk=kk, sb_=sb_,
                                 wid=wid: e.matmul(
                        self.psb[2][0:65, 0:128],
                        self.v_h[:, jj * 65:(jj + 1) * 65],
                        self.phat[sb_][:, kk * 128:(kk + 1) * 128],
                        start=(kk == 0), stop=(kk == wid // 128 - 1)),
                        deps=[t_e] + ([l_war] if l_war and kk == 0 else []))
                phat_war[sb_] = pv_mm
                t_le = p.op("ve", lambda e, w=w, WPC=WPC: e.tensor_copy(
                    self.l65[0:1, (w % WPC) * 128:(w % WPC + 1) * 128],
                    self.psb[2][64:65, 0:128]),
                    deps=[pv_mm] + ([self._t_lbc_prev]
                                    if getattr(self, "_t_lbc_prev", None) else []))
                self._l_war1 = t_le
                t_oe = p.op("ve", lambda e, h=h, w=w: e.tensor_copy(
                    self.xnT_bf[(h % 2) * 64:(h % 2) * 64 + 64,
                                (h // 2) * T + w * 128:(h // 2) * T + (w + 1) * 128],
                    self.psb[2][0:64, 0:128]), deps=[pv_mm, t_le])
                pvtok = t_oe
                if (w + 1) % WPC == 0:
                    qc = w // WPC
                    t_lr = p.op("act", lambda e: e.activation(
                        out=self.l65[0:1, 0:QW], in_=self.l65[0:1, 0:QW],
                        func=AF.Ln), deps=[pvtok])
                    t_lr = p.op("act", lambda e: e.activation(
                        out=self.l65[0:1, 0:QW], in_=self.l65[0:1, 0:QW],
                        func=AF.Exp, scale=-1.0), deps=[t_lr])
                    self._l_war1 = t_lr
                    bc_deps = [t_lr, self.t_o65]
                    if getattr(self, "_t_bce_prev", None):
                        bc_deps.append(self._t_bce_prev)
                    t_bc = p.op("te", lambda e: e.matmul(
                        self.psb[4][0:128, 0:QW], self.ones65[0:1, :],
                        self.l65[0:1, 0:QW], start=True, stop=True),
                        deps=bc_deps)
                    self._t_lbc_prev = t_bc
                    t_bce = p.op("ve", lambda e: e.tensor_copy(
                        self.bcst[:, 0:QW], self.psb[4][0:128, 0:QW]),
                        deps=[t_bc])
                    self._t_bce_prev = t_bce
                    hb = (h % 2) * 64
                    t_no = p.op("ve", lambda e, h=h, qc=qc, hb=hb: e.tensor_tensor(
                        self.xnT_bf[hb:hb + 64,
                                    (h // 2) * T + qc * QW:(h // 2) * T + (qc + 1) * QW],
                        self.xnT_bf[hb:hb + 64,
                                    (h // 2) * T + qc * QW:(h // 2) * T + (qc + 1) * QW],
                        self.bcst[hb:hb + 64, 0:QW], OP.mult),
                        deps=[t_bce, pvtok])
                    pvtok = t_no
                    self.t_OT1.append(t_no)
            self._phat_war1 = phat_war
            last_qkT = pvtok
        if self.want("OT1"):            last_qkT = pvtok
        if self.want("OT1"):
            ot = self.eout("dbg_OT1", [128, ND * T], BF16)
            p.dma("qact", ot[:], self.xnT_bf[:], deps=self.t_OT1)

    def phase_wout_l1(self):
        p, dm = self.p, self.dm
        T, D, H, ND, NT = dm.T, dm.D, dm.H, dm.ND, dm.NT
        NS = H * 64 // 128
        DCW = min(512, D)
        NDC = D // DCW
        wout1bf_r = self.wout1bf.ap().rearrange("(s p) n -> p s n", p=128)
        wl = {}
        ev_prev = {}
        self.t_s1 = []
        for dc in range(NDC):
            bsel = 0
            wdep = [self.t_wo1c] + ([wl["w"]] if "w" in wl else [])
            t_wo = p.dma("qsp", self.wrhs[bsel][:, 0:NS * DCW],
                         wout1bf_r[:, :, dc * DCW:(dc + 1) * DCW], deps=wdep)
            for i in range(NT):
                pb = 2 + (i % 2)
                mm = None
                mdeps = self.t_OT1[-1:] + [t_wo]
                if ("y", pb) in ev_prev:
                    mdeps.append(ev_prev[("y", pb)])
                for s in range(NS):
                    mm = p.op("te", lambda e, s=s, i=i, pb=pb, bsel=bsel: e.matmul(
                        self.psb[pb][0:128, 0:DCW],
                        self.xnT_bf[:, s * T + i * 128:s * T + (i + 1) * 128],
                        self.wrhs[bsel][:, s * DCW:(s + 1) * DCW],
                        start=(s == 0), stop=(s == NS - 1)),
                        deps=(mdeps if s == 0 else []))
                t_ev = p.op("ve", lambda e, i=i, dc=dc, pb=pb: e.tensor_copy(
                    self.bufB[:, i * D + dc * DCW:i * D + (dc + 1) * DCW],
                    self.psb[pb][0:128, 0:DCW]), deps=[mm] + self.t_zT[-1:])
                ev_prev[("y", pb)] = t_ev
                self.t_s1.append(t_ev)
            wl["w"] = t_ev
        if self.want("s1"):
            so = self.eout("dbg_s1", [128, NT * D], BF16)
            p.dma("qact", so[:], self.bufB[:, 0:self.dm.ND * self.dm.T], deps=self.t_s1)

    def phase_final(self):
        p, dm = self.p, self.dm
        T, D, NE, NT, ND, R = dm.T, dm.D, dm.NE, dm.NT, dm.ND, dm.R
        DCW = min(512, D)
        NDC = D // DCW
        xs, xn_f = self.xs[0], self.xn_f[0]
        g0f = self.xnT_st
        war = {}
        for dc in range(NDC):
            t_blin = p.dma("qpool", self.blin_sb[0:1, 0:DCW],
                           self.blin_in[0:1, dc * DCW:(dc + 1) * DCW],
                           deps=war.get("blin", []))
            t_right = p.dma("qpool", self.right_sb[0:32, 0:DCW],
                            self.right_in[0:32, dc * DCW:(dc + 1) * DCW],
                            deps=war.get("right", []))
            t_wl = []
            tw = None
            for c in range(ND):
                tw = p.dma("qsp", self.wrhs[0][:, c * DCW:(c + 1) * DCW],
                           self.wlinbf[dc * DCW:(dc + 1) * DCW,
                                       c * 128:(c + 1) * 128],
                           deps=([self.t_wlc] + self.t_s1[-1:] +
                                 war.get("wrhs", [])) if c == 0 else [],
                           transpose=True)
                t_wl.append(tw)
            for i in range(NT):
                t_x = p.dma("qsp", xs[:], self.x1_dram[i * 128:(i + 1) * 128, :],
                            deps=war.get("xs", []) + self.t_x1)
                t_z = p.dma("qsp", self.zbf[:],
                            self.xn1_dram[i * 128:(i + 1) * 128, :],
                            deps=war.get("zbf", []) + self.t_xn1w)
                tpsT = getattr(self, "_t_psT_last", None)
                for c in range(ND):
                    half = c % 2
                    tt = p.op("te", lambda e, c=c, half=half: e.transpose(
                        self.psT[:, half * 128:(half + 1) * 128],
                        self.zbf[:, c * 128:(c + 1) * 128], self.identB[:]),
                        deps=[t_z, self.t_identB] + ([tpsT] if tpsT else []))
                    tpsT = p.op("ve", lambda e, c=c, half=half: e.tensor_copy(
                        self.xt1[:, c * 128:(c + 1) * 128],
                        self.psT[:, half * 128:(half + 1) * 128]),
                        deps=[tt] + war.get("xt1", self.t_zT[-1:]))
                self._t_psT_last = tpsT
                war["zbf"] = [tpsT]
                t_xt = tpsT
                tps = list(self._psS_ev.values())
                tt = p.op("te", lambda e, i=i: e.transpose(
                    self.psS[0:8, 0:128], self.wsel1[:, i * NE:(i + 1) * NE],
                    self.identF[:]), deps=[self.t_wsel] + tps)
                t_wsT = p.op("ve", lambda e: e.tensor_copy(
                    self.wselT[:], self.psS[0:8, 0:128]),
                    deps=[tt] + war.get("wselT", []))
                self._psS_ev = {0: t_wsT}
                t_we_mm = p.op("te", lambda e: e.matmul(
                    self.psb[4][0:32, 0:128], self.E832[:], self.wselT[:],
                    start=True, stop=True),
                    deps=[t_wsT, self.t_E832] + war.get("psb4", []))
                t_we = p.op("ve", lambda e: e.tensor_copy(
                    self.wexpT[:], self.psb[4][0:32, 0:128]),
                    deps=[t_we_mm] + war.get("wexpT", []))
                war["psb4"] = [t_we]
                mmt = None
                for c in range(ND):
                    mmt = p.op("te", lambda e, c=c: e.matmul(
                        self.psb[5][0:32, 0:128],
                        self.left_sb[:, c * 32:(c + 1) * 32],
                        self.xt1[:, c * 128:(c + 1) * 128],
                        start=(c == 0), stop=(c == ND - 1)),
                        deps=([t_xt, self.t_left] + war.get("psb5", [])
                              if c == 0 else []))
                t_t2 = p.op("ve", lambda e: e.tensor_tensor(
                    self.t2_sb[:], self.psb[5][0:32, 0:128], self.wexpT[:],
                    OP.mult), deps=[mmt, t_we] + war.get("t2", []))
                war["psb5"] = [t_t2]
                war["wexpT"] = [t_t2]
                war["wselT"] = [t_t2]
                pb = 2 + (i % 2)
                mm = None
                bdeps = t_wl + [t_xt] + war.get(("o", pb), [])
                for c in range(ND):
                    mm = p.op("te", lambda e, c=c, pb=pb, DCW=DCW: e.matmul(
                        self.psb[pb][0:128, 0:DCW],
                        self.xt1[:, c * 128:(c + 1) * 128],
                        self.wrhs[0][:, c * DCW:(c + 1) * DCW],
                        start=(c == 0), stop=False),
                        deps=(bdeps if c == 0 else []))
                mm = p.op("te", lambda e, pb=pb, DCW=DCW: e.matmul(
                    self.psb[pb][0:128, 0:DCW], self.onesK1_b[:],
                    self.blin_sb[0:1, 0:DCW],
                    start=False, stop=False), deps=[t_blin, self.t_onesK1b])
                mm = p.op("te", lambda e, pb=pb, DCW=DCW: e.matmul(
                    self.psb[pb][0:128, 0:DCW], self.t2_sb[:],
                    self.right_sb[0:32, 0:DCW],
                    start=False, stop=True), deps=[t_t2, t_right])
                war["blin"] = [mm]
                war["right"] = [mm]
                war["t2"] = [mm]
                t_g1 = p.op("act", lambda e, pb=pb, DCW=DCW: e.activation(
                    out=g0f[:, 0:DCW],
                    in_=self.psb[pb][0:128, 0:DCW], func=AF.Silu),
                    deps=[mm] + war.get("g0f", []))
                t_h1 = p.op("ve", lambda e, i=i, dc=dc, DCW=DCW: e.tensor_tensor(
                    g0f[:, 0:DCW],
                    self.bufB[:, i * D + dc * DCW:i * D + (dc + 1) * DCW],
                    g0f[:, 0:DCW], OP.mult),
                    deps=[t_g1] + self.t_s1[-1:])
                war[("o", pb)] = [t_h1]
                t_o = p.op("ve", lambda e, dc=dc, DCW=DCW: e.tensor_tensor(
                    xs[:, dc * DCW:(dc + 1) * DCW],
                    xs[:, dc * DCW:(dc + 1) * DCW],
                    g0f[:, 0:DCW], OP.add), deps=[t_h1, t_x])
                war["g0f"] = [t_o]
                t_od = p.dma("qact",
                             self.out[i * 128:(i + 1) * 128,
                                      dc * DCW:(dc + 1) * DCW],
                             xs[:, dc * DCW:(dc + 1) * DCW], deps=[t_o])
                war["xs"] = [t_od, t_o]
            war["wrhs"] = [mm]



def make_inputs(dm, fi):
    f32 = lambda a: np.ascontiguousarray(a, dtype=np.float32)
    x = f32(fi["L0_attn_x"])
    shared = {
        "rw": f32(fi["L0_ffn_router_w"]),
        "rb": f32(fi["L0_ffn_router_b"]).reshape(1, -1),
        "wqkv": f32(fi["L0_attn_W_qkv"]),
        "wout": f32(fi["L0_attn_W_out"]),
        "l1g": f32(fi["L1_attn_ln_gamma"]).reshape(1, -1),
        "l1b": f32(fi["L1_attn_ln_beta"]).reshape(1, -1),
        "wqkv1": f32(fi["L1_attn_w_qkv"]),
        "wout1": f32(fi["L1_attn_w_out"]),
        "gw": f32(fi["L1_ffn_gate_w"]),
        "wlin": f32(fi["L1_ffn_W_lin"]),
        "blin": f32(fi["L1_ffn_b_lin"]).reshape(1, -1),
        "left_all": f32(np.transpose(fi["L1_ffn_left"], (1, 0, 2)).reshape(dm.D, -1)),
        "right_all": f32(np.asarray(fi["L1_ffn_right"]).reshape(dm.NE * dm.R, dm.D)),
    }
    W = f32(fi["L0_ffn_W"])
    V = f32(fi["L0_ffn_V"])
    WO = f32(fi["L0_ffn_W_out"])
    in_maps = []
    for c in range(8):
        esel = np.zeros((128, 8), np.float32)
        for r in range(8):
            esel[16 * r + c, r] = 1.0
        rmask = np.zeros((8, 1), np.float32)
        rmask[:c] = 1.0
        kq = dm.T // dm.Q
        pp = np.arange(128)
        slmidx = ((16 * (pp // 16) + c) * kq + pp % kq).astype(np.float32)
        m = dict(shared)
        m.update({"x": x[c], "rank_mask": rmask, "esel": esel,
                  "slmidx": slmidx.reshape(128, 1),
                  "mw": W[c], "mv": V[c], "mwo": WO[c]})
        in_maps.append(m)
    return in_maps


def kernel(**inputs):
    dm = Dims()
    b = Builder(dm)
    nc = b.build()
    in_maps = make_inputs(dm, inputs)
    res = run_bass_kernel_spmd(nc, in_maps, core_ids=list(range(8)))
    return np.stack([r["out"] for r in res.results], axis=0)


# version 19
# speedup vs baseline: 1.1925x; 1.1925x over previous
import math
from contextlib import ExitStack

import numpy as np
import concourse.bass as bass
import concourse.mybir as mybir
from concourse.bass_utils import run_bass_kernel_spmd

F32 = mybir.dt.float32
F16 = mybir.dt.float16
BF16 = mybir.dt.bfloat16
I32 = mybir.dt.int32
AF = mybir.ActivationFunctionType
OP = mybir.AluOpType
AX = mybir.AxisListType

ALIBI_SLOPES = [0.7071067811865476 ** (i + 1) for i in range(16)]
C_STAB = 16.0
NEGBIG = -32768.0


class Dims:
    def __init__(self, T=4096, D=1024, H=16, NE=8, CAP=2048, DH=2048, R=4, W=128):
        self.T, self.D, self.H, self.NE, self.CAP, self.DH, self.R, self.W = \
            T, D, H, NE, CAP, DH, R, W
        self.HD = 64
        self.NC = 8
        self.NT = T // 128
        self.ND = D // 128
        self.TG = T * self.NC
        self.Q = self.TG // 128
        self.NW = T // W
        self.slopes = [ALIBI_SLOPES[h * (16 // H)] for h in range(H)]


ENGS = ("te", "ve", "act", "pool", "sp")
QOF = {"qsp": "sp", "qact": "act", "qpool": "pool"}


class Prog:

    NLANES = 6
    QENG = {"qsp": "sp", "qact": "act", "qpool": "pool", "cc": "pool"}

    def __init__(self, nc):
        self.nc = nc
        self.ops = {e: [] for e in ENGS}
        self.count = {s: 0 for s in ("te", "ve", "act", "pool")}
        self.sems = {}
        self.lane = {}
        self.fcount = {}
        self.fcover = {}
        self.rr = {}
        for q in ("qsp", "qact", "qpool", "cc"):
            n = 1 if q == "cc" else self.NLANES
            for l in range(n):
                self.lane[(q, l)] = 0
                self.fcover[(q, l)] = 0
            self.fcount[q] = 0
            self.rr[q] = 0
        self.waited = {e: {} for e in ENGS}

    def make_sems(self, ctx):
        for s in ("te", "ve", "act", "pool"):
            self.sems[s] = ctx.enter_context(self.nc.semaphore("s_" + s))
        for (q, l) in self.lane:
            self.sems[(q, l)] = ctx.enter_context(self.nc.semaphore(f"s_{q}{l}"))
        for q in self.fcount:
            self.sems["f" + q] = ctx.enter_context(self.nc.semaphore("sf_" + q))

    def _wait(self, eng, tok):
        if tok is None:
            return
        sem, val = tok
        if isinstance(sem, tuple):
            q, l = sem
            if self.fcover[(q, l)] < val:
                qeng = self.QENG[q]
                total = self.lane[(q, l)]
                self.ops[qeng].append(("wait", self.sems[(q, l)], total))
                self.waited[qeng][(q, l)] = max(
                    self.waited[qeng].get((q, l), 0), total)
                fs = "f" + q
                self.fcount[fs[1:]] += 1
                self.ops[qeng].append(("inc", self.sems[fs], 1))
                self.fcover[(q, l)] = total
            sem2, val2 = "f" + q, self.fcount[q]
            if self.waited[eng].get(sem2, 0) >= val2:
                return
            self.waited[eng][sem2] = val2
            self.ops[eng].append(("wait", self.sems[sem2], val2))
            return
        if val <= 0 or self.waited[eng].get(sem, 0) >= val:
            return
        self.waited[eng][sem] = val
        self.ops[eng].append(("wait", self.sems[sem], val))

    def fence_queue(self, q):
        qeng = self.QENG[q]
        made = False
        for l in range(1 if q == "cc" else self.NLANES):
            total = self.lane[(q, l)]
            if self.fcover[(q, l)] < total:
                self.ops[qeng].append(("wait", self.sems[(q, l)], total))
                self.waited[qeng][(q, l)] = max(
                    self.waited[qeng].get((q, l), 0), total)
                self.fcover[(q, l)] = total
                made = True
        if made:
            self.fcount[q] += 1
            self.ops[qeng].append(("inc", self.sems["f" + q], 1))

    def op(self, eng, fn, deps=()):
        for d in deps:
            self._wait(eng, d)
        self.count[eng] += 1
        self.ops[eng].append(("op", fn, self.sems[eng]))
        return (eng, self.count[eng])

    def _lane_pick(self, q):
        l = self.rr[q] % (1 if q == "cc" else self.NLANES)
        self.rr[q] += 1
        return l

    def dma(self, queue, out, in_, deps=(), **kw):
        eng = self.QENG[queue]
        for d in deps:
            self._wait(eng, d)
        l = self._lane_pick(queue)
        self.lane[(queue, l)] += 16
        self.ops[eng].append(("dma", (out, in_, kw), self.sems[(queue, l)]))
        return ((queue, l), self.lane[(queue, l)])

    def indirect(self, out, out_off, in_, in_off, bounds, deps=(), compute_op=None):
        for d in deps:
            self._wait("pool", d)
        l = self._lane_pick("qpool")
        self.lane[("qpool", l)] += 16
        kw = {"compute_op": compute_op} if compute_op is not None else {}
        self.ops["pool"].append(
            ("ind", (out, out_off, in_, in_off, bounds, kw),
             self.sems[("qpool", l)]))
        return (("qpool", l), self.lane[("qpool", l)])

    def collective(self, kind, op, ins, outs, deps=()):
        for d in deps:
            self._wait("pool", d)
        self.lane[("cc", 0)] += 1
        self.ops["pool"].append(("cc", (kind, op, ins, outs),
                                 self.sems[("cc", 0)]))
        return (("cc", 0), self.lane[("cc", 0)])

    def finish(self, block):
        def ap(x):
            return x.ap() if not isinstance(x, bass.AP) and hasattr(x, "ap") else x

        def run(engine, stream):
            for n, item in enumerate(stream):
                kind = item[0]
                self._replay_pos = (engine, n, kind)
                if kind == "wait":
                    engine.wait_ge(item[1], item[2])
                elif kind == "inc":
                    engine.sem_inc(item[1], item[2])
                elif kind == "op":
                    item[1](engine).then_inc(item[2], 1)
                elif kind == "dma":
                    (out, in_, kw), sem = item[1], item[2]
                    engine.dma_start(out=ap(out), in_=ap(in_), **kw).then_inc(sem, 16)
                elif kind == "ind":
                    (out, oo, in_, io_, b, kw), sem = item[1], item[2]
                    if b not in self._breg:
                        r = engine.alloc_register(f"bnd{len(self._breg)}")
                        engine.reg_mov(r, b)
                        self._breg[b] = r
                    engine.indirect_dma_start(
                        out=ap(out), out_offset=oo, in_=ap(in_), in_offset=io_,
                        bounds_check=self._breg[b],
                        oob_is_err=False, **kw).then_inc(sem, 16)
                elif kind == "cc":
                    (ck, cop, ins, outs), sem = item[1], item[2]
                    engine.collective_compute(
                        ck, cop, replica_groups=[list(range(8))],
                        ins=ins, outs=outs).then_inc(sem, 1)

        @block.tensor
        def _(tensor):
            run(tensor, self.ops["te"])

        @block.vector
        def _(vector):
            run(vector, self.ops["ve"])

        @block.scalar
        def _(scalar):
            run(scalar, self.ops["act"])
            for l in range(self.NLANES):
                if self.lane[("qact", l)]:
                    scalar.wait_ge(self.sems[("qact", l)], self.lane[("qact", l)])

        @block.gpsimd
        def _(gpsimd):
            self._breg = {}
            run(gpsimd, self.ops["pool"])
            for l in range(self.NLANES):
                if self.lane[("qpool", l)]:
                    gpsimd.wait_ge(self.sems[("qpool", l)], self.lane[("qpool", l)])
            if self.lane[("cc", 0)]:
                gpsimd.wait_ge(self.sems[("cc", 0)], self.lane[("cc", 0)])

        @block.sync
        def _(sync):
            run(sync, self.ops["sp"])
            for l in range(self.NLANES):
                if self.lane[("qsp", l)]:
                    sync.wait_ge(self.sems[("qsp", l)], self.lane[("qsp", l)])


class Builder:
    def __init__(self, dm: Dims, debug=()):
        self.dm = dm
        self.debug = set(debug)
        self.nc = bass.Bass("TRN2", target_bir_lowering=False)
        self.ctx = ExitStack()
        self.p = Prog(self.nc)
        self.io = {}
        self.dbg_names = []

    def ein(self, name, shape, dt=F32):
        t = self.nc.dram_tensor(name, shape, dt, kind="ExternalInput")
        self.io[name] = t
        return t

    def eout(self, name, shape, dt=F32):
        t = self.nc.dram_tensor(name, shape, dt, kind="ExternalOutput")
        self.io[name] = t
        return t

    def idram(self, name, shape, dt):
        t = self.nc.dram_tensor(name, shape, dt)
        self.io[name] = t
        return t

    def sb(self, name, shape, dt):
        return self.ctx.enter_context(self.nc.sbuf_tensor(name, shape, dt))

    def ps(self, name, shape, dt=F32):
        return self.ctx.enter_context(self.nc.psum_tensor(name, shape, dt))

    def want(self, name):
        return name in self.debug

    def dump(self, name, shape, dt, src_fn, deps):
        if not self.want(name):
            return None
        ap = self.eout("dbg_" + name, shape, dt)
        self.dbg_names.append("dbg_" + name)
        return self.p.dma("qact", ap, src_fn(), deps=deps)

    def build(self, upto="all"):
        dm = self.dm
        p = self.p
        with self.ctx:
            p.make_sems(self.ctx)
            self.declare_io()
            self.alloc_consts()
            NSQ = dm.CAP // min(512, dm.CAP)
            H = dm.H
            self.PARTB_AT = min(3, H - 2)
            self.PARTC_AT = min(5, H - 1)
            self.SCAT_START = self.PARTB_AT + 1
            self.MOE_AT = {min(7, H - 1): list(range(NSQ))}
            self.AG_AT = min(9, H - 1)
            with self.nc.Block() as block:
                self.emit_consts()
                self.phase_ln_router()
                self.phase_routing()
                if upto in ("p1", "p2"):
                    self.emit_routing_b()
                    self.emit_routing_c()
                if upto != "p1":
                    self.phase_moe_scatter()
                if upto == "p2":
                    self.emit_scatter_groups(10 ** 6)
                    ep = {}
                    pv = None
                    for k in range(NSQ):
                        ht, yt = self.emit_moe_chunk(k, ep, pv)
                        ep = {0: ht, 1: ht}
                        pv = yt
                    self.emit_moe_ag()
                if upto not in ("p1", "p2"):
                    self.phase_attn_l0()
                    self.phase_wout_l0()
                    self.phase_combine_residual()
                    self.phase_attn_l1()
                    self.phase_wout_l1()
                    self.phase_final()
                self.p.finish(block)
        return self.nc

    def declare_io(self):
        dm = self.dm
        T, D, NE, TG = dm.T, dm.D, dm.NE, dm.TG
        self.x_in = self.ein("x", [T, D])
        self.rw_in = self.ein("rw", [NE, D])
        self.rb_in = self.ein("rb", [1, NE])
        self.rmask_in = self.ein("rank_mask", [8, 1])
        self.esel_in = self.ein("esel", [128, 8])
        self.slmidx_in = self.ein("slmidx", [128, 1])
        self.mw_in = self.ein("mw", [D, dm.DH])
        self.mv_in = self.ein("mv", [D, dm.DH])
        self.mwo_in = self.ein("mwo", [dm.DH, D])
        self.wqkv_in = self.ein("wqkv", [D, 3 * D])
        self.wout_in = self.ein("wout", [D, D])
        self.l1g_in = self.ein("l1g", [1, D])
        self.l1b_in = self.ein("l1b", [1, D])
        self.wqkv1_in = self.ein("wqkv1", [D, 3 * D])
        self.wout1_in = self.ein("wout1", [D, D])
        self.gw_in = self.ein("gw", [NE, D])
        self.wlin_in = self.ein("wlin", [D, D])
        self.blin_in = self.ein("blin", [1, D])
        self.left_in = self.ein("left_all", [D, NE * dm.R])
        self.right_in = self.ein("right_all", [NE * dm.R, D])
        self.out = self.eout("out", [T, D])
        self.cc_xn_in = self.idram("cc_xn_in", [T, D], BF16)
        self.cc_xn_out = self.idram("cc_xn_out", [TG, D], BF16)
        self.cc_tot_in = self.idram("cc_tot_in", [1, 16], F32)
        self.cc_tot_out = self.idram("cc_tot_out", [8, 16], F32)
        self.cc_sl_in = self.idram("cc_sl_in", [16, T], F32)
        self.cc_sl_out = self.idram("cc_sl_out", [128, T], F32)
        self.ind_dram = self.idram("ind_dram", [T, 128], BF16)
        self.slm_dram = self.idram("slm_dram", [1, TG], F32)
        self.offs_dram = self.idram("offs_dram", [1, 16], F32)
        self.woutbf = self.idram("woutbf", [D, D], BF16)
        self.wout1bf = self.idram("wout1bf", [D, D], BF16)
        self.wqkv1bf = self.idram("wqkv1bf", [D, 3 * D], BF16)
        self.wlinbf = self.idram("wlinbf", [D, D], BF16)
        self.brow_dram = self.idram("brow_dram", [1, 3 * D], BF16)
        self.x1_dram = self.idram("x1_dram", [T, D], F32)
        self.xn1_dram = self.idram("xn1_dram", [T, D], BF16)
        self.wbf = self.idram("wbf", [dm.DH, D], BF16)
        self.wqkvbf = self.idram("wqkvbf", [D, 3 * D], BF16)
        self.vbf = self.idram("vbf", [dm.DH, D], BF16)
        self.wobf = self.idram("wobf", [dm.DH, D], BF16)
        self.disp = self.idram("disp", [dm.CAP, D], BF16)
        self.cc_y_in = self.idram("cc_y_in", [dm.CAP, D], BF16)
        self.cc_y_out = self.idram("cc_y_out", [8 * dm.CAP, D], BF16)

    def alloc_consts(self):
        dm = self.dm
        T, D, NE, NT, ND = dm.T, dm.D, dm.NE, dm.NT, dm.ND
        sb = self.sb
        self.identF = sb("identF", [128, 128], F32)
        self.identB = sb("identB", [128, 128], BF16)
        self.iotaE = sb("iotaE", [128, NE], F32)
        self.e2048 = sb("e2048", [128, 16], F32)
        self.onesK1_f = sb("onesK1_f", [1, 128], F32)
        self.onesK1_b = sb("onesK1_b", [1, 128], BF16)
        self.ones_col_b = sb("ones_col_b", [128, 1], BF16)
        self.xnT_bf = sb("xnT_bf", [128, ND * T], BF16)
        self.logits = sb("logits", [128, NT * NE], F32)
        self.p1c = sb("p1c", [128, NT], F32)
        self.p2c = sb("p2c", [128, NT], F32)
        self.oh1 = sb("oh1", [128, NT * NE], BF16)
        self.oh2 = sb("oh2", [128, NT * NE], BF16)
        self.flat1 = sb("flat1", [128, NT], I32)
        self.flat2 = sb("flat2", [128, NT], I32)
        self.xs = [sb("xs0", [128, D], F32)]
        self.xs.append(self.xs[0])
        self.stat = sb("stat", [128, 16], F32)
        self.xn_f = [sb("xn_f0", [128, D], F32)]
        self.xn_f.append(self.xn_f[0])
        self.xnT_st = sb("xnT_st", [128, D], F32)
        self.xn_bf_t = sb("xn_bf_t", [128, D], BF16)
        self.ind_sb = sb("ind_sb", [128, 128], BF16)
        self.rwT = sb("rwT", [128, ND * NE], F32)
        self.rb_sb = sb("rb_sb", [1, NE], F32)
        self.rmask_sb = sb("rmask_sb", [8, 1], F32)
        self.esel_sb = sb("esel_sb", [128, 8], F32)
        self.tsc = [sb(f"tsc{i}", [128, NE], F32) for i in range(4)]
        self.vcol = [sb(f"vcol{i}", [128, 1], F32) for i in range(6)]
        self.totAG = sb("totAG", [8, 16], F32)
        self.offs_c = sb("offs_c", [16, 1], F32)
        self.slots_tok = sb("slots_tok", [128, NT * NE], F32)
        self.myslot_i = sb("myslot_i", [128, dm.Q], I32)
        self.slmidx_sb = sb("slmidx_sb", [128, 1], F32)
        self.slmidx_i = sb("slmidx_i", [128, 1], I32)
        rt = ExitStack()
        self.seqF = rt.enter_context(self.nc.sbuf_tensor("seqF", [16, T], F32))
        self.slchunk = rt.enter_context(
            self.nc.sbuf_tensor("slchunk", [128, max(512, dm.Q)], F32))
        self.myslot_f = self.slchunk
        self.rw_sb = rt.enter_context(
            self.nc.sbuf_tensor("rw_sb", [NE, 128], F32))
        rt.close()
        SQW = min(512, dm.CAP)
        NHT = dm.DH // 128
        DCW = min(512, D)
        carve = {}
        off = 0
        for nm, sz in (("xinT", ND * SQW), ("h_buf", NHT * SQW),
                       ("w0", ND * 128), ("w1", ND * 128),
                       ("v0", ND * 128), ("v1", ND * 128)):
            carve[nm] = off
            off += sz
        self.carve = carve
        NS = dm.H * 64 // 128
        if off <= (NS // 2) * T:
            self.OT_BASE = 0
            self.moe_perm = lambda a, NS=NS: (a + NS // 2) % NS
            bufB_sz = max(ND * T, 4 * T)
        else:
            self.OT_BASE = off
            self.moe_perm = lambda a: a
            bufB_sz = max(off + NS * T, ND * T, 4 * T)
        self.bufB = sb("bufB", [128, bufB_sz], BF16)
        self.qT = sb("qT", [64, T], BF16)
        self.kT = sb("kT", [64, T], BF16)
        self.v_h = sb("v_h", [128, NT * 65], BF16)
        self.l65 = sb("l65", [65, 512], F32)
        self.ones65 = sb("ones65", [65, 128], F32)
        self.srow = self.l65
        self.phat = [sb(f"phat{i}", [128, 512], BF16) for i in range(2)]
        self.hsc = sb("hsc", [128, 512], BF16)
        self.ysb = sb("ysb", [128, 512], BF16)
        MM = T // 128 - 1
        self.MM = MM
        self.Tbase = sb("Tbase", [128, min(512, T)], F16)
        self.biascols = sb("biascols", [128, dm.H * (MM + 1)], F32)
        self.mpcols = sb("mpcols", [128, MM + 1], F32)
        self.m16 = sb("m16", [128, 1], F32)
        QW_ = min(512, T)
        BASEW = 128 * (QW_ // 128 - 1)
        self.dmaskG = sb("dmaskG", [128, QW_ + BASEW], BF16)
        self.dmask_off = [BASEW - 128 * m for m in range(QW_ // 128)]
        self.BASEW = BASEW
        self.bcst = sb("bcst", [128, 512], F32)
        self.wq_sb = [sb("wq_sb0", [128, ND * 192], BF16)]
        self.wq_sb.append(self.wq_sb[0])
        self.wrhs = [sb("wrhs0", [128, (D // 128) * min(512, D)], BF16)]
        self.wrhs.append(self.wrhs[0])
        self.gA = sb("gA", [128, D], BF16)
        self.xt1 = self.gA
        self.gB = sb("gB", [128, D], BF16)
        self.zbf = sb("zbf", [128, D], BF16)
        self.gwT = sb("gwT", [128, ND * NE], F32)
        self.wsel1 = sb("wsel1", [128, NT * NE], F32)
        self.gcols = sb("gcols", [128, ND], F32)
        self.bcols = sb("bcols", [128, ND], F32)
        self.bcols_b = sb("bcols_b", [128, ND], BF16)
        self.brow_h = sb("brow_h", [1, 192], BF16)
        self.ones_row_b = sb("ones_row_b", [1, 512], BF16)
        self.lmask = sb("lmask", [128, 256], BF16)
        self.t2_sb = sb("t2_sb", [32, 128], BF16)
        self.wselT = sb("wselT", [8, 128], BF16)
        self.wexpT = sb("wexpT", [32, 128], BF16)
        self.E832 = sb("E832", [8, 32], BF16)
        self.blin_sb = sb("blin_sb", [1, min(512, D)], BF16)
        self.right_sb = sb("right_sb", [32, min(512, D)], BF16)
        self.left_sb = sb("left_sb", [128, ND * 32], BF16)
        self.psb = [self.ps(f"psb{i}", [128, 512], F32) for i in range(6)]
        self.psT = self.ps("psT", [128, 256], BF16)
        self.psS = self.ps("psS", [128, 256], F32)

    def emit_consts(self):
        p, dm = self.p, self.dm
        NE = dm.NE

        def aff_ident(tile):
            t = p.op("pool", lambda e, _t=tile: e.memset(_t[:], 1.0))
            return p.op("pool", lambda e, _t=tile: e.affine_select(
                out=_t[:], in_=_t[:], pattern=[[-1, 128]],
                compare_op=OP.is_equal, fill=0.0, base=0, channel_multiplier=1),
                deps=[t])

        self.t_identB = aff_ident(self.identB)
        self.t_identF = aff_ident(self.identF)
        self.t_iotaE = p.op("pool", lambda e: e.iota(
            self.iotaE[:], [[1, NE]], base=0, channel_multiplier=0,
            allow_small_or_imprecise_dtypes=True))
        self.t_e2048 = p.op("pool", lambda e: e.iota(
            self.e2048[:], [[dm.CAP, 16]], base=0, channel_multiplier=0,
            allow_small_or_imprecise_dtypes=True))
        self.t_onesK1f = p.op("pool", lambda e: e.memset(self.onesK1_f[:], 1.0))
        self.t_onesK1b = p.op("pool", lambda e: e.memset(self.onesK1_b[:], 1.0))
        self.t_onescb = p.op("pool", lambda e: e.memset(self.ones_col_b[:], 1.0))
        self.t_vones = p.op("pool", lambda e: e.memset(self.v_h[:], 1.0))
        self.t_o65 = p.op("pool", lambda e: e.memset(self.ones65[:], 1.0))
        self.t_z16 = None
        QW = min(512, dm.T)
        self.QW = QW
        t_tb = p.op("pool", lambda e, QW=QW: e.iota(
            self.Tbase[:, 0:QW], [[-1, QW]], base=0,
            channel_multiplier=1, allow_small_or_imprecise_dtypes=True))
        self.t_Tbase = t_tb
        self.t_m16 = p.op("pool", lambda e: e.memset(self.m16[:], -C_STAB))
        MM1 = self.MM + 1
        t_mp = p.op("pool", lambda e, MM1=MM1: e.iota(
            self.mpcols[:, 0:MM1], [[128, MM1]], base=0,
            channel_multiplier=0, allow_small_or_imprecise_dtypes=True))
        tb = t_mp
        for h in range(dm.H):
            sl = dm.slopes[h]
            tb = p.op("act", lambda e, h=h, sl=sl, MM1=MM1: e.activation(
                out=self.biascols[:, h * MM1:(h + 1) * MM1],
                in_=self.mpcols[:, 0:MM1], func=AF.Copy, scale=-sl),
                deps=[t_mp])
        t_bc2 = p.op("ve", lambda e: e.tensor_scalar(
            self.biascols[:], self.biascols[:], -C_STAB, None, OP.add),
            deps=[tb])
        self.t_biascols = t_bc2
        WG = QW + self.BASEW
        tz = p.op("pool", lambda e, WG=WG: e.memset(self.dmaskG[:, 0:WG], 0.0))
        t_dg = p.op("pool", lambda e, WG=WG: e.affine_select(
            out=self.dmaskG[:, 0:WG], in_=self.dmaskG[:, 0:WG],
            pattern=[[-1, WG]], compare_op=OP.is_ge, fill=NEGBIG,
            base=self.BASEW, channel_multiplier=1), deps=[tz])
        self.t_dmask = [t_dg] * (QW // 128)
        self.t_rb = p.dma("qsp", self.rb_sb[:], self.rb_in[:])
        self.t_rmask = p.dma("qsp", self.rmask_sb[:], self.rmask_in[:])
        self.t_esel = p.dma("qsp", self.esel_sb[:], self.esel_in[:])
        self.t_slmidx = p.dma("qsp", self.slmidx_sb[:], self.slmidx_in[:])
        tprev = None
        for src_t, dst in ((self.rw_in, self.rwT), (self.gw_in, self.gwT)):
            for c in range(dm.ND):
                t_rw = p.dma("qsp", self.rw_sb[:],
                             src_t[0:NE, c * 128:(c + 1) * 128],
                             deps=([tprev] if tprev else []))
                tt = p.op("te", lambda e: e.transpose(
                    self.psS[0:128, 0:NE], self.rw_sb[0:NE, 0:128],
                    self.identF[0:NE, 0:NE]),
                    deps=[t_rw, self.t_identF] + ([tprev] if tprev else []))
                tprev = p.op("ve", lambda e, c=c, dst=dst: e.tensor_copy(
                    dst[:, c * NE:(c + 1) * NE], self.psS[0:128, 0:NE]),
                    deps=[tt])
        self.t_rwT = tprev
        self.t_gwT = tprev
        self.t_orow = p.op("pool", lambda e: e.memset(self.ones_row_b[:], 1.0))
        tE = p.op("pool", lambda e: e.memset(self.E832[:], 1.0))
        self.t_E832 = p.op("pool", lambda e: e.affine_select(
            out=self.E832[:], in_=self.E832[:], pattern=[[-1, 8], [0, dm.R]],
            compare_op=OP.is_equal, fill=0.0, base=0, channel_multiplier=1),
            deps=[tE])
        tm = p.op("pool", lambda e: e.memset(self.lmask[:], 0.0))
        tm = p.op("pool", lambda e: e.affine_select(
            out=self.lmask[:, 0:128], in_=self.lmask[:, 0:128],
            pattern=[[-1, 128]], compare_op=OP.is_ge, fill=8.0 * NEGBIG,
            base=0, channel_multiplier=1), deps=[tm])
        self.t_lmask = p.op("pool", lambda e: e.affine_select(
            out=self.lmask[:, 128:256], in_=self.lmask[:, 128:256],
            pattern=[[1, 128]], compare_op=OP.is_ge, fill=8.0 * NEGBIG,
            base=0, channel_multiplier=-1), deps=[tm])
        leftr = self.left_in.ap().rearrange("(c p) r -> p c r", p=128)
        self.t_left = p.dma("qpool", self.left_sb.ap().rearrange(
            "p (c r) -> p c r", r=32), leftr)
        self.t_gcols = []
        self.t_bcols = []
        for c in range(dm.ND):
            gc = self.l1g_in.ap().rearrange("a (c p) -> (a c) p", p=128)
            self.t_gcols.append(p.dma("qsp", self.gcols[:, c:c + 1],
                                gc[c:c + 1, :].rearrange("a p -> p a")))
            bc = self.l1b_in.ap().rearrange("a (c p) -> (a c) p", p=128)
            self.t_bcols.append(p.dma("qsp", self.bcols[:, c:c + 1],
                                bc[c:c + 1, :].rearrange("a p -> p a")))
        self.t_bcolsb = p.op("ve", lambda e: e.tensor_copy(
            self.bcols_b[:], self.bcols[:]), deps=self.t_bcols)
        self.t_wqkvc = p.dma("qpool", self.wqkvbf[:], self.wqkv_in[:])
        mwr = self.mw_in.ap().rearrange("(c p) h -> p c h", p=128)
        mvr = self.mv_in.ap().rearrange("(c p) h -> p c h", p=128)
        self.t_wc = []
        self.t_vc = []
        NHT_ = dm.DH // 128
        for t_ in range(NHT_):
            self.t_wc.append(p.dma(
                "qpool", self.wbf[t_ * 128:(t_ + 1) * 128, :],
                mwr[:, :, t_ * 128:(t_ + 1) * 128]))
            self.t_vc.append(p.dma(
                "qpool", self.vbf[t_ * 128:(t_ + 1) * 128, :],
                mvr[:, :, t_ * 128:(t_ + 1) * 128]))
        self.t_oc = p.dma("qpool", self.wobf[:], self.mwo_in[:])
        self.t_wq1c = p.dma("qpool", self.wqkv1bf[:], self.wqkv1_in[:])
        self.t_wo1c = p.dma("qpool", self.wout1bf[:], self.wout1_in[:])
        self.t_wlc = p.dma("qpool", self.wlinbf[:], self.wlin_in[:])
        p.fence_queue("qpool")

    def phase_ln_router(self):
        p, dm = self.p, self.dm
        T, D, NE, NT, ND = dm.T, dm.D, dm.NE, dm.NT, dm.ND
        eps = 1e-5
        xn_done = []
        logit_done = []
        self.t_logits = logit_done
        self.t_xnbf = []
        xs, xn_f = self.xs[0], self.xn_f[0]
        st = lambda b, k: self.stat[:, b * 8 + k:b * 8 + k + 1]
        stA = {}
        stB = {}

        def stage_a(i):
            b = i % 2
            t_x = p.dma("qsp", xs[:], self.x_in[i * 128:(i + 1) * 128, :],
                        deps=stA.get("xs_war", []))
            t_s1 = p.op("ve", lambda e, b=b: e.reduce_sum(
                st(b, 0), xs[:], axis=AX.X), deps=[t_x])
            t_sq = p.op("act", lambda e, b=b: e.activation(
                out=self.xn_bf_t[:], in_=xs[:], func=AF.Square,
                accum_out=st(b, 1)), deps=[t_x] + stA.get("sq_war", []))
            t_m = p.op("ve", lambda e, b=b: e.tensor_scalar(
                st(b, 2), st(b, 0), 1.0 / D, None, OP.mult), deps=[t_s1])
            t_ex2 = p.op("ve", lambda e, b=b: e.tensor_scalar(
                st(b, 3), st(b, 1), 1.0 / D, eps, OP.mult, OP.add),
                deps=[t_sq])
            t_m2 = p.op("ve", lambda e, b=b: e.tensor_tensor(
                st(b, 4), st(b, 2), st(b, 2), OP.mult), deps=[t_m])
            t_var = p.op("ve", lambda e, b=b: e.tensor_tensor(
                st(b, 5), st(b, 3), st(b, 4), OP.subtract),
                deps=[t_ex2, t_m2])
            t_sd = p.op("act", lambda e, b=b: e.activation(
                out=st(b, 6), in_=st(b, 5), func=AF.Ln), deps=[t_var])
            t_rstd = p.op("act", lambda e, b=b: e.activation(
                out=st(b, 5), in_=st(b, 6), func=AF.Exp, scale=-0.5),
                deps=[t_sd])
            t_nmr = p.op("ve", lambda e, b=b: e.scalar_tensor_tensor(
                st(b, 4), st(b, 2), -1.0, st(b, 5), OP.mult, OP.mult),
                deps=[t_rstd, t_m])
            stA[("done", i)] = (t_x, t_nmr, t_sq)

        def stage_b(i):
            b = i % 2
            t_x, t_nmr, t_sq = stA.pop(("done", i))
            t_xn = p.op("ve", lambda e, b=b: e.tensor_scalar(
                xn_f[:], xs[:], st(b, 5), st(b, 4), OP.mult, OP.add),
                deps=[t_nmr, t_x] + stB.get("xnf_war", []))
            t_cb = p.op("ve", lambda e: e.tensor_copy(
                self.xn_bf_t[:], xn_f[:]),
                deps=[t_xn, t_sq] + (xn_done[-1:] if xn_done else []))
            t_st = p.dma("qact", self.cc_xn_in[i * 128:(i + 1) * 128, :],
                         self.xn_bf_t[:], deps=[t_cb])
            xn_done.append(t_st)
            stB[i] = t_xn
            stA["xs_war"] = [t_xn, t_sq]
            stA["sq_war"] = [t_st]

        def stage_c(i):
            t_xn = stB.pop(i)
            t_rmm = None
            psev = getattr(self, "_psS_ev", {})
            t_bf = None
            last_tt = None
            for c in range(ND):
                sl_ = c % 2
                tt = p.op("te", lambda e, c=c, sl_=sl_: e.transpose(
                    self.psS[:, sl_ * 128:(sl_ + 1) * 128],
                    xn_f[:, c * 128:(c + 1) * 128],
                    self.identF[:]),
                    deps=[t_xn, self.t_identF] +
                         ([psev[sl_]] if sl_ in psev else []))
                last_tt = tt
                tev = p.op("ve", lambda e, c=c, sl_=sl_: e.tensor_copy(
                    self.xnT_st[:, c * 128:(c + 1) * 128],
                    self.psS[:, sl_ * 128:(sl_ + 1) * 128]),
                    deps=[tt])
                psev[sl_] = tev
                self._psS_ev = psev
                self._t_psS_last = tev
                t_bf = p.op("ve", lambda e, c=c, i=i: e.tensor_copy(
                    self.xnT_bf[:, c * T + i * 128:c * T + (i + 1) * 128],
                    self.xnT_st[:, c * 128:(c + 1) * 128]), deps=[tev])
                rdeps = [tev, self.t_rwT]
                if c == 0 and logit_done:
                    rdeps.append(logit_done[-1])
                t_rmm = p.op("te", lambda e, c=c, i=i: e.matmul(
                    self.psb[3][0:128, 0:NE],
                    self.xnT_st[:, c * 128:(c + 1) * 128],
                    self.rwT[:, c * NE:(c + 1) * NE],
                    start=(c == 0), stop=False), deps=rdeps)
            t_rmm = p.op("te", lambda e: e.matmul(
                self.psb[3][0:128, 0:NE], self.onesK1_f[:], self.rb_sb[:],
                start=False, stop=True),
                deps=[t_rmm, self.t_onesK1f, self.t_rb])
            t_lg = p.op("ve", lambda e, i=i: e.tensor_copy(
                self.logits[:, i * NE:(i + 1) * NE], self.psb[3][0:128, 0:NE]),
                deps=[t_rmm])
            logit_done.append(t_lg)
            self.t_xnbf.append(t_bf)
            stB["xnf_war"] = [last_tt, t_bf]

        stage_a(0)
        stage_b(0)
        for i in range(NT):
            stage_c(i)
            if i + 1 < NT:
                stage_a(i + 1)
                stage_b(i + 1)
            self.emit_indicator(i)
        self.t_xn_all = xn_done
        self.t_logits = logit_done
        self.t_ag_xn = p.collective("AllGather", OP.bypass,
                                    [self.cc_xn_in.ap().opt()],
                                    [self.cc_xn_out.ap().opt()],
                                    deps=xn_done)
        self.dump("logits", [128, NT * NE], F32, lambda: self.logits[:],
                  logit_done)
        self.dump("xnT_bf", [128, ND * T], BF16, lambda: self.xnT_bf[:],
                  logit_done)

    def emit_indicator(self, i):
        p, dm = self.p, self.dm
        T, NE, NT = dm.T, dm.NE, dm.NT
        a, b2, c3, d4 = self.tsc
        mx, mx2, icol, dcol, pcol, qcol = self.vcol
        lg = self.logits[:, i * NE:(i + 1) * NE]
        dep0 = [self.t_logits[i]]
        if i == 0:
            self._ind_st = []
            self._t_indz = p.op("pool", lambda e: e.memset(self.ind_sb[:], 0.0))
        ind_st = self._ind_st
        t1 = p.op("ve", lambda e, lg=lg: e.reduce_max(mx[:], lg, axis=AX.X),
                  deps=dep0)
        teq = p.op("ve", lambda e, lg=lg: e.tensor_scalar(
            a[:], lg, mx[:], None, OP.is_equal), deps=[t1])
        trv = p.op("ve", lambda e: e.scalar_tensor_tensor(
            b2[:], self.iotaE[:], -1.0, a[:], OP.mult, OP.mult),
            deps=[teq, self.t_iotaE])
        trv2 = p.op("ve", lambda e: e.scalar_tensor_tensor(
            b2[:], a[:], float(NE), b2[:], OP.mult, OP.add), deps=[trv])
        tmx = p.op("ve", lambda e: e.reduce_max(icol[:], b2[:], axis=AX.X),
                   deps=[trv2])
        ti1 = p.op("ve", lambda e: e.tensor_scalar(
            icol[:], icol[:], -1.0, float(NE), OP.mult, OP.add), deps=[tmx])
        to1 = p.op("ve", lambda e, i=i: e.tensor_scalar(
            self.oh1[:, i * NE:(i + 1) * NE], self.iotaE[:], icol[:], None,
            OP.is_equal), deps=[ti1])
        tml = p.op("ve", lambda e, lg=lg, i=i: e.scalar_tensor_tensor(
            c3[:], self.oh1[:, i * NE:(i + 1) * NE], float(NEGBIG), lg,
            OP.mult, OP.add), deps=[to1])
        t2 = p.op("ve", lambda e: e.reduce_max(mx2[:], c3[:], axis=AX.X),
                  deps=[tml])
        teq2 = p.op("ve", lambda e: e.tensor_scalar(
            a[:], c3[:], mx2[:], None, OP.is_equal), deps=[t2])
        trv3 = p.op("ve", lambda e: e.scalar_tensor_tensor(
            b2[:], self.iotaE[:], -1.0, a[:], OP.mult, OP.mult), deps=[teq2])
        trv4 = p.op("ve", lambda e: e.scalar_tensor_tensor(
            b2[:], a[:], float(NE), b2[:], OP.mult, OP.add), deps=[trv3])
        tmx2 = p.op("ve", lambda e: e.reduce_max(dcol[:], b2[:], axis=AX.X),
                    deps=[trv4])
        ti2 = p.op("ve", lambda e: e.tensor_scalar(
            dcol[:], dcol[:], -1.0, float(NE), OP.mult, OP.add), deps=[tmx2])
        to2 = p.op("ve", lambda e, i=i: e.tensor_scalar(
            self.oh2[:, i * NE:(i + 1) * NE], self.iotaE[:], dcol[:], None,
            OP.is_equal), deps=[ti2])
        tdv = p.op("ve", lambda e: e.tensor_tensor(
            pcol[:], mx[:], mx2[:], OP.subtract), deps=[t1, t2])
        tp1 = p.op("act", lambda e, i=i: e.activation(
            out=self.p1c[:, i:i + 1], in_=pcol[:], func=AF.Exp,
            scale=-1.0), deps=[tdv])
        tp1 = p.op("ve", lambda e, i=i: e.tensor_scalar(
            self.p1c[:, i:i + 1], self.p1c[:, i:i + 1], 1.0, None, OP.add),
            deps=[tp1])
        tp1 = p.op("ve", lambda e, i=i: e.reciprocal(
            self.p1c[:, i:i + 1], self.p1c[:, i:i + 1]), deps=[tp1])
        tp2 = p.op("ve", lambda e, i=i: e.tensor_scalar(
            self.p2c[:, i:i + 1], self.p1c[:, i:i + 1], -1.0, 1.0,
            OP.mult, OP.add), deps=[tp1])
        tind = p.op("ve", lambda e, i=i: e.tensor_tensor(
            d4[:], self.oh1[:, i * NE:(i + 1) * NE],
            self.oh2[:, i * NE:(i + 1) * NE], OP.add), deps=[to1, to2])
        tz = self._t_indz if not ind_st else ind_st[-1][0]
        tcv = p.op("ve", lambda e: e.tensor_copy(
            self.ind_sb[0:128, 0:8], d4[:]), deps=[tind, tz])
        tst = p.dma("qact", self.ind_dram[i * 128:(i + 1) * 128, :],
                    self.ind_sb[0:128, 0:128], deps=[tcv])
        ind_st.append((tst, tp2))

    def phase_routing(self):
        p, dm = self.p, self.dm
        T, D, NE, NT = dm.T, dm.D, dm.NE, dm.NT
        CAP, TG, Q = dm.CAP, dm.TG, dm.Q
        a, b2, c3, d4 = self.tsc
        mx, mx2, icol, dcol, pcol, qcol = self.vcol
        ind_st = self._ind_st
        self.t_z16 = p.op("ve", lambda e: e.memset(
            self.bufB[0:16, 3 * T:4 * T], 0.0))
        cum = self.seqF[0:16, 0:T]
        t_ld = p.dma("qact", self.bufB[0:128, 2 * T:3 * T], self.ind_dram[:],
                     deps=[t for t, _ in ind_st], transpose=True)
        t_scan = p.op("ve", lambda e, cum=cum: e.tensor_tensor_scan(
            cum, self.bufB[0:16, 2 * T:3 * T], self.bufB[0:16, 3 * T:4 * T],
            0.0, OP.add, OP.add), deps=[t_ld, self.t_z16])
        t_tot = p.dma("qact", self.cc_tot_in[0:1, 0:16],
                      self.seqF[0:16, T - 1:T], deps=[t_scan])
        t_agt = p.collective("AllGather", OP.bypass,
                             [self.cc_tot_in.ap().opt()],
                             [self.cc_tot_out.ap().opt()], deps=[t_tot])
        kmask = self.bufB[0:16, 3 * T:4 * T]
        self._rb = dict(t_agt=t_agt, t_scan=t_scan, t_tot=t_tot, cum=cum,
                        kmask=kmask)
        return

    def emit_routing_b(self):
        p, dm = self.p, self.dm
        T, NE, NT, CAP = dm.T, dm.NE, dm.NT, dm.CAP
        t_agt = self._rb["t_agt"]
        t_scan = self._rb["t_scan"]
        t_tot = self._rb["t_tot"]
        cum = self._rb["cum"]
        kmask = self._rb["kmask"]
        t_ldt = p.dma("qsp", self.totAG[:], self.cc_tot_out[:], deps=[t_agt])
        t_off = p.op("te", lambda e: e.matmul(
            self.psS[0:1, 0:16], self.rmask_sb[:], self.totAG[:],
            start=True, stop=True), deps=[t_ldt, self.t_rmask])
        t_offr = p.op("ve", lambda e: e.tensor_copy(
            self.srow[0:1, 0:16], self.psS[0:1, 0:16]), deps=[t_off])
        t_offd = p.dma("qact", self.offs_dram[:], self.srow[0:1, 0:16],
                       deps=[t_offr])
        t_offc = p.dma("qsp", self.offs_c[:],
                       self.offs_dram.ap().rearrange("a b -> b a"), deps=[t_offd])
        kcol = self.vcol[0][0:16, 0:1]
        ocol = self.vcol[1][0:16, 0:1]
        t_kc = p.op("ve", lambda e, kcol=kcol: e.tensor_scalar(
            kcol, self.offs_c[:], -1.0, float(CAP) + 0.5, OP.mult, OP.add),
            deps=[t_offc])
        t_oc2 = p.op("ve", lambda e, ocol=ocol: e.tensor_scalar(
            ocol, self.offs_c[:], 1.0, -1.0 - 1.0e6, OP.mult, OP.add),
            deps=[t_offc])
        t_k0 = p.op("ve", lambda e, cum=cum, kmask=kmask, kcol=kcol: e.tensor_scalar(
            kmask, cum, kcol, None, OP.is_lt), deps=[t_kc, t_scan, t_tot])
        t_k1 = p.op("ve", lambda e, kmask=kmask: e.tensor_tensor(
            kmask, kmask, self.bufB[0:16, 2 * T:3 * T], OP.mult), deps=[t_k0])
        t_s0 = p.op("ve", lambda e, cum=cum, ocol=ocol: e.tensor_scalar(
            cum, cum, ocol, None, OP.add), deps=[t_oc2, t_k0])
        t_s1 = p.op("ve", lambda e, cum=cum, kmask=kmask: e.tensor_tensor(
            cum, cum, kmask, OP.mult), deps=[t_s0, t_k1])
        t_s2 = p.op("ve", lambda e, cum=cum: e.tensor_scalar(
            cum, cum, 1.0e6, None, OP.add), deps=[t_s1])
        self.t_slots_done = t_s2
        t_sst = p.dma("qact", self.cc_sl_in[:], self.seqF[0:16, 0:T],
                      deps=[t_s2])
        t_ags = p.collective("AllGather", OP.bypass,
                             [self.cc_sl_in.ap().opt()],
                             [self.cc_sl_out.ap().opt()], deps=[t_sst])
        self.t_ags = t_ags
        t_mi = p.op("ve", lambda e: e.tensor_copy(
            self.slmidx_i[:], self.slmidx_sb[:]), deps=[self.t_slmidx])
        slr = self.cc_sl_out.ap().rearrange("p (k q) -> (p k) q", q=dm.Q)
        t_ms = p.indirect(self.myslot_f[:, 0:dm.Q], None, slr,
                          bass.IndirectOffsetOnAxis(self.slmidx_i[:], 0),
                          128 * (T // dm.Q) - 1, deps=[t_ags, t_mi])
        self.t_myslot = p.op("ve", lambda e: e.tensor_copy(
            self.myslot_i[:], self.myslot_f[:, 0:dm.Q]), deps=[t_ms])
        self._rb["t_ms"] = t_ms
        self._rb["t_s2"] = self.t_slots_done

    def emit_routing_c(self):
        p, dm = self.p, self.dm
        T, NE, NT, CAP = dm.T, dm.NE, dm.NT, dm.CAP
        t_s2 = self._rb["t_s2"]
        t_scan = self._rb["t_scan"]
        tok_sl = []
        tprev = None
        for i in range(NT):
            tt = p.op("te", lambda e, i=i: e.transpose(
                self.psS[0:128, 0:16],
                self.seqF[0:16, i * 128:(i + 1) * 128],
                self.identF[0:16, 0:16]),
                deps=[t_s2, self.t_identF] + ([tprev] if tprev else []))
            tprev = p.op("ve", lambda e, i=i: e.tensor_copy(
                self.slots_tok[:, i * NE:(i + 1) * NE], self.psS[0:128, 0:NE]),
                deps=[tt])
            tok_sl.append(tprev)
            self._last_seqF_rd = tt
        a = self.tsc[0]
        fl_done = []
        for i in range(NT):
            tfc = p.op("ve", lambda e, i=i: e.tensor_tensor(
                a[:], self.slots_tok[:, i * NE:(i + 1) * NE], self.e2048[:, 0:NE],
                OP.add), deps=[tok_sl[i], self.t_e2048])
            tm1 = p.op("ve", lambda e, i=i: e.tensor_tensor(
                self.tsc[1][:], a[:], self.oh1[:, i * NE:(i + 1) * NE], OP.mult),
                deps=[tfc])
            tr1 = p.op("ve", lambda e: e.reduce_sum(
                self.vcol[0][:], self.tsc[1][:], axis=AX.X), deps=[tm1])
            tf1 = p.op("ve", lambda e, i=i: e.tensor_copy(
                self.flat1[:, i:i + 1], self.vcol[0][:]), deps=[tr1])
            tm2 = p.op("ve", lambda e, i=i: e.tensor_tensor(
                self.tsc[1][:], a[:], self.oh2[:, i * NE:(i + 1) * NE], OP.mult),
                deps=[tf1])
            tr2 = p.op("ve", lambda e: e.reduce_sum(
                self.vcol[0][:], self.tsc[1][:], axis=AX.X), deps=[tm2])
            tf2 = p.op("ve", lambda e, i=i: e.tensor_copy(
                self.flat2[:, i:i + 1], self.vcol[0][:]), deps=[tr2])
            fl_done.append(tf2)
        self.t_flats = fl_done
        d1 = self.dump("slotsT", [16, T], F32,
                       lambda: self.seqF[0:16, 0:T], [t_s2])
        d2 = self.dump("cum", [16, T], F32, lambda: self.seqF[0:16, 0:T],
                       [t_scan])
        self.t_seqF_done = [self._last_seqF_rd] + [t for t in (d1, d2) if t]
        self.dump("myslot", [128, dm.Q], F32,
                  lambda: self.myslot_f[:, 0:dm.Q], [self._rb["t_ms"]])
        self.dump("slots_tok", [128, NT * NE], F32, lambda: self.slots_tok[:],
                  tok_sl)
        self.dump("oh1", [128, NT * NE], BF16, lambda: self.oh1[:], fl_done)
        self.dump("oh2", [128, NT * NE], BF16, lambda: self.oh2[:], fl_done)
        self.dump("p1c", [128, NT], F32, lambda: self.p1c[:], fl_done)
        self.dump("p2c", [128, NT], F32, lambda: self.p2c[:], fl_done)
        fl32 = self.sb("fl32", [128, 2 * NT], F32)
        if self.want("flats"):
            t1 = p.op("ve", lambda e: e.tensor_copy(fl32[:, 0:NT], self.flat1[:]),
                      deps=fl_done)
            t2 = p.op("ve", lambda e: e.tensor_copy(fl32[:, NT:2 * NT], self.flat2[:]),
                      deps=[t1])
            self.dump("flats", [128, 2 * NT], F32, lambda: fl32[:], [t2])

    def phase_moe_scatter(self):
        p, dm = self.p, self.dm
        D, CAP, Q = dm.D, dm.CAP, dm.Q
        nwr = ((dm.D // 128) * min(512, dm.D)) // D
        xr_v = [lambda k=k: self.wrhs[0][:, k * D:(k + 1) * D]
                for k in range(nwr)]
        xr_v.append(lambda: self.gA[:])
        xr_v.append(lambda: self.gB[:])
        xr_v.append(lambda: self.zbf[:])
        NXR = len(xr_v)
        t_z = p.op("pool", lambda e: e.memset(xr_v[0](), 0.0))
        zd = []
        for rb in range(CAP // 128):
            zd.append(p.dma("qpool", self.disp[rb * 128:(rb + 1) * 128, :],
                            xr_v[0](), deps=[t_z]))
        xn_all_r = self.cc_xn_out.ap().rearrange("(p q) d -> p q d", q=Q)
        self._scat = dict(xr_v=xr_v, NXR=NXR, nwr=nwr, zd=zd,
                          xn_all_r=xn_all_r, prev_grp=[], last_buf_sc={},
                          next_g0=0)
        self.sc_all = []
        self.t_xr_war = {}
        self._moe = dict(prev_ab={}, y_done=[], wl={}, first_wo=True)

    def emit_scatter_groups(self, ngrp):
        p, dm = self.p, self.dm
        D, CAP, Q = dm.D, dm.CAP, dm.Q
        st_ = self._scat
        xr_v, NXR, nwr = st_["xr_v"], st_["NXR"], st_["nwr"]
        for _ in range(ngrp):
            g0 = st_["next_g0"]
            if g0 >= Q:
                break
            st_["next_g0"] = g0 + NXR
            lds = []
            for k in range(min(NXR, Q - g0)):
                j = g0 + k
                ldeps = (([self.t_ag_xn] + st_["zd"] + st_["prev_grp"])
                         if k == 0 else [])
                lds.append(p.dma("qsp", xr_v[k](),
                                 st_["xn_all_r"][:, j, :], deps=ldeps))
            grp = []
            for k in range(min(NXR, Q - g0)):
                j = g0 + k
                t_sc = p.indirect(self.disp[:], bass.IndirectOffsetOnAxis(
                    self.myslot_i[:, j:j + 1], 0), xr_v[k](), None,
                    CAP - 1, deps=[lds[k], self.t_myslot])
                grp.append(t_sc)
                st_["last_buf_sc"][k] = t_sc
            self.sc_all.extend(grp)
            st_["prev_grp"] = grp
        if st_["next_g0"] >= Q and not self.t_xr_war:
            lb = st_["last_buf_sc"]
            self.t_xr_war = {"gA": lb.get(nwr), "gB": lb.get(nwr + 1),
                             "zbf": lb.get(nwr + 2)}
            if self.want("disp"):
                dd = self.eout("dbg_disp", [dm.CAP, dm.D], BF16)
                p.dma("qpool", dd[:], self.disp[:], deps=self.sc_all)

    def emit_moe_chunk(self, sq, exp_prev, pvtok):
        p, dm = self.p, self.dm
        D, DH, CAP, ND = dm.D, dm.DH, dm.CAP, dm.ND
        SQW = min(512, CAP)
        NHT = DH // 128
        DCW = min(256, D)
        NDC = D // DCW
        cv = self.carve
        B = self.bufB
        xinT = lambda sl: B[:, cv["xinT"] + sl[0]:cv["xinT"] + sl[1]]
        h_buf = lambda sl: B[:, cv["h_buf"] + sl[0]:cv["h_buf"] + sl[1]]
        wo_v = lambda sl: self.wrhs[0][:, sl[0]:sl[1]]
        w_v = [lambda sl, o=cv["w0"]: B[:, o + sl[0]:o + sl[1]],
               lambda sl, o=cv["w1"]: B[:, o + sl[0]:o + sl[1]]]
        v_v = [lambda sl, o=cv["v0"]: B[:, o + sl[0]:o + sl[1]],
               lambda sl, o=cv["v1"]: B[:, o + sl[0]:o + sl[1]]]
        hsc = self.hsc
        ysb = self.ysb
        rbar = self.t_slots_done
        t_wcl, t_vcl, t_oc = self.t_wc, self.t_vc, self.t_oc
        st_ = self._moe
        prev_ab, y_done, wl = st_["prev_ab"], st_["y_done"], st_["wl"]
        wobf_r = self.wobf.ap().rearrange("(t p) d -> p t d", p=128)
        xt = []
        for c in range(ND):
            xdep = self.sc_all + [rbar] + getattr(self, "t_seqF_done", [])
            if "xt" in prev_ab:
                xdep.append(prev_ab["xt"])
            xt.append(p.dma("qsp", xinT((c * SQW, (c + 1) * SQW)),
                            self.disp[sq * SQW:(sq + 1) * SQW,
                                      c * 128:(c + 1) * 128],
                            deps=xdep, transpose=True))
        for ht in range(NHT):
            bsel = ht % 2
            wld = wl.get(("w", bsel))
            t_w = p.dma("qsp", w_v[bsel]((0, ND * 128)),
                        self.wbf[ht * 128:(ht + 1) * 128, :],
                        deps=[t_wcl[ht], rbar] + ([wld] if wld else []))
            t_v = p.dma("qsp", v_v[bsel]((0, ND * 128)),
                        self.vbf[ht * 128:(ht + 1) * 128, :],
                        deps=[t_vcl[ht]] + ([wld] if wld else []))
            mma = None
            mmb = None
            adeps = xt + [t_w]
            bdeps = [t_v]
            if "ab" in prev_ab:
                adeps = adeps + [prev_ab["ab"]]
                bdeps = bdeps + [prev_ab["ab"]]
            if exp_prev.get(0) is not None:
                adeps = adeps + [exp_prev[0]]
            if exp_prev.get(1) is not None:
                bdeps = bdeps + [exp_prev[1]]
            for c in range(ND):
                mma = p.op("te", lambda e, c=c, b=bsel, SQW=SQW: e.matmul(
                    self.psb[0][0:128, 0:SQW],
                    w_v[b]((c * 128, (c + 1) * 128)),
                    xinT((c * SQW, (c + 1) * SQW)),
                    start=(c == 0), stop=(c == ND - 1)),
                    deps=(adeps if c == 0 else []))
            for c in range(ND):
                mmb = p.op("te", lambda e, c=c, b=bsel, SQW=SQW: e.matmul(
                    self.psb[1][0:128, 0:SQW],
                    v_v[b]((c * 128, (c + 1) * 128)),
                    xinT((c * SQW, (c + 1) * SQW)),
                    start=(c == 0), stop=(c == ND - 1)),
                    deps=(bdeps if c == 0 else []))
            t_si = p.op("act", lambda e, SQW=SQW: e.activation(
                out=hsc[:, 0:SQW], in_=self.psb[1][0:128, 0:SQW],
                func=AF.Silu),
                deps=[mmb] + ([prev_ab["ab"]] if "ab" in prev_ab else []))
            t_h = p.op("ve", lambda e, ht=ht, SQW=SQW: e.tensor_tensor(
                h_buf((ht * SQW, (ht + 1) * SQW)), self.psb[0][0:128, 0:SQW],
                hsc[:, 0:SQW], OP.mult), deps=[mma, t_si])
            prev_ab["ab"] = t_h
            wl[("w", bsel)] = t_h
        prev_ab["xt"] = prev_ab["ab"]
        t_h_all = prev_ab["ab"]
        t_ye = None
        for dc in range(NDC):
            wod = wl.get("wo")
            wo_deps = [t_oc, rbar] + ([wod] if wod else [])
            if st_["first_wo"]:
                wo_deps += self.sc_all
                st_["first_wo"] = False
            t_wo = p.dma("qsp", wo_v((0, NHT * DCW)),
                         wobf_r[:, :, dc * DCW:(dc + 1) * DCW],
                         deps=wo_deps)
            for st in range(SQW // 128):
                pb = self.psb[2 + (st % 2)]
                mmy = None
                ydeps = [t_h_all, t_wo, self.t_logits[-1]]
                pv = prev_ab.get(("y", st % 2))
                if pv:
                    ydeps.append(pv)
                if pvtok is not None:
                    ydeps.append(pvtok)
                for ht in range(NHT):
                    mmy = p.op("te", lambda e, ht=ht, st=st, pb=pb, SQW=SQW,
                               DCW=DCW: e.matmul(
                        pb[0:128, 0:DCW],
                        h_buf((ht * SQW + st * 128, ht * SQW + (st + 1) * 128)),
                        wo_v((ht * DCW, (ht + 1) * DCW)),
                        start=(ht == 0), stop=(ht == NHT - 1)),
                        deps=(ydeps if ht == 0 else []))
                t_ye = p.op("ve", lambda e, pb=pb, DCW=DCW: e.tensor_copy(
                    ysb[:, 0:DCW], pb[0:128, 0:DCW]), deps=[mmy] +
                    (y_done[-1:] if y_done else []))
                t_yd = p.dma("qact",
                             self.cc_y_in[sq * SQW + st * 128:
                                          sq * SQW + (st + 1) * 128,
                                          dc * DCW:(dc + 1) * DCW],
                             ysb[:, 0:DCW], deps=[t_ye])
                prev_ab[("y", st % 2)] = t_ye
                y_done.append(t_yd)
            wl["wo"] = t_ye
        return prev_ab["ab"], t_ye

    def emit_moe_ag(self):
        p, dm = self.p, self.dm
        self.t_ag_y = p.collective("AllGather", OP.bypass,
                                   [self.cc_y_in.ap().opt()],
                                   [self.cc_y_out.ap().opt()],
                                   deps=self._moe["y_done"])
        if self.want("y_out"):
            yo = self.eout("dbg_y_out", [8 * dm.CAP, dm.D], BF16)
            p.dma("qpool", yo[:], self.cc_y_out[:], deps=[self.t_ag_y])

    def phase_attn_l0(self):
        p, dm = self.p, self.dm
        T, D, H, ND, NT = dm.T, dm.D, dm.H, dm.ND, dm.NT
        QW = self.QW
        NQC = T // QW
        wqkv_r = self.wqkvbf.ap().rearrange("(c p) n -> p c n", p=128)
        self.t_OT = []
        exp_prev = {}
        last_wq = {}
        last_qkT = None
        t_pe = None
        pvtok = None
        import math as _math
        ngrp_tot = (dm.Q + self._scat["NXR"] - 1) // self._scat["NXR"]
        gph = _math.ceil(ngrp_tot / max(1, min(4, H - self.SCAT_START)))
        for h in range(H):
            sl = self.dm.slopes[h]
            bsel = h % 2
            wq = self.wq_sb[bsel]
            wdep = [last_wq['w']] if 'w' in last_wq else [self.t_wqkvc]
            tws = []
            for g, off in ((0, h * 64), (1, D + h * 64), (2, 2 * D + h * 64)):
                tws.append(p.dma("qsp",
                                 wq.ap().rearrange("p (c x) -> p c x", x=192)[:, :, g * 64:(g + 1) * 64],
                                 wqkv_r[:, :, off:off + 64],
                                 deps=wdep if g == 0 else []))
            qk_war = [last_qkT] if last_qkT else []
            TCW = min(512, T)
            qkvev = {0: getattr(self, "_t_bce_prev", None), 1: t_pe}
            gi = 0
            qkev = []
            for tc in range(T // TCW):
                bk = 4 + gi % 2
                mm = None
                wdeps = [tws[0], tws[1]] + qk_war
                wdeps.append(self.t_xnbf[min((tc + 1) * (TCW // 128) - 1,
                                             NT - 1)])
                if qkvev[gi % 2] is not None:
                    wdeps.append(qkvev[gi % 2])
                for c in range(ND):
                    mm = p.op("te", lambda e, tc=tc, c=c, wq=wq,
                              TCW=TCW, bk=bk: e.matmul(
                        self.psb[bk][0:128, 0:TCW],
                        wq[:, c * 192:c * 192 + 128],
                        self.xnT_bf[:, c * T + tc * TCW:c * T + (tc + 1) * TCW],
                        start=(c == 0), stop=(c == ND - 1)),
                        deps=(wdeps if c == 0 else []))
                ev1 = p.op("ve", lambda e, tc=tc, TCW=TCW, bk=bk: e.tensor_copy(
                    self.qT[0:64, tc * TCW:(tc + 1) * TCW],
                    self.psb[bk][0:64, 0:TCW]), deps=[mm])
                qkvev[gi % 2] = p.op("ve", lambda e, tc=tc, TCW=TCW,
                                     bk=bk: e.tensor_copy(
                    self.kT[0:64, tc * TCW:(tc + 1) * TCW],
                    self.psb[bk][64:128, 0:TCW]), deps=[ev1])
                qkev.append(qkvev[gi % 2])
                gi += 1
            vev = []
            for tt in range(NT):
                bk = 4 + gi % 2
                mm = None
                wdeps = [tws[2], self.t_xnbf[tt]]
                if qkvev[gi % 2] is not None:
                    wdeps.append(qkvev[gi % 2])
                for c in range(ND):
                    mm = p.op("te", lambda e, tt=tt, c=c, wq=wq, bk=bk: e.matmul(
                        self.psb[bk][0:128, 0:64],
                        self.xnT_bf[:, c * T + tt * 128:c * T + (tt + 1) * 128],
                        wq[:, c * 192 + 128:c * 192 + 192],
                        start=(c == 0), stop=(c == ND - 1)),
                        deps=(wdeps if c == 0 else []))
                qkvev[gi % 2] = p.op("ve", lambda e, tt=tt, bk=bk: e.tensor_copy(
                    self.v_h[:, tt * 65:tt * 65 + 64], self.psb[bk][0:128, 0:64]),
                    deps=[mm, self.t_vones])
                vev.append(qkvev[gi % 2])
                gi += 1
            t_pe = qkvev[(gi - 1) % 2]
            t_qkv = p.op("ve", lambda e: e.engine_nop(),
                         deps=[qkvev[0], qkvev[1]])
            last_wq['w'] = t_qkv
            DTRUNC = int(16.0 / sl) + 1
            for qc in range(NQC):
                q0 = qc * QW
                j_lo = q0 // 128
                j_hi = min(T // 128 - 1, (q0 + QW - 1 + DTRUNC) // 128)
                first = True
                pv_mm = None
                phat_war = getattr(self, "_phat_war", {})

                def emit_S(j):
                    sb_ = j % 2
                    sdeps = [qkev[j // (TCW // 128)], qkev[qc * QW // TCW]]
                    if sb_ in exp_prev:
                        sdeps.append(exp_prev[sb_])
                    return p.op("te", lambda e, j=j, qc=qc, sb_=sb_: e.matmul(
                        self.psb[sb_][0:128, 0:QW],
                        self.kT[0:64, j * 128:(j + 1) * 128],
                        self.qT[0:64, qc * QW:(qc + 1) * QW],
                        start=True, stop=True), deps=sdeps)

                t_s = emit_S(j_lo)
                for j in range(j_lo, j_hi + 1):
                    sb_ = j % 2
                    t_s_cur = t_s
                    t_b = p.op("ve", lambda e, sb_=sb_, sl=sl: e.scalar_tensor_tensor(
                        self.psb[sb_][0:128, 0:QW], self.Tbase[:, 0:QW],
                        -8.0 * sl, self.psb[sb_][0:128, 0:QW], OP.mult, OP.add),
                        deps=[t_s_cur, self.t_Tbase])
                    m = j - q0 // 128
                    if 0 <= m < QW // 128:
                        mo = self.dmask_off[m]
                        t_b = p.op("ve", lambda e, sb_=sb_, mo=mo: e.tensor_tensor(
                            self.psb[sb_][0:128, 0:QW], self.psb[sb_][0:128, 0:QW],
                            self.dmaskG[:, mo:mo + QW], OP.add),
                            deps=[t_b, self.t_dmask[m]])
                    edeps = [t_b, self.t_biascols]
                    if sb_ in phat_war:
                        edeps.append(phat_war[sb_])
                    if pvtok is not None:
                        edeps.append(pvtok)
                    bcol = h * (self.MM + 1) + (j - q0 // 128)
                    t_e = p.op("act", lambda e, sb_=sb_, bcol=bcol: e.activation(
                        out=self.phat[sb_][:, 0:QW], in_=self.psb[sb_][0:128, 0:QW],
                        func=AF.Exp, bias=self.biascols[:, bcol:bcol + 1],
                        scale=0.125), deps=edeps)
                    exp_prev[sb_] = t_e
                    if j < j_hi:
                        t_s = emit_S(j + 1)
                    pv_mm = p.op("te", lambda e, j=j, sb_=sb_, first=first,
                                 j_hi=j_hi: e.matmul(
                        self.psb[2][0:65, 0:QW], self.v_h[:, j * 65:(j + 1) * 65],
                        self.phat[sb_][:, 0:QW],
                        start=first, stop=(j == j_hi)),
                        deps=[t_e, vev[j]])
                    phat_war[sb_] = pv_mm
                    first = False
                self._phat_war = phat_war
                pvtok = pv_mm
                t_le = p.op("ve", lambda e: e.tensor_copy(
                    self.l65[0:1, 0:QW], self.psb[2][64:65, 0:QW]),
                    deps=[pv_mm] + ([self._t_lbc_prev]
                                    if getattr(self, "_t_lbc_prev", None) else []))
                t_lr = p.op("act", lambda e: e.activation(
                    out=self.l65[0:1, 0:QW], in_=self.l65[0:1, 0:QW],
                    func=AF.Ln), deps=[t_le])
                t_lr = p.op("act", lambda e: e.activation(
                    out=self.l65[0:1, 0:QW], in_=self.l65[0:1, 0:QW],
                    func=AF.Exp, scale=-1.0), deps=[t_lr])
                bc_deps = [t_lr, self.t_o65]
                if getattr(self, "_t_bce_prev", None):
                    bc_deps.append(self._t_bce_prev)
                t_bc = p.op("te", lambda e: e.matmul(
                    self.psb[4][0:128, 0:QW], self.ones65[0:1, :],
                    self.l65[0:1, 0:QW], start=True, stop=True),
                    deps=bc_deps)
                self._t_lbc_prev = t_bc
                t_bce = p.op("ve", lambda e: e.tensor_copy(
                    self.bcst[:, 0:QW], self.psb[4][0:128, 0:QW]), deps=[t_bc])
                self._t_bce_prev = t_bce
                spb = self.OT_BASE + self.moe_perm(h // 2) * T
                t_no = p.op("ve", lambda e, h=h, qc=qc, spb=spb: e.tensor_tensor(
                    self.bufB[(h % 2) * 64:(h % 2) * 64 + 64,
                              spb + qc * QW:spb + (qc + 1) * QW],
                    self.psb[2][0:64, 0:QW], self.bcst[0:64, 0:QW], OP.mult),
                    deps=[pv_mm, t_bce])
                pvtok = t_no
                self.t_OT.append(t_no)
            last_qkT = pvtok
            if h == self.PARTB_AT:
                self.emit_routing_b()
            if h == self.PARTC_AT:
                self.emit_routing_c()
            if h >= self.SCAT_START:
                self.emit_scatter_groups(gph)
            if h in self.MOE_AT:
                for k in self.MOE_AT[h]:
                    htok, ytok = self.emit_moe_chunk(k, exp_prev, pvtok)
                    exp_prev[0] = htok
                    exp_prev[1] = htok
                    pvtok = ytok
                    last_qkT = ytok
            if h == self.AG_AT:
                self.emit_moe_ag()
        if self.want("OT"):
            ot = self.eout("dbg_OT", [128, ND * T], BF16)
            p.dma("qact", ot[:], self.bufB[:, 0:self.dm.ND * self.dm.T], deps=self.t_OT)
        if self.want("l_h"):
            lh = self.eout("dbg_l_h", [1, T], F32)
            p.dma("qact", lh[:], self.l_h[:], deps=self.t_OT)

    def phase_wout_l0(self):
        p, dm = self.p, self.dm
        T, D, H, ND, NT = dm.T, dm.D, dm.H, dm.ND, dm.NT
        NS = H * 64 // 128
        DCW = min(512, D)
        NDC = D // DCW
        t_cast = p.dma("qpool", self.woutbf[:], self.wout_in[:])
        woutbf_r = self.woutbf.ap().rearrange("(s p) n -> p s n", p=128)
        wl = {}
        ev_prev = {}
        self.t_s0 = []
        for dc in range(NDC):
            bsel = 0
            wdep = [t_cast] + ([wl["w"]] if "w" in wl else [])
            if dc == 0 and getattr(self, "_moe", None):
                wdep += [t for t in (self._moe["wl"].get("wo"),) if t]
            t_wo = p.dma("qsp", self.wrhs[bsel][:, 0:NS * DCW],
                         woutbf_r[:, :, dc * DCW:(dc + 1) * DCW], deps=wdep)
            for i in range(NT):
                pb = 2 + (i % 2)
                mm = None
                mdeps = self.t_OT + [t_wo]
                if pb in ev_prev:
                    mdeps.append(ev_prev[pb])
                for s in range(NS):
                    sb_ = self.OT_BASE + self.moe_perm(s) * T
                    mm = p.op("te", lambda e, s=s, i=i, dc=dc, pb=pb, bsel=bsel,
                              sb_=sb_: e.matmul(
                        self.psb[pb][0:128, 0:DCW],
                        self.bufB[:, sb_ + i * 128:sb_ + (i + 1) * 128],
                        self.wrhs[bsel][:, s * DCW:(s + 1) * DCW],
                        start=(s == 0), stop=(s == NS - 1)),
                        deps=(mdeps if s == 0 else []))
                t_ev = p.op("ve", lambda e, i=i, dc=dc, pb=pb: e.tensor_copy(
                    self.xnT_bf[:, i * D + dc * DCW:i * D + (dc + 1) * DCW],
                    self.psb[pb][0:128, 0:DCW]), deps=[mm])
                ev_prev[pb] = t_ev
                self.t_s0.append(t_ev)
            wl["w"] = t_ev
        if self.want("s0"):
            so = self.eout("dbg_s0", [128, NT * D], BF16)
            p.dma("qact", so[:], self.xnT_bf[:], deps=self.t_s0)

    def phase_combine_residual(self):
        p, dm = self.p, self.dm
        T, D, NE, NT, ND, R = dm.T, dm.D, dm.NE, dm.NT, dm.ND, dm.R
        eps = 1e-5
        xs, xn_f = self.xs[0], self.xn_f[0]
        g0f = self.xnT_st
        a4, b4, c4, d4 = self.tsc
        mx, mx2, icol, dcol, pcol, qcol = self.vcol
        war = {}
        for nm_ in ("gA", "gB", "zbf"):
            tkn = getattr(self, "t_xr_war", {}).get(nm_)
            if tkn is not None:
                war[nm_] = tkn
        x1_done = []
        self.t_zT = []
        t_glg = None
        for i in range(NT):
            t_x = p.dma("qsp", xs[:], self.x_in[i * 128:(i + 1) * 128, :],
                        deps=war.get("xs_list", []))
            gz1 = p.op("pool", lambda e: e.memset(self.gA[:], 0.0),
                       deps=[war.get("gA")] if "gA" in war else [])
            gz2 = p.op("pool", lambda e: e.memset(self.gB[:], 0.0),
                       deps=[war.get("gB")] if "gB" in war else [])
            tg1 = p.indirect(self.gA[:], None, self.cc_y_out[:],
                             bass.IndirectOffsetOnAxis(self.flat1[:, i:i + 1], 0),
                             8 * dm.CAP - 1,
                             deps=[self.t_ag_y, self.t_flats[i], gz1])
            tg2 = p.indirect(self.gB[:], None, self.cc_y_out[:],
                             bass.IndirectOffsetOnAxis(self.flat2[:, i:i + 1], 0),
                             8 * dm.CAP - 1, deps=[gz2, tg1])
            t_c1 = p.op("ve", lambda e, i=i: e.tensor_scalar(
                xn_f[:], self.gB[:], self.p2c[:, i:i + 1], None, OP.mult),
                deps=[tg2, war.get("xn_f"), war.get("gB2")])
            t_c2 = p.op("ve", lambda e, i=i: e.tensor_scalar(
                g0f[:], self.gA[:], self.p1c[:, i:i + 1], None, OP.mult),
                deps=[tg1, tg2, war.get("g0f")])
            t_c3 = p.op("ve", lambda e: e.tensor_tensor(
                g0f[:], g0f[:], xn_f[:], OP.add), deps=[t_c1, t_c2])
            t_g0 = p.op("act", lambda e: e.activation(
                out=g0f[:], in_=g0f[:], func=AF.Silu), deps=[t_c3])
            war["gA"] = t_c2
            war["gB"] = t_c1
            war["gB2"] = t_c3
            t_m1 = p.op("ve", lambda e, i=i: e.tensor_tensor(
                g0f[:], self.xnT_bf[:, i * D:(i + 1) * D], g0f[:], OP.mult),
                deps=[t_g0] + self.t_s0[-1:])
            t_x1 = p.op("ve", lambda e: e.tensor_tensor(
                xs[:], xs[:], g0f[:], OP.add), deps=[t_m1, t_x])
            war["g0f"] = t_x1
            t_xd = p.dma("qact", self.x1_dram[i * 128:(i + 1) * 128, :], xs[:],
                         deps=[t_x1])
            x1_done.append(t_xd)
            t_s1 = p.op("ve", lambda e: e.reduce_sum(
                self.stat[:, 0:1], xs[:], axis=AX.X), deps=[t_x1])
            t_sq = p.op("act", lambda e: e.activation(
                out=self.zbf[:], in_=xs[:], func=AF.Square,
                accum_out=self.stat[:, 1:2]),
                deps=[t_x1, war.get("zbf")])
            t_m = p.op("ve", lambda e: e.tensor_scalar(
                self.stat[:, 2:3], self.stat[:, 0:1], 1.0 / D, None, OP.mult),
                deps=[t_s1])
            t_ex2 = p.op("ve", lambda e: e.tensor_scalar(
                self.stat[:, 3:4], self.stat[:, 1:2], 1.0 / D, eps, OP.mult, OP.add),
                deps=[t_sq])
            t_m2 = p.op("ve", lambda e: e.tensor_tensor(
                self.stat[:, 4:5], self.stat[:, 2:3], self.stat[:, 2:3], OP.mult),
                deps=[t_m])
            t_var = p.op("ve", lambda e: e.tensor_tensor(
                self.stat[:, 5:6], self.stat[:, 3:4], self.stat[:, 4:5],
                OP.subtract), deps=[t_ex2, t_m2])
            t_sd = p.op("act", lambda e: e.activation(
                out=self.stat[:, 6:7], in_=self.stat[:, 5:6], func=AF.Ln),
                deps=[t_var])
            t_rstd = p.op("act", lambda e: e.activation(
                out=self.stat[:, 5:6], in_=self.stat[:, 6:7], func=AF.Exp,
                scale=-0.5), deps=[t_sd])
            t_nmr = p.op("ve", lambda e: e.scalar_tensor_tensor(
                self.stat[:, 4:5], self.stat[:, 2:3], -1.0, self.stat[:, 5:6],
                OP.mult, OP.mult), deps=[t_rstd, t_m])
            t_xn1 = p.op("ve", lambda e: e.tensor_scalar(
                xn_f[:], xs[:], self.stat[:, 5:6], self.stat[:, 4:5],
                OP.mult, OP.add), deps=[t_nmr, t_x1])
            war["xn_f"] = t_xn1
            war["xs_list"] = [t_xn1, t_xd]
            t_xb = p.op("ve", lambda e: e.tensor_copy(self.zbf[:], xn_f[:]),
                        deps=[t_xn1, t_sq])
            t_xbd = p.dma("qact", self.xn1_dram[i * 128:(i + 1) * 128, :],
                          self.zbf[:], deps=[t_xb])
            self.t_xn1w = getattr(self, "t_xn1w", [])
            self.t_xn1w.append(t_xbd)
            psev = self._psS_ev
            t_gmm = None
            for c in range(ND):
                sl_ = c % 2
                tt = p.op("te", lambda e, c=c, sl_=sl_: e.transpose(
                    self.psS[:, sl_ * 128:(sl_ + 1) * 128],
                    xn_f[:, c * 128:(c + 1) * 128],
                    self.identF[:]), deps=[t_xn1] +
                    ([psev[sl_]] if sl_ in psev else []))
                tev = p.op("ve", lambda e, c=c, sl_=sl_: e.tensor_copy(
                    g0f[:, c * 128:(c + 1) * 128],
                    self.psS[:, sl_ * 128:(sl_ + 1) * 128]),
                    deps=[tt, t_m1])
                psev[sl_] = tev
                gdeps = [tev, self.t_gwT]
                if c == 0 and t_glg is not None:
                    gdeps.append(t_glg)
                t_gmm = p.op("te", lambda e, c=c: e.matmul(
                    self.psb[4][0:128, 0:NE], g0f[:, c * 128:(c + 1) * 128],
                    self.gwT[:, c * NE:(c + 1) * NE],
                    start=(c == 0), stop=(c == ND - 1)), deps=gdeps)
            self._t_psS_last = tev
            self._psS_ev = psev
            war["g0f"] = tev
            t_lg1 = p.op("ve", lambda e: e.tensor_copy(
                a4[:], self.psb[4][0:128, 0:NE]), deps=[t_gmm, t_glg])
            t_glg = t_lg1
            self.t_gate_last = t_lg1
            t1 = p.op("ve", lambda e: e.reduce_max(mx[:], a4[:], axis=AX.X),
                      deps=[t_lg1])
            tneg = p.op("ve", lambda e: e.tensor_scalar(
                qcol[:], mx[:], -1.0, None, OP.mult), deps=[t1])
            t_ee = p.op("act", lambda e: e.activation(
                out=b4[:], in_=a4[:], func=AF.Exp, bias=qcol[:]), deps=[tneg])
            t_ss = p.op("ve", lambda e: e.reduce_sum(pcol[:], b4[:], axis=AX.X),
                        deps=[t_ee])
            t_rr = p.op("ve", lambda e: e.reciprocal(pcol[:], pcol[:]), deps=[t_ss])
            t_sc = p.op("ve", lambda e: e.tensor_scalar(
                b4[:], b4[:], pcol[:], None, OP.mult), deps=[t_rr])
            tm1_ = p.op("ve", lambda e: e.reduce_max(mx[:], b4[:], axis=AX.X),
                        deps=[t_sc])
            teq = p.op("ve", lambda e: e.tensor_scalar(
                c4[:], b4[:], mx[:], None, OP.is_equal), deps=[tm1_])
            trv = p.op("ve", lambda e: e.scalar_tensor_tensor(
                d4[:], self.iotaE[:], -1.0, c4[:], OP.mult, OP.mult),
                deps=[teq, self.t_iotaE])
            trv2 = p.op("ve", lambda e: e.scalar_tensor_tensor(
                d4[:], c4[:], float(NE), d4[:], OP.mult, OP.add), deps=[trv])
            tmx = p.op("ve", lambda e: e.reduce_max(icol[:], d4[:], axis=AX.X),
                       deps=[trv2])
            ti1 = p.op("ve", lambda e: e.tensor_scalar(
                icol[:], icol[:], -1.0, float(NE), OP.mult, OP.add), deps=[tmx])
            to1 = p.op("ve", lambda e: e.tensor_scalar(
                c4[:], self.iotaE[:], icol[:], None, OP.is_equal), deps=[ti1])
            tml = p.op("ve", lambda e: e.scalar_tensor_tensor(
                d4[:], c4[:], float(NEGBIG), b4[:], OP.mult, OP.add), deps=[to1])
            t2_ = p.op("ve", lambda e: e.reduce_max(mx2[:], d4[:], axis=AX.X),
                       deps=[tml])
            teq2 = p.op("ve", lambda e: e.tensor_scalar(
                d4[:], d4[:], mx2[:], None, OP.is_equal), deps=[t2_])
            trv3 = p.op("ve", lambda e: e.scalar_tensor_tensor(
                a4[:], self.iotaE[:], -1.0, d4[:], OP.mult, OP.mult), deps=[teq2])
            trv4 = p.op("ve", lambda e: e.scalar_tensor_tensor(
                a4[:], d4[:], float(NE), a4[:], OP.mult, OP.add), deps=[trv3])
            tmx2 = p.op("ve", lambda e: e.reduce_max(dcol[:], a4[:], axis=AX.X),
                        deps=[trv4])
            ti2 = p.op("ve", lambda e: e.tensor_scalar(
                dcol[:], dcol[:], -1.0, float(NE), OP.mult, OP.add), deps=[tmx2])
            to2 = p.op("ve", lambda e: e.tensor_scalar(
                d4[:], self.iotaE[:], dcol[:], None, OP.is_equal), deps=[ti2])
            tw1 = p.op("ve", lambda e: e.tensor_scalar(
                c4[:], c4[:], mx[:], None, OP.mult), deps=[to1, tm1_, tml])
            tw2 = p.op("ve", lambda e: e.tensor_scalar(
                d4[:], d4[:], mx2[:], None, OP.mult), deps=[to2])
            t_ws = p.op("ve", lambda e, i=i: e.tensor_tensor(
                self.wsel1[:, i * NE:(i + 1) * NE], c4[:], d4[:], OP.add),
                deps=[tw1, tw2])
            self.t_wsel = t_ws
            t_r1 = p.op("ve", lambda e: e.tensor_tensor(
                self.stat[:, 0:1], self.stat[:, 5:6], self.stat[:, 5:6],
                OP.mult), deps=[t_xn1])
            t_r2 = p.op("ve", lambda e: e.tensor_scalar(
                self.stat[:, 0:1], self.stat[:, 0:1], -eps, 1.0 + eps,
                OP.mult, OP.add), deps=[t_r1])
            t_sdb = p.op("act", lambda e: e.activation(
                out=self.stat[:, 0:1], in_=self.stat[:, 0:1], func=AF.Ln),
                deps=[t_r2])
            t_rstdb = p.op("act", lambda e: e.activation(
                out=self.stat[:, 0:1], in_=self.stat[:, 0:1], func=AF.Exp,
                scale=-0.5), deps=[t_sdb])
            t_z = p.op("ve", lambda e: e.tensor_scalar(
                self.zbf[:], xn_f[:], self.stat[:, 0:1], None,
                OP.mult), deps=[t_rstdb, t_xbd])
            war["zbf"] = t_z
            tpsT = getattr(self, "_t_psT_last", None)
            for c in range(ND):
                half = c % 2
                tt = p.op("te", lambda e, c=c, half=half: e.transpose(
                    self.psT[:, half * 128:(half + 1) * 128],
                    self.zbf[:, c * 128:(c + 1) * 128], self.identB[:]),
                    deps=[t_z, self.t_identB] + ([tpsT] if tpsT else []))
                tpsT = p.op("ve", lambda e, c=c, i=i, half=half: e.tensor_copy(
                    self.bufB[:, c * T + i * 128:c * T + (i + 1) * 128],
                    self.psT[:, half * 128:(half + 1) * 128]), deps=[tt])
                self.t_zT.append(tpsT)
            self._t_psT_last = tpsT
            war["zbf"] = tpsT
        self.t_x1 = x1_done
        if self.want("x1"):
            xo = self.eout("dbg_x1", [T, D], F32)
            p.dma("qpool", xo[:], self.x1_dram[:], deps=x1_done)
        if self.want("zT"):
            zo = self.eout("dbg_zT", [128, ND * T], BF16)
            tzd = p.dma("qact", zo[:], self.bufB[:, 0:self.dm.ND * self.dm.T],
                        deps=self.t_zT)
            self.t_zT.append(tzd)
        if self.want("wsel1"):
            wo = self.eout("dbg_wsel1", [128, NT * NE], F32)
            p.dma("qact", wo[:], self.wsel1[:], deps=[self.t_wsel])

    def phase_attn_l1(self):
        p, dm = self.p, self.dm
        T, D, H, ND, NT, W = dm.T, dm.D, dm.H, dm.ND, dm.NT, dm.W
        NW = dm.NW
        QW = self.QW
        t_bb = p.op("ve", lambda e: e.tensor_copy(self.bcols_b[:], self.bcols[:]),
                    deps=self.t_bcols)
        t_row = None
        brow_all = []
        CW = min(512, D)
        for ch in range(3 * D // CW):
            mm = None
            for c in range(ND):
                t_wl = p.dma("qsp", self.wrhs[0][:, 0:CW],
                             self.wqkv1bf[c * 128:(c + 1) * 128,
                                          ch * CW:(ch + 1) * CW],
                             deps=[self.t_wq1c] + ([mm] if mm else
                                                   ([t_row] if t_row else [])))
                mm = p.op("te", lambda e, c=c, ch=ch, CW=CW: e.matmul(
                    self.psb[4][0:1, 0:CW], self.bcols_b[:, c:c + 1],
                    self.wrhs[0][:, 0:CW], start=(c == 0), stop=(c == ND - 1)),
                    deps=[t_wl, t_bb] + ([t_row] if t_row and c == 0 else []))
            t_rowc = p.op("ve", lambda e, CW=CW: e.tensor_copy(
                self.phat[0][0:1, 0:CW], self.psb[4][0:1, 0:CW]),
                deps=[mm] + ([t_row] if t_row else []))
            t_row = p.dma("qact", self.brow_dram[0:1, ch * CW:(ch + 1) * CW],
                          self.phat[0][0:1, 0:CW], deps=[t_rowc])
            brow_all.append(t_row)
        t_brow = t_row
        self.t_OT1 = []
        self._l_war1 = self.t_s0[-1]
        exp_prev = {}
        last_wq = {}
        last_qkT = None
        t_pe = None
        pvtok = None
        wqkv1_r = self.wqkv1bf.ap().rearrange("(c p) n -> p c n", p=128)
        for h in range(H):
            bsel = h % 2
            wq = self.wq_sb[bsel]
            wdep = [last_wq['w']] if 'w' in last_wq else []
            tws = []
            for g, off in ((0, h * 64), (1, D + h * 64), (2, 2 * D + h * 64)):
                tws.append(p.dma("qsp",
                                 wq.ap().rearrange("p (c x) -> p c x", x=192)[:, :, g * 64:(g + 1) * 64],
                                 wqkv1_r[:, :, off:off + 64],
                                 deps=wdep if g == 0 else []))
            t_bhs = []
            for g, off in ((0, h * 64), (1, D + h * 64), (2, 2 * D + h * 64)):
                t_bhs.append(p.dma("qsp", self.brow_h[0:1, g * 64:(g + 1) * 64],
                             self.brow_dram[0:1, off:off + 64],
                             deps=brow_all + wdep if g == 0 else []))
            t_bh = t_bhs
            t_gs = None
            for c in range(ND):
                t_gs = p.op("ve", lambda e, c=c, wq=wq: e.tensor_scalar(
                    wq[:, c * 192:(c + 1) * 192], wq[:, c * 192:(c + 1) * 192],
                    self.gcols[:, c:c + 1], None, OP.mult),
                    deps=[tws[0], tws[1], tws[2]] + self.t_gcols)
            qk_war = [last_qkT] if last_qkT else []
            TCW = min(512, T)
            qkvev = {0: (self.t_gate_last if h == 0
                         else getattr(self, "_t_bce_prev", None)), 1: t_pe}
            gi = 0
            for tc in range(T // TCW):
                bk = 4 + gi % 2
                mm = None
                wdeps = [t_gs] + t_bh + qk_war + self.t_zT[-1:]
                if qkvev[gi % 2] is not None:
                    wdeps.append(qkvev[gi % 2])
                for c in range(ND):
                    mm = p.op("te", lambda e, tc=tc, c=c, wq=wq,
                              TCW=TCW, bk=bk: e.matmul(
                        self.psb[bk][0:128, 0:TCW],
                        wq[:, c * 192:c * 192 + 128],
                        self.bufB[:, c * T + tc * TCW:c * T + (tc + 1) * TCW],
                        start=(c == 0), stop=False),
                        deps=(wdeps if c == 0 else []))
                mm = p.op("te", lambda e, tc=tc, TCW=TCW, bk=bk: e.matmul(
                    self.psb[bk][0:128, 0:TCW],
                    self.brow_h[0:1, 0:128],
                    self.ones_row_b[0:1, 0:TCW], start=False, stop=True),
                    deps=[self.t_orow] + t_bh)
                ev1 = p.op("ve", lambda e, tc=tc, TCW=TCW, bk=bk: e.tensor_copy(
                    self.qT[0:64, tc * TCW:(tc + 1) * TCW],
                    self.psb[bk][0:64, 0:TCW]), deps=[mm])
                qkvev[gi % 2] = p.op("ve", lambda e, tc=tc, TCW=TCW,
                                     bk=bk: e.tensor_copy(
                    self.kT[0:64, tc * TCW:(tc + 1) * TCW],
                    self.psb[bk][64:128, 0:TCW]), deps=[ev1])
                gi += 1
            for tt in range(NT):
                bk = 4 + gi % 2
                mm = None
                wdeps = [t_gs]
                if qkvev[gi % 2] is not None:
                    wdeps.append(qkvev[gi % 2])
                for c in range(ND):
                    mm = p.op("te", lambda e, tt=tt, c=c, wq=wq, bk=bk: e.matmul(
                        self.psb[bk][0:128, 0:64],
                        self.bufB[:, c * T + tt * 128:c * T + (tt + 1) * 128],
                        wq[:, c * 192 + 128:c * 192 + 192],
                        start=(c == 0), stop=False),
                        deps=(wdeps if c == 0 else []))
                mm = p.op("te", lambda e, tt=tt, bk=bk: e.matmul(
                    self.psb[bk][0:128, 0:64], self.onesK1_b[:],
                    self.brow_h[0:1, 128:192],
                    start=False, stop=True), deps=[self.t_onesK1b] + t_bh)
                qkvev[gi % 2] = p.op("ve", lambda e, tt=tt, bk=bk: e.tensor_copy(
                    self.v_h[:, tt * 65:tt * 65 + 64], self.psb[bk][0:128, 0:64]),
                    deps=[mm, self.t_vones])
                gi += 1
            t_qkv = p.op("ve", lambda e: e.engine_nop(),
                         deps=[qkvev[0], qkvev[1]])
            last_wq['w'] = t_qkv
            if h == 0 and self.want("qkT1"):            t_qkv = t_pe
            last_wq['w'] = t_qkv
            if h == 0 and self.want("qkT1"):
                qo = self.eout("dbg_q1", [64, T], BF16)
                ko = self.eout("dbg_k1", [64, T], BF16)
                vo = self.eout("dbg_v1", [128, NT * 65], BF16)
                p.dma("qact", qo[:], self.qT[0:64, 0:T], deps=[t_qkv])
                p.dma("qact", ko[:], self.kT[0:64, 0:T], deps=[t_qkv])
                t_qkv = p.dma("qact", vo[:], self.v_h[:], deps=[t_qkv])
                last_wq['w'] = t_qkv
            phat_war = getattr(self, "_phat_war1", {})
            WPC = QW // dm.W
            for w in range(NW):
                sb_ = w % 2
                wid = 256 if w > 0 else 128
                k0 = (w - 1) * 128 if w > 0 else 0
                sdeps = [t_qkv]
                if sb_ in exp_prev:
                    sdeps.append(exp_prev[sb_])
                mmS = None
                for kk in range(wid // 128):
                    mmS = p.op("te", lambda e, w=w, k0=k0, kk=kk, sb_=sb_: e.matmul(
                        self.psb[sb_][0:128, kk * 128:(kk + 1) * 128],
                        self.kT[0:64, k0 + kk * 128:k0 + (kk + 1) * 128],
                        self.qT[0:64, w * 128:(w + 1) * 128],
                        start=True, stop=True),
                        deps=(sdeps if kk == 0 else []))
                moff = 0 if w > 0 else 128
                t_b = p.op("ve", lambda e, sb_=sb_, wid=wid, moff=moff: e.tensor_tensor(
                    self.psb[sb_][0:128, 0:wid], self.psb[sb_][0:128, 0:wid],
                    self.lmask[:, moff:moff + wid], OP.add),
                    deps=[mmS, self.t_lmask])
                edeps = [t_b, self.t_m16]
                if sb_ in phat_war:
                    edeps.append(phat_war[sb_])
                if pvtok is not None:
                    edeps.append(pvtok)
                t_e = p.op("act", lambda e, sb_=sb_, wid=wid: e.activation(
                    out=self.phat[sb_][:, 0:wid], in_=self.psb[sb_][0:128, 0:wid],
                    func=AF.Exp, bias=self.m16[:], scale=0.125), deps=edeps)
                exp_prev[sb_] = t_e
                pv_mm = None
                l_war = getattr(self, "_l_war1", None)
                for kk in range(wid // 128):
                    jj = (k0 // 128) + kk
                    pv_mm = p.op("te", lambda e, jj=jj, kk=kk, sb_=sb_,
                                 wid=wid: e.matmul(
                        self.psb[2][0:65, 0:128],
                        self.v_h[:, jj * 65:(jj + 1) * 65],
                        self.phat[sb_][:, kk * 128:(kk + 1) * 128],
                        start=(kk == 0), stop=(kk == wid // 128 - 1)),
                        deps=[t_e] + ([l_war] if l_war and kk == 0 else []))
                phat_war[sb_] = pv_mm
                t_le = p.op("ve", lambda e, w=w, WPC=WPC: e.tensor_copy(
                    self.l65[0:1, (w % WPC) * 128:(w % WPC + 1) * 128],
                    self.psb[2][64:65, 0:128]),
                    deps=[pv_mm] + ([self._t_lbc_prev]
                                    if getattr(self, "_t_lbc_prev", None) else []))
                self._l_war1 = t_le
                t_oe = p.op("ve", lambda e, h=h, w=w: e.tensor_copy(
                    self.xnT_bf[(h % 2) * 64:(h % 2) * 64 + 64,
                                (h // 2) * T + w * 128:(h // 2) * T + (w + 1) * 128],
                    self.psb[2][0:64, 0:128]), deps=[pv_mm, t_le])
                pvtok = t_oe
                if (w + 1) % WPC == 0:
                    qc = w // WPC
                    t_lr = p.op("act", lambda e: e.activation(
                        out=self.l65[0:1, 0:QW], in_=self.l65[0:1, 0:QW],
                        func=AF.Ln), deps=[pvtok])
                    t_lr = p.op("act", lambda e: e.activation(
                        out=self.l65[0:1, 0:QW], in_=self.l65[0:1, 0:QW],
                        func=AF.Exp, scale=-1.0), deps=[t_lr])
                    self._l_war1 = t_lr
                    bc_deps = [t_lr, self.t_o65]
                    if getattr(self, "_t_bce_prev", None):
                        bc_deps.append(self._t_bce_prev)
                    t_bc = p.op("te", lambda e: e.matmul(
                        self.psb[4][0:128, 0:QW], self.ones65[0:1, :],
                        self.l65[0:1, 0:QW], start=True, stop=True),
                        deps=bc_deps)
                    self._t_lbc_prev = t_bc
                    t_bce = p.op("ve", lambda e: e.tensor_copy(
                        self.bcst[:, 0:QW], self.psb[4][0:128, 0:QW]),
                        deps=[t_bc])
                    self._t_bce_prev = t_bce
                    hb = (h % 2) * 64
                    t_no = p.op("ve", lambda e, h=h, qc=qc, hb=hb: e.tensor_tensor(
                        self.xnT_bf[hb:hb + 64,
                                    (h // 2) * T + qc * QW:(h // 2) * T + (qc + 1) * QW],
                        self.xnT_bf[hb:hb + 64,
                                    (h // 2) * T + qc * QW:(h // 2) * T + (qc + 1) * QW],
                        self.bcst[hb:hb + 64, 0:QW], OP.mult),
                        deps=[t_bce, pvtok])
                    pvtok = t_no
                    self.t_OT1.append(t_no)
            self._phat_war1 = phat_war
            last_qkT = pvtok
        if self.want("OT1"):            last_qkT = pvtok
        if self.want("OT1"):
            ot = self.eout("dbg_OT1", [128, ND * T], BF16)
            p.dma("qact", ot[:], self.xnT_bf[:], deps=self.t_OT1)

    def phase_wout_l1(self):
        p, dm = self.p, self.dm
        T, D, H, ND, NT = dm.T, dm.D, dm.H, dm.ND, dm.NT
        NS = H * 64 // 128
        DCW = min(512, D)
        NDC = D // DCW
        wout1bf_r = self.wout1bf.ap().rearrange("(s p) n -> p s n", p=128)
        wl = {}
        ev_prev = {}
        self.t_s1 = []
        for dc in range(NDC):
            bsel = 0
            wdep = [self.t_wo1c] + ([wl["w"]] if "w" in wl else [])
            t_wo = p.dma("qsp", self.wrhs[bsel][:, 0:NS * DCW],
                         wout1bf_r[:, :, dc * DCW:(dc + 1) * DCW], deps=wdep)
            for i in range(NT):
                pb = 2 + (i % 2)
                mm = None
                mdeps = self.t_OT1[-1:] + [t_wo]
                if ("y", pb) in ev_prev:
                    mdeps.append(ev_prev[("y", pb)])
                for s in range(NS):
                    mm = p.op("te", lambda e, s=s, i=i, pb=pb, bsel=bsel: e.matmul(
                        self.psb[pb][0:128, 0:DCW],
                        self.xnT_bf[:, s * T + i * 128:s * T + (i + 1) * 128],
                        self.wrhs[bsel][:, s * DCW:(s + 1) * DCW],
                        start=(s == 0), stop=(s == NS - 1)),
                        deps=(mdeps if s == 0 else []))
                t_ev = p.op("ve", lambda e, i=i, dc=dc, pb=pb: e.tensor_copy(
                    self.bufB[:, i * D + dc * DCW:i * D + (dc + 1) * DCW],
                    self.psb[pb][0:128, 0:DCW]), deps=[mm] + self.t_zT[-1:])
                ev_prev[("y", pb)] = t_ev
                self.t_s1.append(t_ev)
            wl["w"] = t_ev
        if self.want("s1"):
            so = self.eout("dbg_s1", [128, NT * D], BF16)
            p.dma("qact", so[:], self.bufB[:, 0:self.dm.ND * self.dm.T], deps=self.t_s1)

    def phase_final(self):
        p, dm = self.p, self.dm
        T, D, NE, NT, ND, R = dm.T, dm.D, dm.NE, dm.NT, dm.ND, dm.R
        DCW = min(512, D)
        NDC = D // DCW
        xs, xn_f = self.xs[0], self.xn_f[0]
        g0f = self.xnT_st
        war = {}
        for dc in range(NDC):
            t_blin = p.dma("qpool", self.blin_sb[0:1, 0:DCW],
                           self.blin_in[0:1, dc * DCW:(dc + 1) * DCW],
                           deps=war.get("blin", []))
            t_right = p.dma("qpool", self.right_sb[0:32, 0:DCW],
                            self.right_in[0:32, dc * DCW:(dc + 1) * DCW],
                            deps=war.get("right", []))
            t_wl = []
            tw = None
            for c in range(ND):
                tw = p.dma("qsp", self.wrhs[0][:, c * DCW:(c + 1) * DCW],
                           self.wlinbf[dc * DCW:(dc + 1) * DCW,
                                       c * 128:(c + 1) * 128],
                           deps=([self.t_wlc] + self.t_s1[-1:] +
                                 war.get("wrhs", [])) if c == 0 else [],
                           transpose=True)
                t_wl.append(tw)
            for i in range(NT):
                t_x = p.dma("qsp", xs[:], self.x1_dram[i * 128:(i + 1) * 128, :],
                            deps=war.get("xs", []) + self.t_x1)
                t_z = p.dma("qsp", self.zbf[:],
                            self.xn1_dram[i * 128:(i + 1) * 128, :],
                            deps=war.get("zbf", []) + self.t_xn1w)
                tpsT = getattr(self, "_t_psT_last", None)
                for c in range(ND):
                    half = c % 2
                    tt = p.op("te", lambda e, c=c, half=half: e.transpose(
                        self.psT[:, half * 128:(half + 1) * 128],
                        self.zbf[:, c * 128:(c + 1) * 128], self.identB[:]),
                        deps=[t_z, self.t_identB] + ([tpsT] if tpsT else []))
                    tpsT = p.op("ve", lambda e, c=c, half=half: e.tensor_copy(
                        self.xt1[:, c * 128:(c + 1) * 128],
                        self.psT[:, half * 128:(half + 1) * 128]),
                        deps=[tt] + war.get("xt1", self.t_zT[-1:]))
                self._t_psT_last = tpsT
                war["zbf"] = [tpsT]
                t_xt = tpsT
                tps = list(self._psS_ev.values())
                tt = p.op("te", lambda e, i=i: e.transpose(
                    self.psS[0:8, 0:128], self.wsel1[:, i * NE:(i + 1) * NE],
                    self.identF[:]), deps=[self.t_wsel] + tps)
                t_wsT = p.op("ve", lambda e: e.tensor_copy(
                    self.wselT[:], self.psS[0:8, 0:128]),
                    deps=[tt] + war.get("wselT", []))
                self._psS_ev = {0: t_wsT}
                t_we_mm = p.op("te", lambda e: e.matmul(
                    self.psb[4][0:32, 0:128], self.E832[:], self.wselT[:],
                    start=True, stop=True),
                    deps=[t_wsT, self.t_E832] + war.get("psb4", []))
                t_we = p.op("ve", lambda e: e.tensor_copy(
                    self.wexpT[:], self.psb[4][0:32, 0:128]),
                    deps=[t_we_mm] + war.get("wexpT", []))
                war["psb4"] = [t_we]
                mmt = None
                for c in range(ND):
                    mmt = p.op("te", lambda e, c=c: e.matmul(
                        self.psb[5][0:32, 0:128],
                        self.left_sb[:, c * 32:(c + 1) * 32],
                        self.xt1[:, c * 128:(c + 1) * 128],
                        start=(c == 0), stop=(c == ND - 1)),
                        deps=([t_xt, self.t_left] + war.get("psb5", [])
                              if c == 0 else []))
                t_t2 = p.op("ve", lambda e: e.tensor_tensor(
                    self.t2_sb[:], self.psb[5][0:32, 0:128], self.wexpT[:],
                    OP.mult), deps=[mmt, t_we] + war.get("t2", []))
                war["psb5"] = [t_t2]
                war["wexpT"] = [t_t2]
                war["wselT"] = [t_t2]
                pb = 2 + (i % 2)
                mm = None
                bdeps = t_wl + [t_xt] + war.get(("o", pb), [])
                for c in range(ND):
                    mm = p.op("te", lambda e, c=c, pb=pb, DCW=DCW: e.matmul(
                        self.psb[pb][0:128, 0:DCW],
                        self.xt1[:, c * 128:(c + 1) * 128],
                        self.wrhs[0][:, c * DCW:(c + 1) * DCW],
                        start=(c == 0), stop=False),
                        deps=(bdeps if c == 0 else []))
                mm = p.op("te", lambda e, pb=pb, DCW=DCW: e.matmul(
                    self.psb[pb][0:128, 0:DCW], self.onesK1_b[:],
                    self.blin_sb[0:1, 0:DCW],
                    start=False, stop=False), deps=[t_blin, self.t_onesK1b])
                mm = p.op("te", lambda e, pb=pb, DCW=DCW: e.matmul(
                    self.psb[pb][0:128, 0:DCW], self.t2_sb[:],
                    self.right_sb[0:32, 0:DCW],
                    start=False, stop=True), deps=[t_t2, t_right])
                war["blin"] = [mm]
                war["right"] = [mm]
                war["t2"] = [mm]
                t_g1 = p.op("act", lambda e, pb=pb, DCW=DCW: e.activation(
                    out=g0f[:, 0:DCW],
                    in_=self.psb[pb][0:128, 0:DCW], func=AF.Silu),
                    deps=[mm] + war.get("g0f", []))
                t_h1 = p.op("ve", lambda e, i=i, dc=dc, DCW=DCW: e.tensor_tensor(
                    g0f[:, 0:DCW],
                    self.bufB[:, i * D + dc * DCW:i * D + (dc + 1) * DCW],
                    g0f[:, 0:DCW], OP.mult),
                    deps=[t_g1] + self.t_s1[-1:])
                war[("o", pb)] = [t_h1]
                t_o = p.op("ve", lambda e, dc=dc, DCW=DCW: e.tensor_tensor(
                    xs[:, dc * DCW:(dc + 1) * DCW],
                    xs[:, dc * DCW:(dc + 1) * DCW],
                    g0f[:, 0:DCW], OP.add), deps=[t_h1, t_x])
                war["g0f"] = [t_o]
                t_od = p.dma("qact",
                             self.out[i * 128:(i + 1) * 128,
                                      dc * DCW:(dc + 1) * DCW],
                             xs[:, dc * DCW:(dc + 1) * DCW], deps=[t_o])
                war["xs"] = [t_od, t_o]
            war["wrhs"] = [mm]



def make_inputs(dm, fi):
    f32 = lambda a: np.ascontiguousarray(a, dtype=np.float32)
    x = f32(fi["L0_attn_x"])
    shared = {
        "rw": f32(fi["L0_ffn_router_w"]),
        "rb": f32(fi["L0_ffn_router_b"]).reshape(1, -1),
        "wqkv": f32(fi["L0_attn_W_qkv"]),
        "wout": f32(fi["L0_attn_W_out"]),
        "l1g": f32(fi["L1_attn_ln_gamma"]).reshape(1, -1),
        "l1b": f32(fi["L1_attn_ln_beta"]).reshape(1, -1),
        "wqkv1": f32(fi["L1_attn_w_qkv"]),
        "wout1": f32(fi["L1_attn_w_out"]),
        "gw": f32(fi["L1_ffn_gate_w"]),
        "wlin": f32(fi["L1_ffn_W_lin"]),
        "blin": f32(fi["L1_ffn_b_lin"]).reshape(1, -1),
        "left_all": f32(np.transpose(fi["L1_ffn_left"], (1, 0, 2)).reshape(dm.D, -1)),
        "right_all": f32(np.asarray(fi["L1_ffn_right"]).reshape(dm.NE * dm.R, dm.D)),
    }
    W = f32(fi["L0_ffn_W"])
    V = f32(fi["L0_ffn_V"])
    WO = f32(fi["L0_ffn_W_out"])
    in_maps = []
    for c in range(8):
        esel = np.zeros((128, 8), np.float32)
        for r in range(8):
            esel[16 * r + c, r] = 1.0
        rmask = np.zeros((8, 1), np.float32)
        rmask[:c] = 1.0
        kq = dm.T // dm.Q
        pp = np.arange(128)
        slmidx = ((16 * (pp // 16) + c) * kq + pp % kq).astype(np.float32)
        m = dict(shared)
        m.update({"x": x[c], "rank_mask": rmask, "esel": esel,
                  "slmidx": slmidx.reshape(128, 1),
                  "mw": W[c], "mv": V[c], "mwo": WO[c]})
        in_maps.append(m)
    return in_maps


def kernel(**inputs):
    dm = Dims()
    b = Builder(dm)
    nc = b.build()
    in_maps = make_inputs(dm, inputs)
    res = run_bass_kernel_spmd(nc, in_maps, core_ids=list(range(8)))
    return np.stack([r["out"] for r in res.results], axis=0)


# version 20
# speedup vs baseline: 1.2579x; 1.0548x over previous
import math
from contextlib import ExitStack

import numpy as np
import concourse.bass as bass
import concourse.mybir as mybir
from concourse.bass_utils import run_bass_kernel_spmd

F32 = mybir.dt.float32
F16 = mybir.dt.float16
BF16 = mybir.dt.bfloat16
I32 = mybir.dt.int32
AF = mybir.ActivationFunctionType
OP = mybir.AluOpType
AX = mybir.AxisListType

ALIBI_SLOPES = [0.7071067811865476 ** (i + 1) for i in range(16)]
C_STAB = 16.0
NEGBIG = -32768.0


class Dims:
    def __init__(self, T=4096, D=1024, H=16, NE=8, CAP=2048, DH=2048, R=4, W=128):
        self.T, self.D, self.H, self.NE, self.CAP, self.DH, self.R, self.W = \
            T, D, H, NE, CAP, DH, R, W
        self.HD = 64
        self.NC = 8
        self.NT = T // 128
        self.ND = D // 128
        self.TG = T * self.NC
        self.Q = self.TG // 128
        self.NW = T // W
        self.slopes = [ALIBI_SLOPES[h * (16 // H)] for h in range(H)]


ENGS = ("te", "ve", "act", "pool", "sp")
QOF = {"qsp": "sp", "qact": "act", "qpool": "pool"}


class Prog:

    NLANES = 6
    QENG = {"qsp": "sp", "qact": "act", "qpool": "pool", "cc": "pool"}

    def __init__(self, nc):
        self.nc = nc
        self.ops = {e: [] for e in ENGS}
        self.count = {s: 0 for s in ("te", "ve", "act", "pool")}
        self.sems = {}
        self.lane = {}
        self.fcount = {}
        self.fcover = {}
        self.rr = {}
        for q in ("qsp", "qact", "qpool", "cc"):
            n = 1 if q == "cc" else self.NLANES
            for l in range(n):
                self.lane[(q, l)] = 0
                self.fcover[(q, l)] = 0
            self.fcount[q] = 0
            self.rr[q] = 0
        self.waited = {e: {} for e in ENGS}

    def make_sems(self, ctx):
        for s in ("te", "ve", "act", "pool"):
            self.sems[s] = ctx.enter_context(self.nc.semaphore("s_" + s))
        for (q, l) in self.lane:
            self.sems[(q, l)] = ctx.enter_context(self.nc.semaphore(f"s_{q}{l}"))
        for q in self.fcount:
            self.sems["f" + q] = ctx.enter_context(self.nc.semaphore("sf_" + q))

    def _wait(self, eng, tok):
        if tok is None:
            return
        sem, val = tok
        if isinstance(sem, tuple):
            q, l = sem
            if self.fcover[(q, l)] < val:
                qeng = self.QENG[q]
                total = self.lane[(q, l)]
                self.ops[qeng].append(("wait", self.sems[(q, l)], total))
                self.waited[qeng][(q, l)] = max(
                    self.waited[qeng].get((q, l), 0), total)
                fs = "f" + q
                self.fcount[fs[1:]] += 1
                self.ops[qeng].append(("inc", self.sems[fs], 1))
                self.fcover[(q, l)] = total
            sem2, val2 = "f" + q, self.fcount[q]
            if self.waited[eng].get(sem2, 0) >= val2:
                return
            self.waited[eng][sem2] = val2
            self.ops[eng].append(("wait", self.sems[sem2], val2))
            return
        if val <= 0 or self.waited[eng].get(sem, 0) >= val:
            return
        self.waited[eng][sem] = val
        self.ops[eng].append(("wait", self.sems[sem], val))

    def fence_queue(self, q):
        qeng = self.QENG[q]
        made = False
        for l in range(1 if q == "cc" else self.NLANES):
            total = self.lane[(q, l)]
            if self.fcover[(q, l)] < total:
                self.ops[qeng].append(("wait", self.sems[(q, l)], total))
                self.waited[qeng][(q, l)] = max(
                    self.waited[qeng].get((q, l), 0), total)
                self.fcover[(q, l)] = total
                made = True
        if made:
            self.fcount[q] += 1
            self.ops[qeng].append(("inc", self.sems["f" + q], 1))

    def op(self, eng, fn, deps=()):
        for d in deps:
            self._wait(eng, d)
        self.count[eng] += 1
        self.ops[eng].append(("op", fn, self.sems[eng]))
        return (eng, self.count[eng])

    def _lane_pick(self, q):
        l = self.rr[q] % (1 if q == "cc" else self.NLANES)
        self.rr[q] += 1
        return l

    def dma(self, queue, out, in_, deps=(), **kw):
        eng = self.QENG[queue]
        for d in deps:
            self._wait(eng, d)
        l = self._lane_pick(queue)
        self.lane[(queue, l)] += 16
        self.ops[eng].append(("dma", (out, in_, kw), self.sems[(queue, l)]))
        return ((queue, l), self.lane[(queue, l)])

    def indirect(self, out, out_off, in_, in_off, bounds, deps=(), compute_op=None):
        for d in deps:
            self._wait("pool", d)
        l = self._lane_pick("qpool")
        self.lane[("qpool", l)] += 16
        kw = {"compute_op": compute_op} if compute_op is not None else {}
        self.ops["pool"].append(
            ("ind", (out, out_off, in_, in_off, bounds, kw),
             self.sems[("qpool", l)]))
        return (("qpool", l), self.lane[("qpool", l)])

    def collective(self, kind, op, ins, outs, deps=()):
        for d in deps:
            self._wait("pool", d)
        self.lane[("cc", 0)] += 1
        self.ops["pool"].append(("cc", (kind, op, ins, outs),
                                 self.sems[("cc", 0)]))
        return (("cc", 0), self.lane[("cc", 0)])

    def finish(self, block):
        def ap(x):
            return x.ap() if not isinstance(x, bass.AP) and hasattr(x, "ap") else x

        def run(engine, stream):
            for n, item in enumerate(stream):
                kind = item[0]
                self._replay_pos = (engine, n, kind)
                if kind == "wait":
                    engine.wait_ge(item[1], item[2])
                elif kind == "inc":
                    engine.sem_inc(item[1], item[2])
                elif kind == "op":
                    item[1](engine).then_inc(item[2], 1)
                elif kind == "dma":
                    (out, in_, kw), sem = item[1], item[2]
                    engine.dma_start(out=ap(out), in_=ap(in_), **kw).then_inc(sem, 16)
                elif kind == "ind":
                    (out, oo, in_, io_, b, kw), sem = item[1], item[2]
                    if b not in self._breg:
                        r = engine.alloc_register(f"bnd{len(self._breg)}")
                        engine.reg_mov(r, b)
                        self._breg[b] = r
                    engine.indirect_dma_start(
                        out=ap(out), out_offset=oo, in_=ap(in_), in_offset=io_,
                        bounds_check=self._breg[b],
                        oob_is_err=False, **kw).then_inc(sem, 16)
                elif kind == "cc":
                    (ck, cop, ins, outs), sem = item[1], item[2]
                    engine.collective_compute(
                        ck, cop, replica_groups=[list(range(8))],
                        ins=ins, outs=outs).then_inc(sem, 1)

        @block.tensor
        def _(tensor):
            run(tensor, self.ops["te"])

        @block.vector
        def _(vector):
            run(vector, self.ops["ve"])

        @block.scalar
        def _(scalar):
            run(scalar, self.ops["act"])
            for l in range(self.NLANES):
                if self.lane[("qact", l)]:
                    scalar.wait_ge(self.sems[("qact", l)], self.lane[("qact", l)])

        @block.gpsimd
        def _(gpsimd):
            self._breg = {}
            run(gpsimd, self.ops["pool"])
            for l in range(self.NLANES):
                if self.lane[("qpool", l)]:
                    gpsimd.wait_ge(self.sems[("qpool", l)], self.lane[("qpool", l)])
            if self.lane[("cc", 0)]:
                gpsimd.wait_ge(self.sems[("cc", 0)], self.lane[("cc", 0)])

        @block.sync
        def _(sync):
            run(sync, self.ops["sp"])
            for l in range(self.NLANES):
                if self.lane[("qsp", l)]:
                    sync.wait_ge(self.sems[("qsp", l)], self.lane[("qsp", l)])


class Builder:
    def __init__(self, dm: Dims, debug=()):
        self.dm = dm
        self.debug = set(debug)
        self.nc = bass.Bass("TRN2", target_bir_lowering=False)
        self.ctx = ExitStack()
        self.p = Prog(self.nc)
        self.io = {}
        self.dbg_names = []

    def ein(self, name, shape, dt=F32):
        t = self.nc.dram_tensor(name, shape, dt, kind="ExternalInput")
        self.io[name] = t
        return t

    def eout(self, name, shape, dt=F32):
        t = self.nc.dram_tensor(name, shape, dt, kind="ExternalOutput")
        self.io[name] = t
        return t

    def idram(self, name, shape, dt):
        t = self.nc.dram_tensor(name, shape, dt)
        self.io[name] = t
        return t

    def sb(self, name, shape, dt):
        return self.ctx.enter_context(self.nc.sbuf_tensor(name, shape, dt))

    def ps(self, name, shape, dt=F32):
        return self.ctx.enter_context(self.nc.psum_tensor(name, shape, dt))

    def want(self, name):
        return name in self.debug

    def dump(self, name, shape, dt, src_fn, deps):
        if not self.want(name):
            return None
        ap = self.eout("dbg_" + name, shape, dt)
        self.dbg_names.append("dbg_" + name)
        return self.p.dma("qact", ap, src_fn(), deps=deps)

    def build(self, upto="all"):
        dm = self.dm
        p = self.p
        with self.ctx:
            p.make_sems(self.ctx)
            self.declare_io()
            self.alloc_consts()
            NSQ = dm.CAP // min(512, dm.CAP)
            H = dm.H
            self.PARTB_AT = min(3, H - 2)
            self.PARTC_AT = min(5, H - 1)
            self.SCAT_START = self.PARTB_AT + 1
            self.MOE_AT = {min(7, H - 1): list(range(NSQ))}
            self.AG_AT = min(9, H - 1)
            with self.nc.Block() as block:
                self.emit_consts()
                self.phase_ln_router()
                self.phase_routing()
                if upto in ("p1", "p2"):
                    self.emit_routing_b()
                    self.emit_routing_c()
                if upto != "p1":
                    self.phase_moe_scatter()
                if upto == "p2":
                    self.emit_scatter_groups(10 ** 6)
                    ep = {}
                    pv = None
                    for k in range(NSQ):
                        ht, yt = self.emit_moe_chunk(k, ep, pv)
                        ep = {0: ht, 1: ht}
                        pv = yt
                    self.emit_moe_ag()
                if upto not in ("p1", "p2"):
                    self.phase_attn_l0()
                    self.phase_wout_l0()
                    self.phase_combine_residual()
                    self.phase_attn_l1()
                    self.phase_wout_l1()
                    self.phase_final()
                self.p.finish(block)
        return self.nc

    def declare_io(self):
        dm = self.dm
        T, D, NE, TG = dm.T, dm.D, dm.NE, dm.TG
        self.x_in = self.ein("x", [T, D])
        self.rw_in = self.ein("rw", [NE, D])
        self.rb_in = self.ein("rb", [1, NE])
        self.rmask_in = self.ein("rank_mask", [8, 1])
        self.esel_in = self.ein("esel", [128, 8])
        self.slmidx_in = self.ein("slmidx", [128, 1])
        self.mw_in = self.ein("mw", [D, dm.DH])
        self.mv_in = self.ein("mv", [D, dm.DH])
        self.mwo_in = self.ein("mwo", [dm.DH, D])
        self.wqkv_in = self.ein("wqkv", [D, 3 * D])
        self.wout_in = self.ein("wout", [D, D])
        self.l1g_in = self.ein("l1g", [1, D])
        self.l1b_in = self.ein("l1b", [1, D])
        self.wqkv1_in = self.ein("wqkv1", [D, 3 * D])
        self.wout1_in = self.ein("wout1", [D, D])
        self.gw_in = self.ein("gw", [NE, D])
        self.wlin_in = self.ein("wlin", [D, D])
        self.blin_in = self.ein("blin", [1, D])
        self.left_in = self.ein("left_all", [D, NE * dm.R])
        self.right_in = self.ein("right_all", [NE * dm.R, D])
        self.out = self.eout("out", [T, D])
        self.cc_xn_in = self.idram("cc_xn_in", [T, D], BF16)
        self.cc_xn_out = self.idram("cc_xn_out", [TG, D], BF16)
        self.cc_tot_in = self.idram("cc_tot_in", [1, 16], F32)
        self.cc_tot_out = self.idram("cc_tot_out", [8, 16], F32)
        self.cc_sl_in = self.idram("cc_sl_in", [16, T], F32)
        self.cc_sl_out = self.idram("cc_sl_out", [128, T], F32)
        self.ind_dram = self.idram("ind_dram", [T, 128], BF16)
        self.slm_dram = self.idram("slm_dram", [1, TG], F32)
        self.offs_dram = self.idram("offs_dram", [1, 16], F32)
        self.woutbf = self.idram("woutbf", [D, D], BF16)
        self.wout1bf = self.idram("wout1bf", [D, D], BF16)
        self.wqkv1bf = self.idram("wqkv1bf", [D, 3 * D], BF16)
        self.wlinbf = self.idram("wlinbf", [D, D], BF16)
        self.brow_dram = self.idram("brow_dram", [1, 3 * D], BF16)
        self.x1_dram = self.idram("x1_dram", [T, D], F32)
        self.xn1_dram = self.idram("xn1_dram", [T, D], BF16)
        self.wbf = self.idram("wbf", [dm.DH, D], BF16)
        self.wqkvbf = self.idram("wqkvbf", [D, 3 * D], BF16)
        self.vbf = self.idram("vbf", [dm.DH, D], BF16)
        self.wobf = self.idram("wobf", [dm.DH, D], BF16)
        self.disp = self.idram("disp", [dm.CAP, D], BF16)
        self.cc_y_in = self.idram("cc_y_in", [dm.CAP, D], BF16)
        self.cc_y_out = self.idram("cc_y_out", [8 * dm.CAP, D], BF16)

    def alloc_consts(self):
        dm = self.dm
        T, D, NE, NT, ND = dm.T, dm.D, dm.NE, dm.NT, dm.ND
        sb = self.sb
        self.identF = sb("identF", [128, 128], F32)
        self.identB = sb("identB", [128, 128], BF16)
        self.iotaE = sb("iotaE", [128, NE], F32)
        self.e2048 = sb("e2048", [128, 16], F32)
        self.onesK1_f = sb("onesK1_f", [1, 128], F32)
        self.onesK1_b = sb("onesK1_b", [1, 128], BF16)
        self.ones_col_b = sb("ones_col_b", [128, 1], BF16)
        self.xnT_bf = sb("xnT_bf", [128, ND * T], BF16)
        self.logits = sb("logits", [128, NT * NE], F32)
        self.p1c = sb("p1c", [128, NT], F32)
        self.p2c = sb("p2c", [128, NT], F32)
        self.oh1 = sb("oh1", [128, NT * NE], BF16)
        self.oh2 = sb("oh2", [128, NT * NE], BF16)
        self.flat1 = sb("flat1", [128, NT], I32)
        self.flat2 = sb("flat2", [128, NT], I32)
        self.xs = [sb("xs0", [128, D], F32)]
        self.xs.append(self.xs[0])
        self.stat = sb("stat", [128, 16], F32)
        self.xn_f = [sb("xn_f0", [128, D], F32)]
        self.xn_f.append(self.xn_f[0])
        self.xnT_st = sb("xnT_st", [128, D], F32)
        self.xn_bf_t = sb("xn_bf_t", [128, D], BF16)
        self.ind_sb = sb("ind_sb", [128, 128], BF16)
        self.rwT = sb("rwT", [128, ND * NE], F32)
        self.rb_sb = sb("rb_sb", [1, NE], F32)
        self.rmask_sb = sb("rmask_sb", [8, 1], F32)
        self.esel_sb = sb("esel_sb", [128, 8], F32)
        self.tsc = [sb(f"tsc{i}", [128, NE], F32) for i in range(4)]
        self.vcol = [sb(f"vcol{i}", [128, 1], F32) for i in range(6)]
        self.totAG = sb("totAG", [8, 16], F32)
        self.offs_c = sb("offs_c", [16, 1], F32)
        self.slots_tok = sb("slots_tok", [128, NT * NE], F32)
        self.myslot_i = sb("myslot_i", [128, dm.Q], I32)
        self.slmidx_sb = sb("slmidx_sb", [128, 1], F32)
        self.slmidx_i = sb("slmidx_i", [128, 1], I32)
        rt = ExitStack()
        self.seqF = rt.enter_context(self.nc.sbuf_tensor("seqF", [16, T], F32))
        self.slchunk = rt.enter_context(
            self.nc.sbuf_tensor("slchunk", [128, max(512, dm.Q)], F32))
        self.myslot_f = self.slchunk
        self.rw_sb = rt.enter_context(
            self.nc.sbuf_tensor("rw_sb", [NE, 128], F32))
        rt.close()
        SQW = min(512, dm.CAP)
        NHT = dm.DH // 128
        DCW = min(512, D)
        carve = {}
        off = 0
        for nm, sz in (("xinT", ND * SQW), ("h_buf", NHT * SQW),
                       ("w0", ND * 128), ("w1", ND * 128),
                       ("v0", ND * 128), ("v1", ND * 128)):
            carve[nm] = off
            off += sz
        self.carve = carve
        NS = dm.H * 64 // 128
        if off <= (NS // 2) * T:
            self.OT_BASE = 0
            self.moe_perm = lambda a, NS=NS: (a + NS // 2) % NS
            bufB_sz = max(ND * T, 4 * T)
        else:
            self.OT_BASE = off
            self.moe_perm = lambda a: a
            bufB_sz = max(off + NS * T, ND * T, 4 * T)
        self.bufB = sb("bufB", [128, bufB_sz], BF16)
        self.qT = sb("qT", [64, T], BF16)
        self.kT = sb("kT", [64, T], BF16)
        self.v_h = sb("v_h", [128, NT * 65], BF16)
        self.l65 = sb("l65", [65, 512], F32)
        self.ones65 = sb("ones65", [65, 128], F32)
        self.srow = self.l65
        self.phat = [sb(f"phat{i}", [128, 512], BF16) for i in range(2)]
        self.hsc = sb("hsc", [128, 512], BF16)
        self.ysb = sb("ysb", [128, 512], BF16)
        MM = T // 128 - 1
        self.MM = MM
        self.Tbase = sb("Tbase", [128, min(512, T)], F16)
        self.biascols = sb("biascols", [128, dm.H * (MM + 1)], F32)
        self.mpcols = sb("mpcols", [128, MM + 1], F32)
        self.m16 = sb("m16", [128, 1], F32)
        QW_ = min(512, T)
        BASEW = 128 * (QW_ // 128 - 1)
        self.dmaskG = sb("dmaskG", [128, QW_ + BASEW], BF16)
        self.dmask_off = [BASEW - 128 * m for m in range(QW_ // 128)]
        self.BASEW = BASEW
        self.bcst = sb("bcst", [128, 512], F32)
        self.wq_sb = [sb("wq_sb0", [128, ND * 192], BF16)]
        self.wq_sb.append(self.wq_sb[0])
        self.wrhs = [sb("wrhs0", [128, (D // 128) * min(512, D)], BF16)]
        self.wrhs.append(self.wrhs[0])
        self.gA = sb("gA", [128, D], BF16)
        self.xt1 = self.gA
        self.gB = sb("gB", [128, D], BF16)
        self.zbf = sb("zbf", [128, D], BF16)
        self.gwT = sb("gwT", [128, ND * NE], F32)
        self.wsel1 = sb("wsel1", [128, NT * NE], F32)
        self.gcols = sb("gcols", [128, ND], F32)
        self.bcols = sb("bcols", [128, ND], F32)
        self.bcols_b = sb("bcols_b", [128, ND], BF16)
        self.brow_h = sb("brow_h", [1, 192], BF16)
        self.ones_row_b = sb("ones_row_b", [1, 512], BF16)
        self.lmask = sb("lmask", [128, 256], BF16)
        self.t2_sb = sb("t2_sb", [32, 128], BF16)
        self.wselT = sb("wselT", [8, 128], BF16)
        self.wexpT = sb("wexpT", [32, 128], BF16)
        self.E832 = sb("E832", [8, 32], BF16)
        self.blin_sb = sb("blin_sb", [1, min(512, D)], BF16)
        self.right_sb = sb("right_sb", [32, min(512, D)], BF16)
        self.left_sb = sb("left_sb", [128, ND * 32], BF16)
        self.psb = [self.ps(f"psb{i}", [128, 512], F32) for i in range(6)]
        self.psT = self.ps("psT", [128, 256], BF16)
        self.psS = self.ps("psS", [128, 256], F32)

    def emit_consts(self):
        p, dm = self.p, self.dm
        NE = dm.NE

        def aff_ident(tile):
            t = p.op("pool", lambda e, _t=tile: e.memset(_t[:], 1.0))
            return p.op("pool", lambda e, _t=tile: e.affine_select(
                out=_t[:], in_=_t[:], pattern=[[-1, 128]],
                compare_op=OP.is_equal, fill=0.0, base=0, channel_multiplier=1),
                deps=[t])

        self.t_identB = aff_ident(self.identB)
        self.t_identF = aff_ident(self.identF)
        self.t_iotaE = p.op("pool", lambda e: e.iota(
            self.iotaE[:], [[1, NE]], base=0, channel_multiplier=0,
            allow_small_or_imprecise_dtypes=True))
        self.t_e2048 = p.op("pool", lambda e: e.iota(
            self.e2048[:], [[dm.CAP, 16]], base=0, channel_multiplier=0,
            allow_small_or_imprecise_dtypes=True))
        self.t_onesK1f = p.op("pool", lambda e: e.memset(self.onesK1_f[:], 1.0))
        self.t_onesK1b = p.op("pool", lambda e: e.memset(self.onesK1_b[:], 1.0))
        self.t_onescb = p.op("pool", lambda e: e.memset(self.ones_col_b[:], 1.0))
        self.t_vones = p.op("pool", lambda e: e.memset(self.v_h[:], 1.0))
        self.t_o65 = p.op("pool", lambda e: e.memset(self.ones65[:], 1.0))
        self.t_z16 = None
        QW = min(512, dm.T)
        self.QW = QW
        t_tb = p.op("pool", lambda e, QW=QW: e.iota(
            self.Tbase[:, 0:QW], [[-1, QW]], base=0,
            channel_multiplier=1, allow_small_or_imprecise_dtypes=True))
        self.t_Tbase = t_tb
        self.t_m16 = p.op("pool", lambda e: e.memset(self.m16[:], -C_STAB))
        MM1 = self.MM + 1
        t_mp = p.op("pool", lambda e, MM1=MM1: e.iota(
            self.mpcols[:, 0:MM1], [[128, MM1]], base=0,
            channel_multiplier=0, allow_small_or_imprecise_dtypes=True))
        tb = t_mp
        for h in range(dm.H):
            sl = dm.slopes[h]
            tb = p.op("act", lambda e, h=h, sl=sl, MM1=MM1: e.activation(
                out=self.biascols[:, h * MM1:(h + 1) * MM1],
                in_=self.mpcols[:, 0:MM1], func=AF.Copy, scale=-sl),
                deps=[t_mp])
        t_bc2 = p.op("ve", lambda e: e.tensor_scalar(
            self.biascols[:], self.biascols[:], -C_STAB, None, OP.add),
            deps=[tb])
        self.t_biascols = t_bc2
        WG = QW + self.BASEW
        tz = p.op("pool", lambda e, WG=WG: e.memset(self.dmaskG[:, 0:WG], 0.0))
        t_dg = p.op("pool", lambda e, WG=WG: e.affine_select(
            out=self.dmaskG[:, 0:WG], in_=self.dmaskG[:, 0:WG],
            pattern=[[-1, WG]], compare_op=OP.is_ge, fill=NEGBIG,
            base=self.BASEW, channel_multiplier=1), deps=[tz])
        self.t_dmask = [t_dg] * (QW // 128)
        self.t_rb = p.dma("qsp", self.rb_sb[:], self.rb_in[:])
        self.t_rmask = p.dma("qsp", self.rmask_sb[:], self.rmask_in[:])
        self.t_esel = p.dma("qsp", self.esel_sb[:], self.esel_in[:])
        self.t_slmidx = p.dma("qsp", self.slmidx_sb[:], self.slmidx_in[:])
        tprev = None
        for src_t, dst in ((self.rw_in, self.rwT), (self.gw_in, self.gwT)):
            for c in range(dm.ND):
                t_rw = p.dma("qsp", self.rw_sb[:],
                             src_t[0:NE, c * 128:(c + 1) * 128],
                             deps=([tprev] if tprev else []))
                tt = p.op("te", lambda e: e.transpose(
                    self.psS[0:128, 0:NE], self.rw_sb[0:NE, 0:128],
                    self.identF[0:NE, 0:NE]),
                    deps=[t_rw, self.t_identF] + ([tprev] if tprev else []))
                tprev = p.op("ve", lambda e, c=c, dst=dst: e.tensor_copy(
                    dst[:, c * NE:(c + 1) * NE], self.psS[0:128, 0:NE]),
                    deps=[tt])
        self.t_rwT = tprev
        self.t_gwT = tprev
        self.t_orow = p.op("pool", lambda e: e.memset(self.ones_row_b[:], 1.0))
        tE = p.op("pool", lambda e: e.memset(self.E832[:], 1.0))
        self.t_E832 = p.op("pool", lambda e: e.affine_select(
            out=self.E832[:], in_=self.E832[:], pattern=[[-1, 8], [0, dm.R]],
            compare_op=OP.is_equal, fill=0.0, base=0, channel_multiplier=1),
            deps=[tE])
        tm = p.op("pool", lambda e: e.memset(self.lmask[:], 0.0))
        tm = p.op("pool", lambda e: e.affine_select(
            out=self.lmask[:, 0:128], in_=self.lmask[:, 0:128],
            pattern=[[-1, 128]], compare_op=OP.is_ge, fill=8.0 * NEGBIG,
            base=0, channel_multiplier=1), deps=[tm])
        self.t_lmask = p.op("pool", lambda e: e.affine_select(
            out=self.lmask[:, 128:256], in_=self.lmask[:, 128:256],
            pattern=[[1, 128]], compare_op=OP.is_ge, fill=8.0 * NEGBIG,
            base=0, channel_multiplier=-1), deps=[tm])
        leftr = self.left_in.ap().rearrange("(c p) r -> p c r", p=128)
        self.t_left = p.dma("qpool", self.left_sb.ap().rearrange(
            "p (c r) -> p c r", r=32), leftr)
        self.t_gcols = []
        self.t_bcols = []
        for c in range(dm.ND):
            gc = self.l1g_in.ap().rearrange("a (c p) -> (a c) p", p=128)
            self.t_gcols.append(p.dma("qsp", self.gcols[:, c:c + 1],
                                gc[c:c + 1, :].rearrange("a p -> p a")))
            bc = self.l1b_in.ap().rearrange("a (c p) -> (a c) p", p=128)
            self.t_bcols.append(p.dma("qsp", self.bcols[:, c:c + 1],
                                bc[c:c + 1, :].rearrange("a p -> p a")))
        self.t_bcolsb = p.op("ve", lambda e: e.tensor_copy(
            self.bcols_b[:], self.bcols[:]), deps=self.t_bcols)
        self.t_wqkvc = p.dma("qpool", self.wqkvbf[:], self.wqkv_in[:])
        mwr = self.mw_in.ap().rearrange("(c p) h -> p c h", p=128)
        mvr = self.mv_in.ap().rearrange("(c p) h -> p c h", p=128)
        self.t_wc = []
        self.t_vc = []
        NHT_ = dm.DH // 128
        for t_ in range(NHT_):
            self.t_wc.append(p.dma(
                "qpool", self.wbf[t_ * 128:(t_ + 1) * 128, :],
                mwr[:, :, t_ * 128:(t_ + 1) * 128]))
            self.t_vc.append(p.dma(
                "qpool", self.vbf[t_ * 128:(t_ + 1) * 128, :],
                mvr[:, :, t_ * 128:(t_ + 1) * 128]))
        self.t_oc = p.dma("qpool", self.wobf[:], self.mwo_in[:])
        self.t_wq1c = p.dma("qpool", self.wqkv1bf[:], self.wqkv1_in[:])
        self.t_wo1c = p.dma("qpool", self.wout1bf[:], self.wout1_in[:])
        self.t_wlc = p.dma("qpool", self.wlinbf[:], self.wlin_in[:])
        p.fence_queue("qpool")

    def phase_ln_router(self):
        p, dm = self.p, self.dm
        T, D, NE, NT, ND = dm.T, dm.D, dm.NE, dm.NT, dm.ND
        eps = 1e-5
        xn_done = []
        logit_done = []
        self.t_logits = logit_done
        self.t_xnbf = []
        xs, xn_f = self.xs[0], self.xn_f[0]
        st = lambda b, k: self.stat[:, b * 8 + k:b * 8 + k + 1]
        stA = {}
        stB = {}

        def stage_a(i):
            b = i % 2
            t_x = p.dma("qsp", xs[:], self.x_in[i * 128:(i + 1) * 128, :],
                        deps=stA.get("xs_war", []))
            t_s1 = p.op("ve", lambda e, b=b: e.reduce_sum(
                st(b, 0), xs[:], axis=AX.X), deps=[t_x])
            t_sq = p.op("act", lambda e, b=b: e.activation(
                out=self.xn_bf_t[:], in_=xs[:], func=AF.Square,
                accum_out=st(b, 1)), deps=[t_x] + stA.get("sq_war", []))
            t_m = p.op("ve", lambda e, b=b: e.tensor_scalar(
                st(b, 2), st(b, 0), 1.0 / D, None, OP.mult), deps=[t_s1])
            t_ex2 = p.op("ve", lambda e, b=b: e.tensor_scalar(
                st(b, 3), st(b, 1), 1.0 / D, eps, OP.mult, OP.add),
                deps=[t_sq])
            t_m2 = p.op("ve", lambda e, b=b: e.tensor_tensor(
                st(b, 4), st(b, 2), st(b, 2), OP.mult), deps=[t_m])
            t_var = p.op("ve", lambda e, b=b: e.tensor_tensor(
                st(b, 5), st(b, 3), st(b, 4), OP.subtract),
                deps=[t_ex2, t_m2])
            t_sd = p.op("act", lambda e, b=b: e.activation(
                out=st(b, 6), in_=st(b, 5), func=AF.Ln), deps=[t_var])
            t_rstd = p.op("act", lambda e, b=b: e.activation(
                out=st(b, 5), in_=st(b, 6), func=AF.Exp, scale=-0.5),
                deps=[t_sd])
            t_nmr = p.op("ve", lambda e, b=b: e.scalar_tensor_tensor(
                st(b, 4), st(b, 2), -1.0, st(b, 5), OP.mult, OP.mult),
                deps=[t_rstd, t_m])
            stA[("done", i)] = (t_x, t_nmr, t_sq)

        def stage_b(i):
            b = i % 2
            t_x, t_nmr, t_sq = stA.pop(("done", i))
            t_xn = p.op("ve", lambda e, b=b: e.tensor_scalar(
                xn_f[:], xs[:], st(b, 5), st(b, 4), OP.mult, OP.add),
                deps=[t_nmr, t_x] + stB.get("xnf_war", []))
            t_cb = p.op("ve", lambda e: e.tensor_copy(
                self.xn_bf_t[:], xn_f[:]),
                deps=[t_xn, t_sq] + (xn_done[-1:] if xn_done else []))
            t_st = p.dma("qact", self.cc_xn_in[i * 128:(i + 1) * 128, :],
                         self.xn_bf_t[:], deps=[t_cb])
            xn_done.append(t_st)
            stB[i] = t_xn
            stA["xs_war"] = [t_xn, t_sq]
            stA["sq_war"] = [t_st]

        def stage_c(i):
            t_xn = stB.pop(i)
            t_rmm = None
            psev = getattr(self, "_psS_ev", {})
            t_bf = None
            last_tt = None
            for c in range(ND):
                sl_ = c % 2
                tt = p.op("te", lambda e, c=c, sl_=sl_: e.transpose(
                    self.psS[:, sl_ * 128:(sl_ + 1) * 128],
                    xn_f[:, c * 128:(c + 1) * 128],
                    self.identF[:]),
                    deps=[t_xn, self.t_identF] +
                         ([psev[sl_]] if sl_ in psev else []))
                last_tt = tt
                tev = p.op("ve", lambda e, c=c, sl_=sl_: e.tensor_copy(
                    self.xnT_st[:, c * 128:(c + 1) * 128],
                    self.psS[:, sl_ * 128:(sl_ + 1) * 128]),
                    deps=[tt])
                psev[sl_] = tev
                self._psS_ev = psev
                self._t_psS_last = tev
                t_bf = p.op("ve", lambda e, c=c, i=i: e.tensor_copy(
                    self.xnT_bf[:, c * T + i * 128:c * T + (i + 1) * 128],
                    self.xnT_st[:, c * 128:(c + 1) * 128]), deps=[tev])
                rdeps = [tev, self.t_rwT]
                if c == 0 and logit_done:
                    rdeps.append(logit_done[-1])
                t_rmm = p.op("te", lambda e, c=c, i=i: e.matmul(
                    self.psb[3][0:128, 0:NE],
                    self.xnT_st[:, c * 128:(c + 1) * 128],
                    self.rwT[:, c * NE:(c + 1) * NE],
                    start=(c == 0), stop=False), deps=rdeps)
            t_rmm = p.op("te", lambda e: e.matmul(
                self.psb[3][0:128, 0:NE], self.onesK1_f[:], self.rb_sb[:],
                start=False, stop=True),
                deps=[t_rmm, self.t_onesK1f, self.t_rb])
            t_lg = p.op("ve", lambda e, i=i: e.tensor_copy(
                self.logits[:, i * NE:(i + 1) * NE], self.psb[3][0:128, 0:NE]),
                deps=[t_rmm])
            logit_done.append(t_lg)
            self.t_xnbf.append(t_bf)
            stB["xnf_war"] = [last_tt, t_bf]

        stage_a(0)
        stage_b(0)
        for i in range(NT):
            stage_c(i)
            if i + 1 < NT:
                stage_a(i + 1)
                stage_b(i + 1)
            self.emit_indicator(i)
        self.t_xn_all = xn_done
        self.t_logits = logit_done
        self.t_ag_xn = p.collective("AllGather", OP.bypass,
                                    [self.cc_xn_in.ap().opt()],
                                    [self.cc_xn_out.ap().opt()],
                                    deps=xn_done)
        self.dump("logits", [128, NT * NE], F32, lambda: self.logits[:],
                  logit_done)
        self.dump("xnT_bf", [128, ND * T], BF16, lambda: self.xnT_bf[:],
                  logit_done)

    def emit_indicator(self, i):
        p, dm = self.p, self.dm
        T, NE, NT = dm.T, dm.NE, dm.NT
        a, b2, c3, d4 = self.tsc
        mx, mx2, icol, dcol, pcol, qcol = self.vcol
        lg = self.logits[:, i * NE:(i + 1) * NE]
        dep0 = [self.t_logits[i]]
        if i == 0:
            self._ind_st = []
            self._t_indz = p.op("pool", lambda e: e.memset(self.ind_sb[:], 0.0))
        ind_st = self._ind_st
        t1 = p.op("ve", lambda e, lg=lg: e.reduce_max(mx[:], lg, axis=AX.X),
                  deps=dep0)
        teq = p.op("ve", lambda e, lg=lg: e.tensor_scalar(
            a[:], lg, mx[:], None, OP.is_equal), deps=[t1])
        trv = p.op("ve", lambda e: e.scalar_tensor_tensor(
            b2[:], self.iotaE[:], -1.0, a[:], OP.mult, OP.mult),
            deps=[teq, self.t_iotaE])
        trv2 = p.op("ve", lambda e: e.scalar_tensor_tensor(
            b2[:], a[:], float(NE), b2[:], OP.mult, OP.add), deps=[trv])
        tmx = p.op("ve", lambda e: e.reduce_max(icol[:], b2[:], axis=AX.X),
                   deps=[trv2])
        ti1 = p.op("ve", lambda e: e.tensor_scalar(
            icol[:], icol[:], -1.0, float(NE), OP.mult, OP.add), deps=[tmx])
        to1 = p.op("ve", lambda e, i=i: e.tensor_scalar(
            self.oh1[:, i * NE:(i + 1) * NE], self.iotaE[:], icol[:], None,
            OP.is_equal), deps=[ti1])
        tml = p.op("ve", lambda e, lg=lg, i=i: e.scalar_tensor_tensor(
            c3[:], self.oh1[:, i * NE:(i + 1) * NE], float(NEGBIG), lg,
            OP.mult, OP.add), deps=[to1])
        t2 = p.op("ve", lambda e: e.reduce_max(mx2[:], c3[:], axis=AX.X),
                  deps=[tml])
        teq2 = p.op("ve", lambda e: e.tensor_scalar(
            a[:], c3[:], mx2[:], None, OP.is_equal), deps=[t2])
        trv3 = p.op("ve", lambda e: e.scalar_tensor_tensor(
            b2[:], self.iotaE[:], -1.0, a[:], OP.mult, OP.mult), deps=[teq2])
        trv4 = p.op("ve", lambda e: e.scalar_tensor_tensor(
            b2[:], a[:], float(NE), b2[:], OP.mult, OP.add), deps=[trv3])
        tmx2 = p.op("ve", lambda e: e.reduce_max(dcol[:], b2[:], axis=AX.X),
                    deps=[trv4])
        ti2 = p.op("ve", lambda e: e.tensor_scalar(
            dcol[:], dcol[:], -1.0, float(NE), OP.mult, OP.add), deps=[tmx2])
        to2 = p.op("ve", lambda e, i=i: e.tensor_scalar(
            self.oh2[:, i * NE:(i + 1) * NE], self.iotaE[:], dcol[:], None,
            OP.is_equal), deps=[ti2])
        tdv = p.op("ve", lambda e: e.tensor_tensor(
            pcol[:], mx[:], mx2[:], OP.subtract), deps=[t1, t2])
        tp1 = p.op("act", lambda e, i=i: e.activation(
            out=self.p1c[:, i:i + 1], in_=pcol[:], func=AF.Exp,
            scale=-1.0), deps=[tdv])
        tp1 = p.op("ve", lambda e, i=i: e.tensor_scalar(
            self.p1c[:, i:i + 1], self.p1c[:, i:i + 1], 1.0, None, OP.add),
            deps=[tp1])
        tp1 = p.op("ve", lambda e, i=i: e.reciprocal(
            self.p1c[:, i:i + 1], self.p1c[:, i:i + 1]), deps=[tp1])
        tp2 = p.op("ve", lambda e, i=i: e.tensor_scalar(
            self.p2c[:, i:i + 1], self.p1c[:, i:i + 1], -1.0, 1.0,
            OP.mult, OP.add), deps=[tp1])
        tind = p.op("ve", lambda e, i=i: e.tensor_tensor(
            d4[:], self.oh1[:, i * NE:(i + 1) * NE],
            self.oh2[:, i * NE:(i + 1) * NE], OP.add), deps=[to1, to2])
        tz = self._t_indz if not ind_st else ind_st[-1][0]
        tcv = p.op("ve", lambda e: e.tensor_copy(
            self.ind_sb[0:128, 0:8], d4[:]), deps=[tind, tz])
        tst = p.dma("qact", self.ind_dram[i * 128:(i + 1) * 128, :],
                    self.ind_sb[0:128, 0:128], deps=[tcv])
        ind_st.append((tst, tp2))

    def phase_routing(self):
        p, dm = self.p, self.dm
        T, D, NE, NT = dm.T, dm.D, dm.NE, dm.NT
        CAP, TG, Q = dm.CAP, dm.TG, dm.Q
        a, b2, c3, d4 = self.tsc
        mx, mx2, icol, dcol, pcol, qcol = self.vcol
        ind_st = self._ind_st
        self.t_z16 = p.op("ve", lambda e: e.memset(
            self.bufB[0:16, 3 * T:4 * T], 0.0))
        cum = self.seqF[0:16, 0:T]
        t_ld = p.dma("qact", self.bufB[0:128, 2 * T:3 * T], self.ind_dram[:],
                     deps=[t for t, _ in ind_st], transpose=True)
        t_scan = p.op("ve", lambda e, cum=cum: e.tensor_tensor_scan(
            cum, self.bufB[0:16, 2 * T:3 * T], self.bufB[0:16, 3 * T:4 * T],
            0.0, OP.add, OP.add), deps=[t_ld, self.t_z16])
        t_tot = p.dma("qact", self.cc_tot_in[0:1, 0:16],
                      self.seqF[0:16, T - 1:T], deps=[t_scan])
        t_agt = p.collective("AllGather", OP.bypass,
                             [self.cc_tot_in.ap().opt()],
                             [self.cc_tot_out.ap().opt()], deps=[t_tot])
        kmask = self.bufB[0:16, 3 * T:4 * T]
        self._rb = dict(t_agt=t_agt, t_scan=t_scan, t_tot=t_tot, cum=cum,
                        kmask=kmask)
        return

    def emit_routing_b(self):
        p, dm = self.p, self.dm
        T, NE, NT, CAP = dm.T, dm.NE, dm.NT, dm.CAP
        t_agt = self._rb["t_agt"]
        t_scan = self._rb["t_scan"]
        t_tot = self._rb["t_tot"]
        cum = self._rb["cum"]
        kmask = self._rb["kmask"]
        t_ldt = p.dma("qsp", self.totAG[:], self.cc_tot_out[:], deps=[t_agt])
        t_off = p.op("te", lambda e: e.matmul(
            self.psS[0:1, 0:16], self.rmask_sb[:], self.totAG[:],
            start=True, stop=True), deps=[t_ldt, self.t_rmask])
        t_offr = p.op("ve", lambda e: e.tensor_copy(
            self.srow[0:1, 0:16], self.psS[0:1, 0:16]), deps=[t_off])
        t_offd = p.dma("qact", self.offs_dram[:], self.srow[0:1, 0:16],
                       deps=[t_offr])
        t_offc = p.dma("qsp", self.offs_c[:],
                       self.offs_dram.ap().rearrange("a b -> b a"), deps=[t_offd])
        kcol = self.vcol[0][0:16, 0:1]
        ocol = self.vcol[1][0:16, 0:1]
        t_kc = p.op("ve", lambda e, kcol=kcol: e.tensor_scalar(
            kcol, self.offs_c[:], -1.0, float(CAP) + 0.5, OP.mult, OP.add),
            deps=[t_offc])
        t_oc2 = p.op("ve", lambda e, ocol=ocol: e.tensor_scalar(
            ocol, self.offs_c[:], 1.0, -1.0 - 1.0e6, OP.mult, OP.add),
            deps=[t_offc])
        t_k0 = p.op("ve", lambda e, cum=cum, kmask=kmask, kcol=kcol: e.tensor_scalar(
            kmask, cum, kcol, None, OP.is_lt), deps=[t_kc, t_scan, t_tot])
        t_k1 = p.op("ve", lambda e, kmask=kmask: e.tensor_tensor(
            kmask, kmask, self.bufB[0:16, 2 * T:3 * T], OP.mult), deps=[t_k0])
        t_s0 = p.op("ve", lambda e, cum=cum, ocol=ocol: e.tensor_scalar(
            cum, cum, ocol, None, OP.add), deps=[t_oc2, t_k0])
        t_s1 = p.op("ve", lambda e, cum=cum, kmask=kmask: e.tensor_tensor(
            cum, cum, kmask, OP.mult), deps=[t_s0, t_k1])
        t_s2 = p.op("ve", lambda e, cum=cum: e.tensor_scalar(
            cum, cum, 1.0e6, None, OP.add), deps=[t_s1])
        self.t_slots_done = t_s2
        t_sst = p.dma("qact", self.cc_sl_in[:], self.seqF[0:16, 0:T],
                      deps=[t_s2])
        t_ags = p.collective("AllGather", OP.bypass,
                             [self.cc_sl_in.ap().opt()],
                             [self.cc_sl_out.ap().opt()], deps=[t_sst])
        self.t_ags = t_ags
        t_mi = p.op("ve", lambda e: e.tensor_copy(
            self.slmidx_i[:], self.slmidx_sb[:]), deps=[self.t_slmidx])
        slr = self.cc_sl_out.ap().rearrange("p (k q) -> (p k) q", q=dm.Q)
        t_ms = p.indirect(self.myslot_f[:, 0:dm.Q], None, slr,
                          bass.IndirectOffsetOnAxis(self.slmidx_i[:], 0),
                          128 * (T // dm.Q) - 1, deps=[t_ags, t_mi])
        self.t_myslot = p.op("ve", lambda e: e.tensor_copy(
            self.myslot_i[:], self.myslot_f[:, 0:dm.Q]), deps=[t_ms])
        self._rb["t_ms"] = t_ms
        self._rb["t_s2"] = self.t_slots_done

    def emit_routing_c(self):
        p, dm = self.p, self.dm
        T, NE, NT, CAP = dm.T, dm.NE, dm.NT, dm.CAP
        t_s2 = self._rb["t_s2"]
        t_scan = self._rb["t_scan"]
        tok_sl = []
        tprev = None
        for i in range(NT):
            tt = p.op("te", lambda e, i=i: e.transpose(
                self.psS[0:128, 0:16],
                self.seqF[0:16, i * 128:(i + 1) * 128],
                self.identF[0:16, 0:16]),
                deps=[t_s2, self.t_identF] + ([tprev] if tprev else []))
            tprev = p.op("ve", lambda e, i=i: e.tensor_copy(
                self.slots_tok[:, i * NE:(i + 1) * NE], self.psS[0:128, 0:NE]),
                deps=[tt])
            tok_sl.append(tprev)
            self._last_seqF_rd = tt
        a = self.tsc[0]
        fl_done = []
        for i in range(NT):
            tfc = p.op("ve", lambda e, i=i: e.tensor_tensor(
                a[:], self.slots_tok[:, i * NE:(i + 1) * NE], self.e2048[:, 0:NE],
                OP.add), deps=[tok_sl[i], self.t_e2048])
            tm1 = p.op("ve", lambda e, i=i: e.tensor_tensor(
                self.tsc[1][:], a[:], self.oh1[:, i * NE:(i + 1) * NE], OP.mult),
                deps=[tfc])
            tr1 = p.op("ve", lambda e: e.reduce_sum(
                self.vcol[0][:], self.tsc[1][:], axis=AX.X), deps=[tm1])
            tf1 = p.op("ve", lambda e, i=i: e.tensor_copy(
                self.flat1[:, i:i + 1], self.vcol[0][:]), deps=[tr1])
            tm2 = p.op("ve", lambda e, i=i: e.tensor_tensor(
                self.tsc[1][:], a[:], self.oh2[:, i * NE:(i + 1) * NE], OP.mult),
                deps=[tf1])
            tr2 = p.op("ve", lambda e: e.reduce_sum(
                self.vcol[0][:], self.tsc[1][:], axis=AX.X), deps=[tm2])
            tf2 = p.op("ve", lambda e, i=i: e.tensor_copy(
                self.flat2[:, i:i + 1], self.vcol[0][:]), deps=[tr2])
            fl_done.append(tf2)
        self.t_flats = fl_done
        d1 = self.dump("slotsT", [16, T], F32,
                       lambda: self.seqF[0:16, 0:T], [t_s2])
        d2 = self.dump("cum", [16, T], F32, lambda: self.seqF[0:16, 0:T],
                       [t_scan])
        self.t_seqF_done = [self._last_seqF_rd] + [t for t in (d1, d2) if t]
        self.dump("myslot", [128, dm.Q], F32,
                  lambda: self.myslot_f[:, 0:dm.Q], [self._rb["t_ms"]])
        self.dump("slots_tok", [128, NT * NE], F32, lambda: self.slots_tok[:],
                  tok_sl)
        self.dump("oh1", [128, NT * NE], BF16, lambda: self.oh1[:], fl_done)
        self.dump("oh2", [128, NT * NE], BF16, lambda: self.oh2[:], fl_done)
        self.dump("p1c", [128, NT], F32, lambda: self.p1c[:], fl_done)
        self.dump("p2c", [128, NT], F32, lambda: self.p2c[:], fl_done)
        fl32 = self.sb("fl32", [128, 2 * NT], F32)
        if self.want("flats"):
            t1 = p.op("ve", lambda e: e.tensor_copy(fl32[:, 0:NT], self.flat1[:]),
                      deps=fl_done)
            t2 = p.op("ve", lambda e: e.tensor_copy(fl32[:, NT:2 * NT], self.flat2[:]),
                      deps=[t1])
            self.dump("flats", [128, 2 * NT], F32, lambda: fl32[:], [t2])

    def phase_moe_scatter(self):
        p, dm = self.p, self.dm
        D, CAP, Q = dm.D, dm.CAP, dm.Q
        nwr = ((dm.D // 128) * min(512, dm.D)) // D
        xr_v = [lambda k=k: self.wrhs[0][:, k * D:(k + 1) * D]
                for k in range(nwr)]
        xr_v.append(lambda: self.gA[:])
        xr_v.append(lambda: self.gB[:])
        xr_v.append(lambda: self.zbf[:])
        NXR = len(xr_v)
        t_z = p.op("pool", lambda e: e.memset(xr_v[0](), 0.0))
        zd = []
        for rb in range(CAP // 128):
            zd.append(p.dma("qpool", self.disp[rb * 128:(rb + 1) * 128, :],
                            xr_v[0](), deps=[t_z]))
        xn_all_r = self.cc_xn_out.ap().rearrange("(p q) d -> p q d", q=Q)
        self._scat = dict(xr_v=xr_v, NXR=NXR, nwr=nwr, zd=zd,
                          xn_all_r=xn_all_r, prev_grp=[], last_buf_sc={},
                          next_g0=0)
        self.sc_all = []
        self.t_xr_war = {}
        self._moe = dict(prev_ab={}, y_done=[], wl={}, first_wo=True)

    def emit_scatter_groups(self, ngrp):
        p, dm = self.p, self.dm
        D, CAP, Q = dm.D, dm.CAP, dm.Q
        st_ = self._scat
        xr_v, NXR, nwr = st_["xr_v"], st_["NXR"], st_["nwr"]
        for _ in range(ngrp):
            g0 = st_["next_g0"]
            if g0 >= Q:
                break
            st_["next_g0"] = g0 + NXR
            lds = []
            for k in range(min(NXR, Q - g0)):
                j = g0 + k
                ldeps = (([self.t_ag_xn] + st_["zd"] + st_["prev_grp"])
                         if k == 0 else [])
                lds.append(p.dma("qsp", xr_v[k](),
                                 st_["xn_all_r"][:, j, :], deps=ldeps))
            grp = []
            for k in range(min(NXR, Q - g0)):
                j = g0 + k
                t_sc = p.indirect(self.disp[:], bass.IndirectOffsetOnAxis(
                    self.myslot_i[:, j:j + 1], 0), xr_v[k](), None,
                    CAP - 1, deps=[lds[k], self.t_myslot])
                grp.append(t_sc)
                st_["last_buf_sc"][k] = t_sc
            self.sc_all.extend(grp)
            st_["prev_grp"] = grp
        if st_["next_g0"] >= Q and not self.t_xr_war:
            lb = st_["last_buf_sc"]
            self.t_xr_war = {"gA": lb.get(nwr), "gB": lb.get(nwr + 1),
                             "zbf": lb.get(nwr + 2)}
            if self.want("disp"):
                dd = self.eout("dbg_disp", [dm.CAP, dm.D], BF16)
                p.dma("qpool", dd[:], self.disp[:], deps=self.sc_all)

    def emit_moe_chunk(self, sq, exp_prev, pvtok):
        p, dm = self.p, self.dm
        D, DH, CAP, ND = dm.D, dm.DH, dm.CAP, dm.ND
        SQW = min(512, CAP)
        NHT = DH // 128
        DCW = min(256, D)
        NDC = D // DCW
        cv = self.carve
        B = self.bufB
        xinT = lambda sl: B[:, cv["xinT"] + sl[0]:cv["xinT"] + sl[1]]
        h_buf = lambda sl: B[:, cv["h_buf"] + sl[0]:cv["h_buf"] + sl[1]]
        wo_v = lambda sl: self.wrhs[0][:, sl[0]:sl[1]]
        w_v = [lambda sl, o=cv["w0"]: B[:, o + sl[0]:o + sl[1]],
               lambda sl, o=cv["w1"]: B[:, o + sl[0]:o + sl[1]]]
        v_v = [lambda sl, o=cv["v0"]: B[:, o + sl[0]:o + sl[1]],
               lambda sl, o=cv["v1"]: B[:, o + sl[0]:o + sl[1]]]
        hsc = self.hsc
        ysb = self.ysb
        rbar = self.t_slots_done
        t_wcl, t_vcl, t_oc = self.t_wc, self.t_vc, self.t_oc
        st_ = self._moe
        prev_ab, y_done, wl = st_["prev_ab"], st_["y_done"], st_["wl"]
        wobf_r = self.wobf.ap().rearrange("(t p) d -> p t d", p=128)
        xt = []
        for c in range(ND):
            xdep = self.sc_all + [rbar] + getattr(self, "t_seqF_done", [])
            if "xt" in prev_ab:
                xdep.append(prev_ab["xt"])
            xt.append(p.dma("qsp", xinT((c * SQW, (c + 1) * SQW)),
                            self.disp[sq * SQW:(sq + 1) * SQW,
                                      c * 128:(c + 1) * 128],
                            deps=xdep, transpose=True))
        for ht in range(NHT):
            bsel = ht % 2
            wld = wl.get(("w", bsel))
            t_w = p.dma("qsp", w_v[bsel]((0, ND * 128)),
                        self.wbf[ht * 128:(ht + 1) * 128, :],
                        deps=[t_wcl[ht], rbar] + ([wld] if wld else []))
            t_v = p.dma("qsp", v_v[bsel]((0, ND * 128)),
                        self.vbf[ht * 128:(ht + 1) * 128, :],
                        deps=[t_vcl[ht]] + ([wld] if wld else []))
            mma = None
            mmb = None
            adeps = xt + [t_w]
            bdeps = [t_v]
            if "ab" in prev_ab:
                adeps = adeps + [prev_ab["ab"]]
                bdeps = bdeps + [prev_ab["ab"]]
            if exp_prev.get(0) is not None:
                adeps = adeps + [exp_prev[0]]
            if exp_prev.get(1) is not None:
                bdeps = bdeps + [exp_prev[1]]
            for c in range(ND):
                mma = p.op("te", lambda e, c=c, b=bsel, SQW=SQW: e.matmul(
                    self.psb[0][0:128, 0:SQW],
                    w_v[b]((c * 128, (c + 1) * 128)),
                    xinT((c * SQW, (c + 1) * SQW)),
                    start=(c == 0), stop=(c == ND - 1)),
                    deps=(adeps if c == 0 else []))
            for c in range(ND):
                mmb = p.op("te", lambda e, c=c, b=bsel, SQW=SQW: e.matmul(
                    self.psb[1][0:128, 0:SQW],
                    v_v[b]((c * 128, (c + 1) * 128)),
                    xinT((c * SQW, (c + 1) * SQW)),
                    start=(c == 0), stop=(c == ND - 1)),
                    deps=(bdeps if c == 0 else []))
            t_si = p.op("act", lambda e, SQW=SQW: e.activation(
                out=hsc[:, 0:SQW], in_=self.psb[1][0:128, 0:SQW],
                func=AF.Silu),
                deps=[mmb] + ([prev_ab["ab"]] if "ab" in prev_ab else []))
            t_h = p.op("ve", lambda e, ht=ht, SQW=SQW: e.tensor_tensor(
                h_buf((ht * SQW, (ht + 1) * SQW)), self.psb[0][0:128, 0:SQW],
                hsc[:, 0:SQW], OP.mult), deps=[mma, t_si])
            prev_ab["ab"] = t_h
            wl[("w", bsel)] = t_h
        prev_ab["xt"] = prev_ab["ab"]
        t_h_all = prev_ab["ab"]
        t_ye = None
        for dc in range(NDC):
            wod = wl.get("wo")
            wo_deps = [t_oc, rbar] + ([wod] if wod else [])
            if st_["first_wo"]:
                wo_deps += self.sc_all
                st_["first_wo"] = False
            t_wo = p.dma("qsp", wo_v((0, NHT * DCW)),
                         wobf_r[:, :, dc * DCW:(dc + 1) * DCW],
                         deps=wo_deps)
            for st in range(SQW // 128):
                pb = self.psb[2 + (st % 2)]
                mmy = None
                ydeps = [t_h_all, t_wo, self.t_logits[-1]]
                pv = prev_ab.get(("y", st % 2))
                if pv:
                    ydeps.append(pv)
                if pvtok is not None:
                    ydeps.append(pvtok)
                for ht in range(NHT):
                    mmy = p.op("te", lambda e, ht=ht, st=st, pb=pb, SQW=SQW,
                               DCW=DCW: e.matmul(
                        pb[0:128, 0:DCW],
                        h_buf((ht * SQW + st * 128, ht * SQW + (st + 1) * 128)),
                        wo_v((ht * DCW, (ht + 1) * DCW)),
                        start=(ht == 0), stop=(ht == NHT - 1)),
                        deps=(ydeps if ht == 0 else []))
                t_ye = p.op("ve", lambda e, pb=pb, DCW=DCW: e.tensor_copy(
                    ysb[:, 0:DCW], pb[0:128, 0:DCW]), deps=[mmy] +
                    (y_done[-1:] if y_done else []))
                t_yd = p.dma("qact",
                             self.cc_y_in[sq * SQW + st * 128:
                                          sq * SQW + (st + 1) * 128,
                                          dc * DCW:(dc + 1) * DCW],
                             ysb[:, 0:DCW], deps=[t_ye])
                prev_ab[("y", st % 2)] = t_ye
                y_done.append(t_yd)
            wl["wo"] = t_ye
        return prev_ab["ab"], t_ye

    def emit_moe_ag(self):
        p, dm = self.p, self.dm
        self.t_ag_y = p.collective("AllGather", OP.bypass,
                                   [self.cc_y_in.ap().opt()],
                                   [self.cc_y_out.ap().opt()],
                                   deps=self._moe["y_done"])
        if self.want("y_out"):
            yo = self.eout("dbg_y_out", [8 * dm.CAP, dm.D], BF16)
            p.dma("qpool", yo[:], self.cc_y_out[:], deps=[self.t_ag_y])

    def phase_attn_l0(self):
        p, dm = self.p, self.dm
        T, D, H, ND, NT = dm.T, dm.D, dm.H, dm.ND, dm.NT
        QW = self.QW
        NQC = T // QW
        wqkv_r = self.wqkvbf.ap().rearrange("(c p) n -> p c n", p=128)
        self.t_OT = []
        exp_prev = {}
        last_wq = {}
        last_qkT = None
        t_pe = None
        pvtok = None
        import math as _math
        ngrp_tot = (dm.Q + self._scat["NXR"] - 1) // self._scat["NXR"]
        gph = _math.ceil(ngrp_tot / max(1, min(4, H - self.SCAT_START)))
        for h in range(H):
            sl = self.dm.slopes[h]
            bsel = h % 2
            wq = self.wq_sb[bsel]
            wdep = [last_wq['w']] if 'w' in last_wq else [self.t_wqkvc]
            tws = []
            for g, off in ((0, h * 64), (1, D + h * 64), (2, 2 * D + h * 64)):
                tws.append(p.dma("qsp",
                                 wq.ap().rearrange("p (c x) -> p c x", x=192)[:, :, g * 64:(g + 1) * 64],
                                 wqkv_r[:, :, off:off + 64],
                                 deps=wdep if g == 0 else []))
            qk_war = [last_qkT] if last_qkT else []
            TCW = min(512, T)
            qkvev = {0: getattr(self, "_t_bce_prev", None), 1: t_pe}
            gi = 0
            qkev = []
            for tc in range(T // TCW):
                bk = 4 + gi % 2
                mm = None
                wdeps = [tws[0], tws[1]] + qk_war
                wdeps.append(self.t_xnbf[min((tc + 1) * (TCW // 128) - 1,
                                             NT - 1)])
                if qkvev[gi % 2] is not None:
                    wdeps.append(qkvev[gi % 2])
                for c in range(ND):
                    mm = p.op("te", lambda e, tc=tc, c=c, wq=wq,
                              TCW=TCW, bk=bk: e.matmul(
                        self.psb[bk][0:128, 0:TCW],
                        wq[:, c * 192:c * 192 + 128],
                        self.xnT_bf[:, c * T + tc * TCW:c * T + (tc + 1) * TCW],
                        start=(c == 0), stop=(c == ND - 1)),
                        deps=(wdeps if c == 0 else []))
                ev1 = p.op("ve", lambda e, tc=tc, TCW=TCW, bk=bk: e.tensor_copy(
                    self.qT[0:64, tc * TCW:(tc + 1) * TCW],
                    self.psb[bk][0:64, 0:TCW]), deps=[mm])
                qkvev[gi % 2] = p.op("ve", lambda e, tc=tc, TCW=TCW,
                                     bk=bk: e.tensor_copy(
                    self.kT[0:64, tc * TCW:(tc + 1) * TCW],
                    self.psb[bk][64:128, 0:TCW]), deps=[ev1])
                qkev.append(qkvev[gi % 2])
                gi += 1
            vev = []
            for tt in range(NT):
                bk = 4 + gi % 2
                mm = None
                wdeps = [tws[2], self.t_xnbf[tt]]
                if qkvev[gi % 2] is not None:
                    wdeps.append(qkvev[gi % 2])
                for c in range(ND):
                    mm = p.op("te", lambda e, tt=tt, c=c, wq=wq, bk=bk: e.matmul(
                        self.psb[bk][0:128, 0:64],
                        self.xnT_bf[:, c * T + tt * 128:c * T + (tt + 1) * 128],
                        wq[:, c * 192 + 128:c * 192 + 192],
                        start=(c == 0), stop=(c == ND - 1)),
                        deps=(wdeps if c == 0 else []))
                qkvev[gi % 2] = p.op("ve", lambda e, tt=tt, bk=bk: e.tensor_copy(
                    self.v_h[:, tt * 65:tt * 65 + 64], self.psb[bk][0:128, 0:64]),
                    deps=[mm, self.t_vones])
                vev.append(qkvev[gi % 2])
                gi += 1
            t_pe = qkvev[(gi - 1) % 2]
            t_qkv = p.op("ve", lambda e: e.engine_nop(),
                         deps=[qkvev[0], qkvev[1]])
            last_wq['w'] = t_qkv
            DTRUNC = int(16.0 / sl) + 1
            for qc in range(NQC):
                q0 = qc * QW
                j_lo = q0 // 128
                j_hi = min(T // 128 - 1, (q0 + QW - 1 + DTRUNC) // 128)
                first = True
                pv_mm = None
                phat_war = getattr(self, "_phat_war", {})

                def emit_S(j):
                    sb_ = j % 2
                    sdeps = [qkev[j // (TCW // 128)], qkev[qc * QW // TCW]]
                    if sb_ in exp_prev:
                        sdeps.append(exp_prev[sb_])
                    return p.op("te", lambda e, j=j, qc=qc, sb_=sb_: e.matmul(
                        self.psb[sb_][0:128, 0:QW],
                        self.kT[0:64, j * 128:(j + 1) * 128],
                        self.qT[0:64, qc * QW:(qc + 1) * QW],
                        start=True, stop=True), deps=sdeps)

                t_s = emit_S(j_lo)
                for j in range(j_lo, j_hi + 1):
                    sb_ = j % 2
                    t_s_cur = t_s
                    t_b = p.op("ve", lambda e, sb_=sb_, sl=sl: e.scalar_tensor_tensor(
                        self.psb[sb_][0:128, 0:QW], self.Tbase[:, 0:QW],
                        -8.0 * sl, self.psb[sb_][0:128, 0:QW], OP.mult, OP.add),
                        deps=[t_s_cur, self.t_Tbase])
                    m = j - q0 // 128
                    if 0 <= m < QW // 128:
                        mo = self.dmask_off[m]
                        t_b = p.op("ve", lambda e, sb_=sb_, mo=mo: e.tensor_tensor(
                            self.psb[sb_][0:128, 0:QW], self.psb[sb_][0:128, 0:QW],
                            self.dmaskG[:, mo:mo + QW], OP.add),
                            deps=[t_b, self.t_dmask[m]])
                    edeps = [t_b, self.t_biascols]
                    if sb_ in phat_war:
                        edeps.append(phat_war[sb_])
                    if pvtok is not None:
                        edeps.append(pvtok)
                    bcol = h * (self.MM + 1) + (j - q0 // 128)
                    t_e = p.op("act", lambda e, sb_=sb_, bcol=bcol: e.activation(
                        out=self.phat[sb_][:, 0:QW], in_=self.psb[sb_][0:128, 0:QW],
                        func=AF.Exp, bias=self.biascols[:, bcol:bcol + 1],
                        scale=0.125), deps=edeps)
                    exp_prev[sb_] = t_e
                    if j < j_hi:
                        t_s = emit_S(j + 1)
                    pv_mm = p.op("te", lambda e, j=j, sb_=sb_, first=first,
                                 j_hi=j_hi: e.matmul(
                        self.psb[2][0:65, 0:QW], self.v_h[:, j * 65:(j + 1) * 65],
                        self.phat[sb_][:, 0:QW],
                        start=first, stop=(j == j_hi)),
                        deps=[t_e, vev[j]])
                    phat_war[sb_] = pv_mm
                    first = False
                self._phat_war = phat_war
                pvtok = pv_mm
                t_le = p.op("ve", lambda e: e.tensor_copy(
                    self.l65[0:1, 0:QW], self.psb[2][64:65, 0:QW]),
                    deps=[pv_mm] + ([self._t_lbc_prev]
                                    if getattr(self, "_t_lbc_prev", None) else []))
                t_lr = p.op("act", lambda e: e.activation(
                    out=self.l65[0:1, 0:QW], in_=self.l65[0:1, 0:QW],
                    func=AF.Ln), deps=[t_le])
                t_lr = p.op("act", lambda e: e.activation(
                    out=self.l65[0:1, 0:QW], in_=self.l65[0:1, 0:QW],
                    func=AF.Exp, scale=-1.0), deps=[t_lr])
                bc_deps = [t_lr, self.t_o65]
                if getattr(self, "_t_bce_prev", None):
                    bc_deps.append(self._t_bce_prev)
                t_bc = p.op("te", lambda e: e.matmul(
                    self.psb[4][0:128, 0:QW], self.ones65[0:1, :],
                    self.l65[0:1, 0:QW], start=True, stop=True),
                    deps=bc_deps)
                self._t_lbc_prev = t_bc
                t_bce = p.op("ve", lambda e: e.tensor_copy(
                    self.bcst[:, 0:QW], self.psb[4][0:128, 0:QW]), deps=[t_bc])
                self._t_bce_prev = t_bce
                spb = self.OT_BASE + self.moe_perm(h // 2) * T
                t_no = p.op("ve", lambda e, h=h, qc=qc, spb=spb: e.tensor_tensor(
                    self.bufB[(h % 2) * 64:(h % 2) * 64 + 64,
                              spb + qc * QW:spb + (qc + 1) * QW],
                    self.psb[2][0:64, 0:QW], self.bcst[0:64, 0:QW], OP.mult),
                    deps=[pv_mm, t_bce])
                pvtok = t_no
                self.t_OT.append(t_no)
            last_qkT = pvtok
            if h == self.PARTB_AT:
                self.emit_routing_b()
            if h == self.PARTC_AT:
                self.emit_routing_c()
            if h >= self.SCAT_START:
                self.emit_scatter_groups(gph)
            if h in self.MOE_AT:
                for k in self.MOE_AT[h]:
                    htok, ytok = self.emit_moe_chunk(k, exp_prev, pvtok)
                    exp_prev[0] = htok
                    exp_prev[1] = htok
                    pvtok = ytok
                    last_qkT = ytok
            if h == self.AG_AT:
                self.emit_moe_ag()
        if self.want("OT"):
            ot = self.eout("dbg_OT", [128, ND * T], BF16)
            p.dma("qact", ot[:], self.bufB[:, 0:self.dm.ND * self.dm.T], deps=self.t_OT)
        if self.want("l_h"):
            lh = self.eout("dbg_l_h", [1, T], F32)
            p.dma("qact", lh[:], self.l_h[:], deps=self.t_OT)

    def phase_wout_l0(self):
        p, dm = self.p, self.dm
        T, D, H, ND, NT = dm.T, dm.D, dm.H, dm.ND, dm.NT
        NS = H * 64 // 128
        DCW = min(512, D)
        NDC = D // DCW
        t_cast = p.dma("qpool", self.woutbf[:], self.wout_in[:])
        woutbf_r = self.woutbf.ap().rearrange("(s p) n -> p s n", p=128)
        wl = {}
        ev_prev = {}
        self.t_s0 = []
        for dc in range(NDC):
            bsel = 0
            wdep = [t_cast] + ([wl["w"]] if "w" in wl else [])
            if dc == 0 and getattr(self, "_moe", None):
                wdep += [t for t in (self._moe["wl"].get("wo"),) if t]
            t_wo = p.dma("qsp", self.wrhs[bsel][:, 0:NS * DCW],
                         woutbf_r[:, :, dc * DCW:(dc + 1) * DCW], deps=wdep)
            for i in range(NT):
                pb = 2 + (i % 2)
                mm = None
                mdeps = self.t_OT + [t_wo]
                if pb in ev_prev:
                    mdeps.append(ev_prev[pb])
                for s in range(NS):
                    sb_ = self.OT_BASE + self.moe_perm(s) * T
                    mm = p.op("te", lambda e, s=s, i=i, dc=dc, pb=pb, bsel=bsel,
                              sb_=sb_: e.matmul(
                        self.psb[pb][0:128, 0:DCW],
                        self.bufB[:, sb_ + i * 128:sb_ + (i + 1) * 128],
                        self.wrhs[bsel][:, s * DCW:(s + 1) * DCW],
                        start=(s == 0), stop=(s == NS - 1)),
                        deps=(mdeps if s == 0 else []))
                t_ev = p.op("ve", lambda e, i=i, dc=dc, pb=pb: e.tensor_copy(
                    self.xnT_bf[:, i * D + dc * DCW:i * D + (dc + 1) * DCW],
                    self.psb[pb][0:128, 0:DCW]), deps=[mm])
                ev_prev[pb] = t_ev
                self.t_s0.append(t_ev)
            wl["w"] = t_ev
        if self.want("s0"):
            so = self.eout("dbg_s0", [128, NT * D], BF16)
            p.dma("qact", so[:], self.xnT_bf[:], deps=self.t_s0)

    def phase_combine_residual(self):
        p, dm = self.p, self.dm
        T, D, NE, NT, ND, R = dm.T, dm.D, dm.NE, dm.NT, dm.ND, dm.R
        eps = 1e-5
        xs, xn_f = self.xs[0], self.xn_f[0]
        g0f = self.xnT_st
        a4, b4, c4, d4 = self.tsc
        mx, mx2, icol, dcol, pcol, qcol = self.vcol
        war = {}
        for nm_ in ("gA", "gB", "zbf"):
            tkn = getattr(self, "t_xr_war", {}).get(nm_)
            if tkn is not None:
                war[nm_] = tkn
        x1_done = []
        self.t_zT = []
        t_glg = None
        for i in range(NT):
            t_x = p.dma("qsp", xs[:], self.x_in[i * 128:(i + 1) * 128, :],
                        deps=war.get("xs_list", []))
            gz1 = p.op("pool", lambda e: e.memset(self.gA[:], 0.0),
                       deps=[war.get("gA")] if "gA" in war else [])
            gz2 = p.op("pool", lambda e: e.memset(self.gB[:], 0.0),
                       deps=[war.get("gB")] if "gB" in war else [])
            tg1 = p.indirect(self.gA[:], None, self.cc_y_out[:],
                             bass.IndirectOffsetOnAxis(self.flat1[:, i:i + 1], 0),
                             8 * dm.CAP - 1,
                             deps=[self.t_ag_y, self.t_flats[i], gz1])
            tg2 = p.indirect(self.gB[:], None, self.cc_y_out[:],
                             bass.IndirectOffsetOnAxis(self.flat2[:, i:i + 1], 0),
                             8 * dm.CAP - 1, deps=[gz2, tg1])
            t_c1 = p.op("ve", lambda e, i=i: e.tensor_scalar(
                xn_f[:], self.gB[:], self.p2c[:, i:i + 1], None, OP.mult),
                deps=[tg2, war.get("xn_f"), war.get("gB2")])
            t_c2 = p.op("ve", lambda e, i=i: e.tensor_scalar(
                g0f[:], self.gA[:], self.p1c[:, i:i + 1], None, OP.mult),
                deps=[tg1, tg2, war.get("g0f")])
            t_c3 = p.op("ve", lambda e: e.tensor_tensor(
                g0f[:], g0f[:], xn_f[:], OP.add), deps=[t_c1, t_c2])
            t_g0 = p.op("act", lambda e: e.activation(
                out=g0f[:], in_=g0f[:], func=AF.Silu), deps=[t_c3])
            war["gA"] = t_c2
            war["gB"] = t_c1
            war["gB2"] = t_c3
            t_m1 = p.op("ve", lambda e, i=i: e.tensor_tensor(
                g0f[:], self.xnT_bf[:, i * D:(i + 1) * D], g0f[:], OP.mult),
                deps=[t_g0] + self.t_s0[-1:])
            t_x1 = p.op("ve", lambda e: e.tensor_tensor(
                xs[:], xs[:], g0f[:], OP.add), deps=[t_m1, t_x])
            war["g0f"] = t_x1
            t_xd = p.dma("qact", self.x1_dram[i * 128:(i + 1) * 128, :], xs[:],
                         deps=[t_x1])
            x1_done.append(t_xd)
            t_s1 = p.op("ve", lambda e: e.reduce_sum(
                self.stat[:, 0:1], xs[:], axis=AX.X), deps=[t_x1])
            t_sq = p.op("act", lambda e: e.activation(
                out=self.zbf[:], in_=xs[:], func=AF.Square,
                accum_out=self.stat[:, 1:2]),
                deps=[t_x1, war.get("zbf")])
            t_m = p.op("ve", lambda e: e.tensor_scalar(
                self.stat[:, 2:3], self.stat[:, 0:1], 1.0 / D, None, OP.mult),
                deps=[t_s1])
            t_ex2 = p.op("ve", lambda e: e.tensor_scalar(
                self.stat[:, 3:4], self.stat[:, 1:2], 1.0 / D, eps, OP.mult, OP.add),
                deps=[t_sq])
            t_m2 = p.op("ve", lambda e: e.tensor_tensor(
                self.stat[:, 4:5], self.stat[:, 2:3], self.stat[:, 2:3], OP.mult),
                deps=[t_m])
            t_var = p.op("ve", lambda e: e.tensor_tensor(
                self.stat[:, 5:6], self.stat[:, 3:4], self.stat[:, 4:5],
                OP.subtract), deps=[t_ex2, t_m2])
            t_sd = p.op("act", lambda e: e.activation(
                out=self.stat[:, 6:7], in_=self.stat[:, 5:6], func=AF.Ln),
                deps=[t_var])
            t_rstd = p.op("act", lambda e: e.activation(
                out=self.stat[:, 5:6], in_=self.stat[:, 6:7], func=AF.Exp,
                scale=-0.5), deps=[t_sd])
            t_nmr = p.op("ve", lambda e: e.scalar_tensor_tensor(
                self.stat[:, 4:5], self.stat[:, 2:3], -1.0, self.stat[:, 5:6],
                OP.mult, OP.mult), deps=[t_rstd, t_m])
            t_xn1 = p.op("ve", lambda e: e.tensor_scalar(
                xn_f[:], xs[:], self.stat[:, 5:6], self.stat[:, 4:5],
                OP.mult, OP.add), deps=[t_nmr, t_x1])
            war["xn_f"] = t_xn1
            war["xs_list"] = [t_xn1, t_xd]
            t_xb = p.op("ve", lambda e: e.tensor_copy(self.zbf[:], xn_f[:]),
                        deps=[t_xn1, t_sq])
            t_xbd = p.dma("qact", self.xn1_dram[i * 128:(i + 1) * 128, :],
                          self.zbf[:], deps=[t_xb])
            self.t_xn1w = getattr(self, "t_xn1w", [])
            self.t_xn1w.append(t_xbd)
            psev = self._psS_ev
            t_gmm = None
            for c in range(ND):
                sl_ = c % 2
                tt = p.op("te", lambda e, c=c, sl_=sl_: e.transpose(
                    self.psS[:, sl_ * 128:(sl_ + 1) * 128],
                    xn_f[:, c * 128:(c + 1) * 128],
                    self.identF[:]), deps=[t_xn1] +
                    ([psev[sl_]] if sl_ in psev else []))
                tev = p.op("ve", lambda e, c=c, sl_=sl_: e.tensor_copy(
                    g0f[:, c * 128:(c + 1) * 128],
                    self.psS[:, sl_ * 128:(sl_ + 1) * 128]),
                    deps=[tt, t_m1])
                psev[sl_] = tev
                gdeps = [tev, self.t_gwT]
                if c == 0 and t_glg is not None:
                    gdeps.append(t_glg)
                t_gmm = p.op("te", lambda e, c=c: e.matmul(
                    self.psb[4][0:128, 0:NE], g0f[:, c * 128:(c + 1) * 128],
                    self.gwT[:, c * NE:(c + 1) * NE],
                    start=(c == 0), stop=(c == ND - 1)), deps=gdeps)
            self._t_psS_last = tev
            self._psS_ev = psev
            war["g0f"] = tev
            t_lg1 = p.op("ve", lambda e: e.tensor_copy(
                a4[:], self.psb[4][0:128, 0:NE]), deps=[t_gmm, t_glg])
            t_glg = t_lg1
            self.t_gate_last = t_lg1
            t1 = p.op("ve", lambda e: e.reduce_max(mx[:], a4[:], axis=AX.X),
                      deps=[t_lg1])
            tneg = p.op("ve", lambda e: e.tensor_scalar(
                qcol[:], mx[:], -1.0, None, OP.mult), deps=[t1])
            t_ee = p.op("act", lambda e: e.activation(
                out=b4[:], in_=a4[:], func=AF.Exp, bias=qcol[:]), deps=[tneg])
            t_ss = p.op("ve", lambda e: e.reduce_sum(pcol[:], b4[:], axis=AX.X),
                        deps=[t_ee])
            t_rr = p.op("ve", lambda e: e.reciprocal(pcol[:], pcol[:]), deps=[t_ss])
            t_sc = p.op("ve", lambda e: e.tensor_scalar(
                b4[:], b4[:], pcol[:], None, OP.mult), deps=[t_rr])
            tm1_ = p.op("ve", lambda e: e.reduce_max(mx[:], b4[:], axis=AX.X),
                        deps=[t_sc])
            teq = p.op("ve", lambda e: e.tensor_scalar(
                c4[:], b4[:], mx[:], None, OP.is_equal), deps=[tm1_])
            trv = p.op("ve", lambda e: e.scalar_tensor_tensor(
                d4[:], self.iotaE[:], -1.0, c4[:], OP.mult, OP.mult),
                deps=[teq, self.t_iotaE])
            trv2 = p.op("ve", lambda e: e.scalar_tensor_tensor(
                d4[:], c4[:], float(NE), d4[:], OP.mult, OP.add), deps=[trv])
            tmx = p.op("ve", lambda e: e.reduce_max(icol[:], d4[:], axis=AX.X),
                       deps=[trv2])
            ti1 = p.op("ve", lambda e: e.tensor_scalar(
                icol[:], icol[:], -1.0, float(NE), OP.mult, OP.add), deps=[tmx])
            to1 = p.op("ve", lambda e: e.tensor_scalar(
                c4[:], self.iotaE[:], icol[:], None, OP.is_equal), deps=[ti1])
            tml = p.op("ve", lambda e: e.scalar_tensor_tensor(
                d4[:], c4[:], float(NEGBIG), b4[:], OP.mult, OP.add), deps=[to1])
            t2_ = p.op("ve", lambda e: e.reduce_max(mx2[:], d4[:], axis=AX.X),
                       deps=[tml])
            teq2 = p.op("ve", lambda e: e.tensor_scalar(
                d4[:], d4[:], mx2[:], None, OP.is_equal), deps=[t2_])
            trv3 = p.op("ve", lambda e: e.scalar_tensor_tensor(
                a4[:], self.iotaE[:], -1.0, d4[:], OP.mult, OP.mult), deps=[teq2])
            trv4 = p.op("ve", lambda e: e.scalar_tensor_tensor(
                a4[:], d4[:], float(NE), a4[:], OP.mult, OP.add), deps=[trv3])
            tmx2 = p.op("ve", lambda e: e.reduce_max(dcol[:], a4[:], axis=AX.X),
                        deps=[trv4])
            ti2 = p.op("ve", lambda e: e.tensor_scalar(
                dcol[:], dcol[:], -1.0, float(NE), OP.mult, OP.add), deps=[tmx2])
            to2 = p.op("ve", lambda e: e.tensor_scalar(
                d4[:], self.iotaE[:], dcol[:], None, OP.is_equal), deps=[ti2])
            tw1 = p.op("ve", lambda e: e.tensor_scalar(
                c4[:], c4[:], mx[:], None, OP.mult), deps=[to1, tm1_, tml])
            tw2 = p.op("ve", lambda e: e.tensor_scalar(
                d4[:], d4[:], mx2[:], None, OP.mult), deps=[to2])
            t_ws = p.op("ve", lambda e, i=i: e.tensor_tensor(
                self.wsel1[:, i * NE:(i + 1) * NE], c4[:], d4[:], OP.add),
                deps=[tw1, tw2])
            self.t_wsel = t_ws
            t_r1 = p.op("ve", lambda e: e.tensor_tensor(
                self.stat[:, 0:1], self.stat[:, 5:6], self.stat[:, 5:6],
                OP.mult), deps=[t_xn1])
            t_r2 = p.op("ve", lambda e: e.tensor_scalar(
                self.stat[:, 0:1], self.stat[:, 0:1], -eps, 1.0 + eps,
                OP.mult, OP.add), deps=[t_r1])
            t_sdb = p.op("act", lambda e: e.activation(
                out=self.stat[:, 0:1], in_=self.stat[:, 0:1], func=AF.Ln),
                deps=[t_r2])
            t_rstdb = p.op("act", lambda e: e.activation(
                out=self.stat[:, 0:1], in_=self.stat[:, 0:1], func=AF.Exp,
                scale=-0.5), deps=[t_sdb])
            t_z = p.op("ve", lambda e: e.tensor_scalar(
                self.zbf[:], xn_f[:], self.stat[:, 0:1], None,
                OP.mult), deps=[t_rstdb, t_xbd])
            war["zbf"] = t_z
            tpsT = getattr(self, "_t_psT_last", None)
            for c in range(ND):
                half = c % 2
                tt = p.op("te", lambda e, c=c, half=half: e.transpose(
                    self.psT[:, half * 128:(half + 1) * 128],
                    self.zbf[:, c * 128:(c + 1) * 128], self.identB[:]),
                    deps=[t_z, self.t_identB] + ([tpsT] if tpsT else []))
                tpsT = p.op("ve", lambda e, c=c, i=i, half=half: e.tensor_copy(
                    self.bufB[:, c * T + i * 128:c * T + (i + 1) * 128],
                    self.psT[:, half * 128:(half + 1) * 128]), deps=[tt])
                self.t_zT.append(tpsT)
            self._t_psT_last = tpsT
            war["zbf"] = tpsT
        self.t_x1 = x1_done
        if self.want("x1"):
            xo = self.eout("dbg_x1", [T, D], F32)
            p.dma("qpool", xo[:], self.x1_dram[:], deps=x1_done)
        if self.want("zT"):
            zo = self.eout("dbg_zT", [128, ND * T], BF16)
            tzd = p.dma("qact", zo[:], self.bufB[:, 0:self.dm.ND * self.dm.T],
                        deps=self.t_zT)
            self.t_zT.append(tzd)
        if self.want("wsel1"):
            wo = self.eout("dbg_wsel1", [128, NT * NE], F32)
            p.dma("qact", wo[:], self.wsel1[:], deps=[self.t_wsel])

    def phase_attn_l1(self):
        p, dm = self.p, self.dm
        T, D, H, ND, NT, W = dm.T, dm.D, dm.H, dm.ND, dm.NT, dm.W
        NW = dm.NW
        QW = self.QW
        t_bb = p.op("ve", lambda e: e.tensor_copy(self.bcols_b[:], self.bcols[:]),
                    deps=self.t_bcols)
        t_row = None
        brow_all = []
        CW = min(512, D)
        for ch in range(3 * D // CW):
            mm = None
            for c in range(ND):
                t_wl = p.dma("qsp", self.wrhs[0][:, 0:CW],
                             self.wqkv1bf[c * 128:(c + 1) * 128,
                                          ch * CW:(ch + 1) * CW],
                             deps=[self.t_wq1c] + ([mm] if mm else
                                                   ([t_row] if t_row else [])))
                mm = p.op("te", lambda e, c=c, ch=ch, CW=CW: e.matmul(
                    self.psb[4][0:1, 0:CW], self.bcols_b[:, c:c + 1],
                    self.wrhs[0][:, 0:CW], start=(c == 0), stop=(c == ND - 1)),
                    deps=[t_wl, t_bb] + ([t_row] if t_row and c == 0 else []))
            t_rowc = p.op("ve", lambda e, CW=CW: e.tensor_copy(
                self.phat[0][0:1, 0:CW], self.psb[4][0:1, 0:CW]),
                deps=[mm] + ([t_row] if t_row else []))
            t_row = p.dma("qact", self.brow_dram[0:1, ch * CW:(ch + 1) * CW],
                          self.phat[0][0:1, 0:CW], deps=[t_rowc])
            brow_all.append(t_row)
        t_brow = t_row
        self.t_OT1 = []
        self._l_war1 = self.t_s0[-1]
        exp_prev = {}
        last_wq = {}
        last_qkT = None
        t_pe = None
        pvtok = None
        wqkv1_r = self.wqkv1bf.ap().rearrange("(c p) n -> p c n", p=128)
        for h in range(H):
            bsel = h % 2
            wq = self.wq_sb[bsel]
            wdep = [last_wq['w']] if 'w' in last_wq else []
            tws = []
            for g, off in ((0, h * 64), (1, D + h * 64), (2, 2 * D + h * 64)):
                tws.append(p.dma("qsp",
                                 wq.ap().rearrange("p (c x) -> p c x", x=192)[:, :, g * 64:(g + 1) * 64],
                                 wqkv1_r[:, :, off:off + 64],
                                 deps=wdep if g == 0 else []))
            t_bhs = []
            for g, off in ((0, h * 64), (1, D + h * 64), (2, 2 * D + h * 64)):
                t_bhs.append(p.dma("qsp", self.brow_h[0:1, g * 64:(g + 1) * 64],
                             self.brow_dram[0:1, off:off + 64],
                             deps=brow_all + wdep if g == 0 else []))
            t_bh = t_bhs
            t_gs = None
            for c in range(ND):
                t_gs = p.op("ve", lambda e, c=c, wq=wq: e.tensor_scalar(
                    wq[:, c * 192:(c + 1) * 192], wq[:, c * 192:(c + 1) * 192],
                    self.gcols[:, c:c + 1], None, OP.mult),
                    deps=[tws[0], tws[1], tws[2]] + self.t_gcols)
            qk_war = [last_qkT] if last_qkT else []
            TCW = min(512, T)
            qkvev = {0: (self.t_gate_last if h == 0
                         else getattr(self, "_t_bce_prev", None)), 1: t_pe}
            gi = 0
            for tc in range(T // TCW):
                bk = 4 + gi % 2
                mm = None
                wdeps = [t_gs] + t_bh + qk_war + self.t_zT[-1:]
                if qkvev[gi % 2] is not None:
                    wdeps.append(qkvev[gi % 2])
                for c in range(ND):
                    mm = p.op("te", lambda e, tc=tc, c=c, wq=wq,
                              TCW=TCW, bk=bk: e.matmul(
                        self.psb[bk][0:128, 0:TCW],
                        wq[:, c * 192:c * 192 + 128],
                        self.bufB[:, c * T + tc * TCW:c * T + (tc + 1) * TCW],
                        start=(c == 0), stop=False),
                        deps=(wdeps if c == 0 else []))
                mm = p.op("te", lambda e, tc=tc, TCW=TCW, bk=bk: e.matmul(
                    self.psb[bk][0:128, 0:TCW],
                    self.brow_h[0:1, 0:128],
                    self.ones_row_b[0:1, 0:TCW], start=False, stop=True),
                    deps=[self.t_orow] + t_bh)
                ev1 = p.op("ve", lambda e, tc=tc, TCW=TCW, bk=bk: e.tensor_copy(
                    self.qT[0:64, tc * TCW:(tc + 1) * TCW],
                    self.psb[bk][0:64, 0:TCW]), deps=[mm])
                qkvev[gi % 2] = p.op("ve", lambda e, tc=tc, TCW=TCW,
                                     bk=bk: e.tensor_copy(
                    self.kT[0:64, tc * TCW:(tc + 1) * TCW],
                    self.psb[bk][64:128, 0:TCW]), deps=[ev1])
                gi += 1
            for tt in range(NT):
                bk = 4 + gi % 2
                mm = None
                wdeps = [t_gs]
                if qkvev[gi % 2] is not None:
                    wdeps.append(qkvev[gi % 2])
                for c in range(ND):
                    mm = p.op("te", lambda e, tt=tt, c=c, wq=wq, bk=bk: e.matmul(
                        self.psb[bk][0:128, 0:64],
                        self.bufB[:, c * T + tt * 128:c * T + (tt + 1) * 128],
                        wq[:, c * 192 + 128:c * 192 + 192],
                        start=(c == 0), stop=False),
                        deps=(wdeps if c == 0 else []))
                mm = p.op("te", lambda e, tt=tt, bk=bk: e.matmul(
                    self.psb[bk][0:128, 0:64], self.onesK1_b[:],
                    self.brow_h[0:1, 128:192],
                    start=False, stop=True), deps=[self.t_onesK1b] + t_bh)
                qkvev[gi % 2] = p.op("ve", lambda e, tt=tt, bk=bk: e.tensor_copy(
                    self.v_h[:, tt * 65:tt * 65 + 64], self.psb[bk][0:128, 0:64]),
                    deps=[mm, self.t_vones])
                gi += 1
            t_qkv = p.op("ve", lambda e: e.engine_nop(),
                         deps=[qkvev[0], qkvev[1]])
            last_wq['w'] = t_qkv
            if h == 0 and self.want("qkT1"):            t_qkv = t_pe
            last_wq['w'] = t_qkv
            if h == 0 and self.want("qkT1"):
                qo = self.eout("dbg_q1", [64, T], BF16)
                ko = self.eout("dbg_k1", [64, T], BF16)
                vo = self.eout("dbg_v1", [128, NT * 65], BF16)
                p.dma("qact", qo[:], self.qT[0:64, 0:T], deps=[t_qkv])
                p.dma("qact", ko[:], self.kT[0:64, 0:T], deps=[t_qkv])
                t_qkv = p.dma("qact", vo[:], self.v_h[:], deps=[t_qkv])
                last_wq['w'] = t_qkv
            phat_war = getattr(self, "_phat_war1", {})
            WPC = QW // dm.W
            for w in range(NW):
                sb_ = w % 2
                wid = 256 if w > 0 else 128
                k0 = (w - 1) * 128 if w > 0 else 0
                sdeps = [t_qkv]
                if sb_ in exp_prev:
                    sdeps.append(exp_prev[sb_])
                mmS = None
                for kk in range(wid // 128):
                    mmS = p.op("te", lambda e, w=w, k0=k0, kk=kk, sb_=sb_: e.matmul(
                        self.psb[sb_][0:128, kk * 128:(kk + 1) * 128],
                        self.kT[0:64, k0 + kk * 128:k0 + (kk + 1) * 128],
                        self.qT[0:64, w * 128:(w + 1) * 128],
                        start=True, stop=True),
                        deps=(sdeps if kk == 0 else []))
                moff = 0 if w > 0 else 128
                t_b = p.op("ve", lambda e, sb_=sb_, wid=wid, moff=moff: e.tensor_tensor(
                    self.psb[sb_][0:128, 0:wid], self.psb[sb_][0:128, 0:wid],
                    self.lmask[:, moff:moff + wid], OP.add),
                    deps=[mmS, self.t_lmask])
                edeps = [t_b, self.t_m16]
                if sb_ in phat_war:
                    edeps.append(phat_war[sb_])
                if pvtok is not None:
                    edeps.append(pvtok)
                t_e = p.op("act", lambda e, sb_=sb_, wid=wid: e.activation(
                    out=self.phat[sb_][:, 0:wid], in_=self.psb[sb_][0:128, 0:wid],
                    func=AF.Exp, bias=self.m16[:], scale=0.125), deps=edeps)
                exp_prev[sb_] = t_e
                pv_mm = None
                l_war = getattr(self, "_l_war1", None)
                for kk in range(wid // 128):
                    jj = (k0 // 128) + kk
                    pv_mm = p.op("te", lambda e, jj=jj, kk=kk, sb_=sb_,
                                 wid=wid: e.matmul(
                        self.psb[2][0:65, 0:128],
                        self.v_h[:, jj * 65:(jj + 1) * 65],
                        self.phat[sb_][:, kk * 128:(kk + 1) * 128],
                        start=(kk == 0), stop=(kk == wid // 128 - 1)),
                        deps=[t_e] + ([l_war] if l_war and kk == 0 else []))
                phat_war[sb_] = pv_mm
                t_le = p.op("ve", lambda e, w=w, WPC=WPC: e.tensor_copy(
                    self.l65[0:1, (w % WPC) * 128:(w % WPC + 1) * 128],
                    self.psb[2][64:65, 0:128]),
                    deps=[pv_mm] + ([self._t_lbc_prev]
                                    if getattr(self, "_t_lbc_prev", None) else []))
                self._l_war1 = t_le
                t_oe = p.op("ve", lambda e, h=h, w=w: e.tensor_copy(
                    self.xnT_bf[(h % 2) * 64:(h % 2) * 64 + 64,
                                (h // 2) * T + w * 128:(h // 2) * T + (w + 1) * 128],
                    self.psb[2][0:64, 0:128]), deps=[pv_mm, t_le])
                pvtok = t_oe
                if (w + 1) % WPC == 0:
                    qc = w // WPC
                    t_lr = p.op("act", lambda e: e.activation(
                        out=self.l65[0:1, 0:QW], in_=self.l65[0:1, 0:QW],
                        func=AF.Ln), deps=[pvtok])
                    t_lr = p.op("act", lambda e: e.activation(
                        out=self.l65[0:1, 0:QW], in_=self.l65[0:1, 0:QW],
                        func=AF.Exp, scale=-1.0), deps=[t_lr])
                    self._l_war1 = t_lr
                    bc_deps = [t_lr, self.t_o65]
                    if getattr(self, "_t_bce_prev", None):
                        bc_deps.append(self._t_bce_prev)
                    t_bc = p.op("te", lambda e: e.matmul(
                        self.psb[4][0:128, 0:QW], self.ones65[0:1, :],
                        self.l65[0:1, 0:QW], start=True, stop=True),
                        deps=bc_deps)
                    self._t_lbc_prev = t_bc
                    t_bce = p.op("ve", lambda e: e.tensor_copy(
                        self.bcst[:, 0:QW], self.psb[4][0:128, 0:QW]),
                        deps=[t_bc])
                    self._t_bce_prev = t_bce
                    hb = (h % 2) * 64
                    t_no = p.op("ve", lambda e, h=h, qc=qc, hb=hb: e.tensor_tensor(
                        self.xnT_bf[hb:hb + 64,
                                    (h // 2) * T + qc * QW:(h // 2) * T + (qc + 1) * QW],
                        self.xnT_bf[hb:hb + 64,
                                    (h // 2) * T + qc * QW:(h // 2) * T + (qc + 1) * QW],
                        self.bcst[hb:hb + 64, 0:QW], OP.mult),
                        deps=[t_bce, pvtok])
                    pvtok = t_no
                    self.t_OT1.append(t_no)
            self._phat_war1 = phat_war
            last_qkT = pvtok
        if self.want("OT1"):            last_qkT = pvtok
        if self.want("OT1"):
            ot = self.eout("dbg_OT1", [128, ND * T], BF16)
            p.dma("qact", ot[:], self.xnT_bf[:], deps=self.t_OT1)

    def phase_wout_l1(self):
        p, dm = self.p, self.dm
        T, D, H, ND, NT = dm.T, dm.D, dm.H, dm.ND, dm.NT
        NS = H * 64 // 128
        DCW = min(512, D)
        NDC = D // DCW
        wout1bf_r = self.wout1bf.ap().rearrange("(s p) n -> p s n", p=128)
        wl = {}
        ev_prev = {}
        self.t_s1 = []
        for dc in range(NDC):
            bsel = 0
            wdep = [self.t_wo1c] + ([wl["w"]] if "w" in wl else [])
            t_wo = p.dma("qsp", self.wrhs[bsel][:, 0:NS * DCW],
                         wout1bf_r[:, :, dc * DCW:(dc + 1) * DCW], deps=wdep)
            for i in range(NT):
                pb = 2 + (i % 2)
                mm = None
                mdeps = self.t_OT1[-1:] + [t_wo]
                if ("y", pb) in ev_prev:
                    mdeps.append(ev_prev[("y", pb)])
                for s in range(NS):
                    mm = p.op("te", lambda e, s=s, i=i, pb=pb, bsel=bsel: e.matmul(
                        self.psb[pb][0:128, 0:DCW],
                        self.xnT_bf[:, s * T + i * 128:s * T + (i + 1) * 128],
                        self.wrhs[bsel][:, s * DCW:(s + 1) * DCW],
                        start=(s == 0), stop=(s == NS - 1)),
                        deps=(mdeps if s == 0 else []))
                t_ev = p.op("ve", lambda e, i=i, dc=dc, pb=pb: e.tensor_copy(
                    self.bufB[:, i * D + dc * DCW:i * D + (dc + 1) * DCW],
                    self.psb[pb][0:128, 0:DCW]), deps=[mm] + self.t_zT[-1:])
                ev_prev[("y", pb)] = t_ev
                self.t_s1.append(t_ev)
            wl["w"] = t_ev
        if self.want("s1"):
            so = self.eout("dbg_s1", [128, NT * D], BF16)
            p.dma("qact", so[:], self.bufB[:, 0:self.dm.ND * self.dm.T], deps=self.t_s1)

    def phase_final(self):
        p, dm = self.p, self.dm
        T, D, NE, NT, ND, R = dm.T, dm.D, dm.NE, dm.NT, dm.ND, dm.R
        DCW = min(512, D)
        NDC = D // DCW
        xs, xn_f = self.xs[0], self.xn_f[0]
        g0f = self.xnT_st
        war = {}
        xt_done = []
        tz_war = []
        for i in range(NT):
            t_z = p.dma("qsp", self.zbf[:],
                        self.xn1_dram[i * 128:(i + 1) * 128, :],
                        deps=tz_war + self.t_xn1w + self.t_zT[-1:])
            tpsT = getattr(self, "_t_psT_last", None)
            for c in range(ND):
                half = c % 2
                tt = p.op("te", lambda e, c=c, half=half: e.transpose(
                    self.psT[:, half * 128:(half + 1) * 128],
                    self.zbf[:, c * 128:(c + 1) * 128], self.identB[:]),
                    deps=[t_z, self.t_identB] + ([tpsT] if tpsT else []))
                tpsT = p.op("ve", lambda e, c=c, i=i, half=half: e.tensor_copy(
                    self.xnT_bf[:, i * D + c * 128:i * D + (c + 1) * 128],
                    self.psT[:, half * 128:(half + 1) * 128]),
                    deps=[tt] + (self.t_s1 if i == 0 and c == 0 else []) +
                         (self.t_zT[-1:] if i == 0 and c == 0 else []))
            self._t_psT_last = tpsT
            tz_war = [tpsT]
            xt_done.append(tpsT)
        xt1 = lambda i, c: self.xnT_bf[:, i * D + c * 128:i * D + (c + 1) * 128]
        for dc in range(NDC):
            t_blin = p.dma("qpool", self.blin_sb[0:1, 0:DCW],
                           self.blin_in[0:1, dc * DCW:(dc + 1) * DCW],
                           deps=war.get("blin", []))
            t_right = p.dma("qpool", self.right_sb[0:32, 0:DCW],
                            self.right_in[0:32, dc * DCW:(dc + 1) * DCW],
                            deps=war.get("right", []))
            t_wl = []
            tw = None
            for c in range(ND):
                tw = p.dma("qsp", self.wrhs[0][:, c * DCW:(c + 1) * DCW],
                           self.wlinbf[dc * DCW:(dc + 1) * DCW,
                                       c * 128:(c + 1) * 128],
                           deps=([self.t_wlc] + self.t_s1[-1:] +
                                 war.get("wrhs", [])) if c == 0 else [],
                           transpose=True)
                t_wl.append(tw)
            for i in range(NT):
                t_x = p.dma("qsp", xs[:], self.x1_dram[i * 128:(i + 1) * 128, :],
                            deps=war.get("xs", []) + self.t_x1)
                t_xt = xt_done[i]
                tps = list(self._psS_ev.values())
                tt = p.op("te", lambda e, i=i: e.transpose(
                    self.psS[0:8, 0:128], self.wsel1[:, i * NE:(i + 1) * NE],
                    self.identF[:]), deps=[self.t_wsel] + tps)
                t_wsT = p.op("ve", lambda e: e.tensor_copy(
                    self.wselT[:], self.psS[0:8, 0:128]),
                    deps=[tt] + war.get("wselT", []))
                self._psS_ev = {0: t_wsT}
                t_we_mm = p.op("te", lambda e: e.matmul(
                    self.psb[4][0:32, 0:128], self.E832[:], self.wselT[:],
                    start=True, stop=True),
                    deps=[t_wsT, self.t_E832] + war.get("psb4", []))
                t_we = p.op("ve", lambda e: e.tensor_copy(
                    self.wexpT[:], self.psb[4][0:32, 0:128]),
                    deps=[t_we_mm] + war.get("wexpT", []))
                war["psb4"] = [t_we]
                mmt = None
                for c in range(ND):
                    mmt = p.op("te", lambda e, c=c, i=i: e.matmul(
                        self.psb[5][0:32, 0:128],
                        self.left_sb[:, c * 32:(c + 1) * 32],
                        xt1(i, c),
                        start=(c == 0), stop=(c == ND - 1)),
                        deps=([t_xt, self.t_left] + war.get("psb5", [])
                              if c == 0 else []))
                t_t2 = p.op("ve", lambda e: e.tensor_tensor(
                    self.t2_sb[:], self.psb[5][0:32, 0:128], self.wexpT[:],
                    OP.mult), deps=[mmt, t_we] + war.get("t2", []))
                war["psb5"] = [t_t2]
                war["wexpT"] = [t_t2]
                war["wselT"] = [t_t2]
                pb = 2 + (i % 2)
                mm = None
                bdeps = t_wl + [t_xt] + war.get(("o", pb), [])
                for c in range(ND):
                    mm = p.op("te", lambda e, c=c, i=i, pb=pb, DCW=DCW: e.matmul(
                        self.psb[pb][0:128, 0:DCW],
                        xt1(i, c),
                        self.wrhs[0][:, c * DCW:(c + 1) * DCW],
                        start=(c == 0), stop=False),
                        deps=(bdeps if c == 0 else []))
                mm = p.op("te", lambda e, pb=pb, DCW=DCW: e.matmul(
                    self.psb[pb][0:128, 0:DCW], self.onesK1_b[:],
                    self.blin_sb[0:1, 0:DCW],
                    start=False, stop=False), deps=[t_blin, self.t_onesK1b])
                mm = p.op("te", lambda e, pb=pb, DCW=DCW: e.matmul(
                    self.psb[pb][0:128, 0:DCW], self.t2_sb[:],
                    self.right_sb[0:32, 0:DCW],
                    start=False, stop=True), deps=[t_t2, t_right])
                war["blin"] = [mm]
                war["right"] = [mm]
                war["t2"] = [mm]
                t_g1 = p.op("act", lambda e, pb=pb, DCW=DCW: e.activation(
                    out=g0f[:, 0:DCW],
                    in_=self.psb[pb][0:128, 0:DCW], func=AF.Silu),
                    deps=[mm] + war.get("g0f", []))
                t_h1 = p.op("ve", lambda e, i=i, dc=dc, DCW=DCW: e.tensor_tensor(
                    g0f[:, 0:DCW],
                    self.bufB[:, i * D + dc * DCW:i * D + (dc + 1) * DCW],
                    g0f[:, 0:DCW], OP.mult),
                    deps=[t_g1] + self.t_s1[-1:])
                war[("o", pb)] = [t_h1]
                t_o = p.op("ve", lambda e, dc=dc, DCW=DCW: e.tensor_tensor(
                    xs[:, dc * DCW:(dc + 1) * DCW],
                    xs[:, dc * DCW:(dc + 1) * DCW],
                    g0f[:, 0:DCW], OP.add), deps=[t_h1, t_x])
                war["g0f"] = [t_o]
                t_od = p.dma("qact",
                             self.out[i * 128:(i + 1) * 128,
                                      dc * DCW:(dc + 1) * DCW],
                             xs[:, dc * DCW:(dc + 1) * DCW], deps=[t_o])
                war["xs"] = [t_od, t_o]
            war["wrhs"] = [mm]



def make_inputs(dm, fi):
    f32 = lambda a: np.ascontiguousarray(a, dtype=np.float32)
    x = f32(fi["L0_attn_x"])
    shared = {
        "rw": f32(fi["L0_ffn_router_w"]),
        "rb": f32(fi["L0_ffn_router_b"]).reshape(1, -1),
        "wqkv": f32(fi["L0_attn_W_qkv"]),
        "wout": f32(fi["L0_attn_W_out"]),
        "l1g": f32(fi["L1_attn_ln_gamma"]).reshape(1, -1),
        "l1b": f32(fi["L1_attn_ln_beta"]).reshape(1, -1),
        "wqkv1": f32(fi["L1_attn_w_qkv"]),
        "wout1": f32(fi["L1_attn_w_out"]),
        "gw": f32(fi["L1_ffn_gate_w"]),
        "wlin": f32(fi["L1_ffn_W_lin"]),
        "blin": f32(fi["L1_ffn_b_lin"]).reshape(1, -1),
        "left_all": f32(np.transpose(fi["L1_ffn_left"], (1, 0, 2)).reshape(dm.D, -1)),
        "right_all": f32(np.asarray(fi["L1_ffn_right"]).reshape(dm.NE * dm.R, dm.D)),
    }
    W = f32(fi["L0_ffn_W"])
    V = f32(fi["L0_ffn_V"])
    WO = f32(fi["L0_ffn_W_out"])
    in_maps = []
    for c in range(8):
        esel = np.zeros((128, 8), np.float32)
        for r in range(8):
            esel[16 * r + c, r] = 1.0
        rmask = np.zeros((8, 1), np.float32)
        rmask[:c] = 1.0
        kq = dm.T // dm.Q
        pp = np.arange(128)
        slmidx = ((16 * (pp // 16) + c) * kq + pp % kq).astype(np.float32)
        m = dict(shared)
        m.update({"x": x[c], "rank_mask": rmask, "esel": esel,
                  "slmidx": slmidx.reshape(128, 1),
                  "mw": W[c], "mv": V[c], "mwo": WO[c]})
        in_maps.append(m)
    return in_maps


def kernel(**inputs):
    dm = Dims()
    b = Builder(dm)
    nc = b.build()
    in_maps = make_inputs(dm, inputs)
    res = run_bass_kernel_spmd(nc, in_maps, core_ids=list(range(8)))
    return np.stack([r["out"] for r in res.results], axis=0)
